# version 18
# baseline (speedup 1.0000x reference)
import contextlib
import math

import numpy as np
import ml_dtypes

import concourse.bass as bass
import concourse.mybir as mybir
from concourse.bass_utils import run_bass_kernel_spmd

F32 = mybir.dt.float32
BF16 = mybir.dt.bfloat16
FP8 = mybir.dt.float8e4
U8 = mybir.dt.uint8
I32 = mybir.dt.int32
AF = mybir.ActivationFunctionType
ALU = mybir.AluOpType
AX = mybir.AxisListType

T = 2048
D = 1024
NTC = T // 128
NDC = D // 128

ENGINES = ("PE", "ACT", "DVE", "POOL", "SP")
QUEUES = ("qSP", "qPOOL", "qACT", "qDVE")
Q_OF = {"SP": "qSP", "POOL": "qPOOL", "ACT": "qACT", "DVE": "qDVE"}
ENG_ATTR = {"PE": "tensor", "ACT": "scalar", "DVE": "vector", "POOL": "gpsimd",
            "SP": "sync"}
SAME_ENGINE_SYNC = True


class Prog:

    def __init__(self, name="prog"):
        self.nc = bass.Bass("TRN2", target_bir_lowering=False)
        self.stack = contextlib.ExitStack()
        self.regions = []
        self.stream = {e: [] for e in ENGINES}
        self.count = {e: 0 for e in ENGINES + QUEUES}
        self.clock = {e: {} for e in ENGINES}
        self.snap = {e: {} for e in ENGINES + QUEUES}
        self.buf = {}
        self.uid = 0

    def _cur_stack(self):
        return self.regions[-1] if self.regions else self.stack

    def push_region(self):
        self.regions.append(contextlib.ExitStack())

    def pop_region(self):
        self.barrier()
        self.regions.pop().close()

    def sb(self, name, p, f, dt):
        return self._cur_stack().enter_context(
            self.nc.sbuf_tensor(name, [p, f], dt))

    def ps(self, name, p, f, dt=F32):
        return self._cur_stack().enter_context(
            self.nc.psum_tensor(name, [p, f], dt))

    def dram_in(self, name, shape, dt):
        return self.nc.dram_tensor(name, list(shape), dt, kind="ExternalInput").ap()

    def dram_out(self, name, shape, dt):
        return self.nc.dram_tensor(name, list(shape), dt, kind="ExternalOutput").ap()

    def dram_tmp(self, name, shape, dt):
        return self.nc.dram_tensor(name, list(shape), dt, kind="Internal").ap()

    def sem(self, name):
        return self.stack.enter_context(self.nc.semaphore(name))

    def op(self, eng, emit, reads=(), writes=(), queue=False, lhsT_reads=None):
        import sys as _sys
        _f = _sys._getframe(1)
        _label = f"{_f.f_lineno}"
        if _f.f_code.co_name in ("dma", "dma_tr"):
            _f2 = _sys._getframe(2)
            _label = f"{_f2.f_lineno}"
        self._last_label = _label
        deps = {}

        def need(ev):
            e, c = ev
            if c > deps.get(e, -1):
                deps[e] = c

        need_fence = False
        if eng == "PE":
            if not hasattr(self, "fclk"):
                self.fclk = {}
            lb = reads if lhsT_reads is None else lhsT_reads
            for b in lb:
                rec = self.buf.get(b)
                if rec and rec["w"]:
                    de, dc = rec["w"]
                    if de != eng and self.fclk.get(de, 0) < dc:
                        need_fence = True
                        break
        for b in reads:
            rec = self.buf.get(b)
            if rec and rec["w"]:
                need(rec["w"])
        for b in writes:
            rec = self.buf.get(b)
            if rec:
                if rec["w"]:
                    need(rec["w"])
                for r in rec["r"]:
                    need(r)
        clk = self.clock[eng]
        for de in sorted(deps):
            dc = deps[de]
            if de == eng and not SAME_ENGINE_SYNC:
                continue
            if clk.get(de, 0) >= dc:
                continue
            self.stream[eng].append(("wait", de, dc))
            snap = self.snap[de].get(dc, {})
            for k, v in snap.items():
                if v > clk.get(k, 0):
                    clk[k] = v
            if dc > clk.get(de, 0):
                clk[de] = dc

        if queue:
            ev_eng = Q_OF[eng]
            self.count[ev_eng] += 16
        else:
            ev_eng = eng
            self.count[ev_eng] += 1
        cnt = self.count[ev_eng]
        if need_fence:
            self.fclk = dict(self.clock["PE"])
            emit0 = emit

            def emit(e, emit0=emit0):
                for _ in range(70):
                    e.nop()
                return emit0(e)
        self.stream[eng].append(("emit", emit, ev_eng, cnt, self._last_label))
        evclk = dict(clk)
        evclk[eng] = self.count[eng]
        evclk[ev_eng] = cnt
        self.snap[ev_eng][cnt] = evclk
        ev = (ev_eng, cnt)
        for b in reads:
            self.buf.setdefault(b, {"w": None, "r": []})["r"].append(ev)
        for b in writes:
            self.buf[b] = {"w": ev, "r": []}
        return ev

    SERIAL_DMA = False

    def dma(self, eng, dst, src, reads=(), writes=(), **kw):
        tok = ([f"__q_{eng}"] if self.SERIAL_DMA else [])
        return self.op(eng, lambda e: e.dma_start(out=dst, in_=src, **kw),
                       reads=reads, writes=list(writes) + tok, queue=True)

    def dma_tr(self, eng, dst, src, reads=(), writes=()):
        tok = ([f"__q_{eng}"] if self.SERIAL_DMA else [])
        return self.op(eng, lambda e: e.dma_start_transpose(dst, src),
                       reads=reads, writes=list(writes) + tok, queue=True)

    def barrier(self):
        counts = dict(self.count)
        for eng in ENGINES:
            clk = self.clock[eng]
            for de in ENGINES + QUEUES:
                if de == eng:
                    continue
                dc = counts[de]
                if dc > 0 and clk.get(de, 0) < dc:
                    self.stream[eng].append(("wait", de, dc))
                    clk[de] = dc
        for b in self.buf:
            self.buf[b] = {"w": None, "r": []}

    def finalize(self):
        nc = self.nc
        sems = {e: self.stack.enter_context(nc.semaphore("sem_" + e))
                for e in ENGINES + QUEUES}
        block = self.stack.enter_context(nc.Block())

        def runner(eng_name):
            def run(eng_obj):
                for cmd in self.stream[eng_name]:
                    if cmd[0] == "wait":
                        eng_obj.wait_ge(sems[cmd[1]], cmd[2])
                    else:
                        _, emit, ev_eng, cnt = cmd[:4]
                        inst = emit(eng_obj)
                        inc = 16 if ev_eng in QUEUES else 1
                        inst.then_inc(sems[ev_eng], inc)
            return run

        block.tensor(runner("PE"))
        block.scalar(runner("ACT"))
        block.vector(runner("DVE"))
        block.gpsimd(runner("POOL"))
        block.sync(runner("SP"))
        self.stack.close()
        return nc

    def fresh(self, tag):
        self.uid += 1
        return f"{tag}_{self.uid}"

    def constf(self, val):
        if not hasattr(self, "_consts"):
            self._consts = {}
            self._ctile = self.stack.enter_context(
                self.nc.sbuf_tensor("cconsts", [128, 16], F32))
            self._cn = 0
        val = float(val)
        if val not in self._consts:
            i = self._cn
            self._cn += 1
            self.op("DVE", lambda e, i=i, val=val: e.memset(
                self._ctile[:, i:i + 1], val), writes=["cconsts"])
            self._consts[val] = i
        i = self._consts[val]
        return self._ctile[:, i:i + 1], "cconsts"



def load_row(p, name, src_ap, n, dt=F32):
    t = p.sb(name, 1, n, dt)
    p.dma("SP", t[:], src_ap.rearrange("(o n) -> o n", o=1), writes=[name])
    return t


def bcast_row(p, name, row_t, row_buf, n, ones_f32, pb, pbk, dt=F32):
    out = p.sb(name, 128, n, dt)
    for c0 in range(0, n, 512):
        w = min(512, n - c0)
        p.op("PE", lambda e, c0=c0, w=w: e.matmul(
            pb[:, :w], ones_f32[:1, :], row_t[:1, c0:c0 + w], start=True, stop=True),
            reads=[row_buf, "ones_f32"], writes=[pbk], lhsT_reads=["ones_f32"])
        p.op("ACT", lambda e, c0=c0, w=w: e.copy(out=out[:, c0:c0 + w], in_=pb[:, :w]),
             reads=[pbk], writes=[name])
    return out


def bias_cols(p, name, row_t, row_buf, n, eye32, pb, pbk):
    ncol = (n + 127) // 128
    out = p.sb(name, 128, ncol, F32)
    for c in range(ncol):
        w = min(128, n - c * 128)
        p.op("PE", lambda e, c=c, w=w: e.transpose(
            pb[:w, :1], row_t[:1, c * 128:c * 128 + w], eye32[:1, :1]),
            reads=[row_buf, "eye32"], writes=[pbk])
        p.op("ACT", lambda e, c=c, w=w: e.copy(out=out[:w, c:c + 1], in_=pb[:w, :1]),
             reads=[pbk], writes=[name])
    return out


def emit_ln(p, x_t, x_buf, out_ap, out_buf, scratch, cols, kind="outer",
            alpha_bc=None, beta_bc=None, ab_bufs=()):
    scr_t, scr_buf = scratch
    cols_t, cols_buf = cols
    s_sum = cols_t[:, 0:1]
    s_mneg = cols_t[:, 1:2]
    s_ss = cols_t[:, 2:3]
    s_std = cols_t[:, 3:4]
    s_r = cols_t[:, 4:5]
    s_bneg = cols_t[:, 5:6]
    p.op("DVE", lambda e: e.reduce_sum(out=s_sum, in_=x_t[:], axis=AX.X),
         reads=[x_buf], writes=[cols_buf])
    p.op("DVE", lambda e: e.tensor_scalar_mul(out=s_mneg, in0=s_sum, scalar1=-1.0 / D),
         reads=[cols_buf], writes=[cols_buf])
    p.op("ACT", lambda e: e.activation(out=scr_t[:], in_=x_t[:], func=AF.Square,
                                       bias=s_mneg, scale=1.0, accum_out=s_ss),
         reads=[x_buf, cols_buf], writes=[scr_buf, cols_buf])
    if kind == "outer":
        b_eps, bk = p.constf(1e-5)
        s_inv, sk = p.constf(1.0 / D)
        p.op("ACT", lambda e: e.activation(out=s_std, in_=s_ss, func=AF.Sqrt,
                                           bias=b_eps, scale=s_inv),
             reads=[cols_buf, bk, sk], writes=[cols_buf])
        p.op("DVE", lambda e: e.reciprocal(out=s_r, in_=s_std),
             reads=[cols_buf], writes=[cols_buf])
    else:
        s_inv1, sk1 = p.constf(1.0 / (D - 1))
        p.op("ACT", lambda e: e.activation(out=s_std, in_=s_ss, func=AF.Sqrt,
                                           bias=0.0, scale=s_inv1),
             reads=[cols_buf, sk1], writes=[cols_buf])
        p.op("DVE", lambda e: e.tensor_scalar_add(out=s_std, in0=s_std, scalar1=1e-6),
             reads=[cols_buf], writes=[cols_buf])
        p.op("DVE", lambda e: e.reciprocal(out=s_r, in_=s_std),
             reads=[cols_buf], writes=[cols_buf])
    p.op("DVE", lambda e: e.tensor_mul(out=s_bneg, in0=s_mneg, in1=s_r),
         reads=[cols_buf], writes=[cols_buf])
    if alpha_bc is None:
        p.op("ACT", lambda e: e.activation(out=out_ap, in_=x_t[:], func=AF.Identity,
                                           bias=s_bneg, scale=s_r),
             reads=[x_buf, cols_buf], writes=[out_buf])
    else:
        p.op("ACT", lambda e: e.activation(out=scr_t[:], in_=x_t[:], func=AF.Identity,
                                           bias=s_bneg, scale=s_r),
             reads=[x_buf, cols_buf], writes=[scr_buf])
        p.op("DVE", lambda e: e.tensor_mul(out=scr_t[:], in0=scr_t[:], in1=alpha_bc[:]),
             reads=[scr_buf, ab_bufs[0]], writes=[scr_buf])
        p.op("DVE", lambda e: e.tensor_add(out=out_ap, in0=scr_t[:], in1=beta_bc[:]),
             reads=[scr_buf, ab_bufs[1]], writes=[out_buf])


def load_wT_bf16(p, tag, w_ap, rows, cols, in_dt=F32):
    nrc = rows // 128
    nkc = cols // 128
    tiles = [p.sb(f"{tag}_k{kc}", 128, rows, BF16) for kc in range(nkc)]
    bufs = [f"{tag}_k{kc}" for kc in range(nkc)]
    p.push_region()
    stg32 = [p.sb(f"{tag}_s32_{i}", 128, cols, in_dt) for i in range(2)]
    stg16 = [p.sb(f"{tag}_s16_{i}", 128, cols, BF16) for i in range(2)]
    for rc in range(nrc):
        i = rc % 2
        b32, b16 = f"{tag}_s32_{i}", f"{tag}_s16_{i}"
        p.dma("SP", stg32[i][:], w_ap[rc * 128:(rc + 1) * 128, :], writes=[b32])
        p.op("ACT", lambda e, i=i: e.copy(out=stg16[i][:], in_=stg32[i][:]),
             reads=[b32], writes=[b16])
        for kc in range(nkc):
            p.dma_tr("SP", tiles[kc][:, rc * 128:(rc + 1) * 128],
                     stg16[i][:, kc * 128:(kc + 1) * 128],
                     reads=[b16], writes=[bufs[kc]])
    p.pop_region()
    return tiles, bufs




def emit_decode_e4m3(p, u8_t, u8_buf, out16_ap, out_buf, tmp, n):
    ti, tb, tf, tf2, k = tmp
    p.op("DVE", lambda e: e.tensor_copy(out=ti[:, :n], in_=u8_t), reads=[u8_buf], writes=[k + "i"])
    p.op("DVE", lambda e: e.tensor_single_scalar(out=tb[:, :n], in_=ti[:, :n], scalar=3, op=ALU.arith_shift_right), reads=[k + "i"], writes=[k + "b"])
    p.op("DVE", lambda e: e.tensor_single_scalar(out=tb[:, :n], in_=tb[:, :n], scalar=15, op=ALU.bitwise_and), reads=[k + "b"], writes=[k + "b"])
    p.op("DVE", lambda e: e.tensor_single_scalar(out=tf2[:, :n].bitcast(I32), in_=ti[:, :n], scalar=7, op=ALU.bitwise_and), reads=[k + "i"], writes=[k + "g"])
    p.op("DVE", lambda e: e.tensor_single_scalar(out=tf[:, :n].bitcast(I32), in_=tb[:, :n], scalar=0, op=ALU.is_gt), reads=[k + "b"], writes=[k + "f"])
    p.op("DVE", lambda e: e.tensor_single_scalar(out=tf[:, :n].bitcast(I32), in_=tf[:, :n].bitcast(I32), scalar=3, op=ALU.logical_shift_left), reads=[k + "f"], writes=[k + "f"])
    p.op("DVE", lambda e: e.tensor_tensor(out=tf[:, :n].bitcast(I32), in0=tf[:, :n].bitcast(I32), in1=tf2[:, :n].bitcast(I32), op=ALU.add), reads=[k + "f", k + "g"], writes=[k + "f"])
    p.op("DVE", lambda e: e.tensor_copy(out=tf[:, :n], in_=tf[:, :n].bitcast(I32)), reads=[k + "f"], writes=[k + "f"])
    p.op("DVE", lambda e: e.tensor_scalar(out=tb[:, :n], in0=tb[:, :n], scalar1=1, scalar2=117, op0=ALU.max, op1=ALU.add), reads=[k + "b"], writes=[k + "b"])
    p.op("DVE", lambda e: e.tensor_single_scalar(out=tb[:, :n], in_=tb[:, :n], scalar=23, op=ALU.logical_shift_left), reads=[k + "b"], writes=[k + "b"])
    p.op("DVE", lambda e: e.tensor_mul(out=tf[:, :n], in0=tf[:, :n], in1=tb[:, :n].bitcast(F32)), reads=[k + "f", k + "b"], writes=[k + "f"])
    p.op("DVE", lambda e: e.tensor_single_scalar(out=tb[:, :n], in_=ti[:, :n], scalar=7, op=ALU.arith_shift_right), reads=[k + "i"], writes=[k + "b"])
    p.op("DVE", lambda e: e.tensor_copy(out=tf2[:, :n], in_=tb[:, :n]), reads=[k + "b"], writes=[k + "g"])
    p.op("DVE", lambda e: e.tensor_scalar(out=tf2[:, :n], in0=tf2[:, :n], scalar1=-2.0, scalar2=1.0, op0=ALU.mult, op1=ALU.add), reads=[k + "g"], writes=[k + "g"])
    p.op("DVE", lambda e: e.tensor_mul(out=out16_ap, in0=tf[:, :n], in1=tf2[:, :n]), reads=[k + "f", k + "g"], writes=[out_buf])


def load_wT_e4m3(p, tag, w_ap, rows, cols):
    nrc = rows // 128
    nkc = cols // 128
    tiles = [p.sb(f"{tag}_k{kc}", 128, rows, BF16) for kc in range(nkc)]
    bufs = [f"{tag}_k{kc}" for kc in range(nkc)]
    stg8 = p.sb(f"{tag}_s8", 128, cols, U8)
    stg16 = p.sb(f"{tag}_s16", 128, cols, BF16)
    ti = p.sb(f"{tag}_ti", 128, cols, I32)
    tb = p.sb(f"{tag}_tb", 128, cols, I32)
    tf = p.sb(f"{tag}_tf", 128, cols, F32)
    tf2 = p.sb(f"{tag}_tf2", 128, cols, F32)
    tmp = (ti, tb, tf, tf2, tag + "_t")
    for rc in range(nrc):
        p.dma("SP", stg8[:], w_ap[rc * 128:(rc + 1) * 128, :], writes=[tag + "_s8"])
        emit_decode_e4m3(p, stg8[:], tag + "_s8", stg16[:], tag + "_s16", tmp, cols)
        for kc in range(nkc):
            p.dma_tr("SP", tiles[kc][:, rc * 128:(rc + 1) * 128],
                     stg16[:, kc * 128:(kc + 1) * 128],
                     reads=[tag + "_s16"], writes=[bufs[kc]])
    return tiles, bufs


def emit_quant_e4m3(p, v_ap, v_buf, out16_ap, out_buf, tmp, n=D):
    qa, qe, qs, k = tmp
    p.op("ACT", lambda e: e.activation(out=qa[:, :n], in_=v_ap, func=AF.Abs,
                                       bias=0.0, scale=1.0),
         reads=[v_buf], writes=[k + "a"])
    p.op("DVE", lambda e: e.tensor_single_scalar(out=qe[:, :n], in_=qa[:, :n].bitcast(I32), scalar=23, op=ALU.arith_shift_right), reads=[k + "a"], writes=[k + "e"])
    p.op("DVE", lambda e: e.tensor_scalar(out=qe[:, :n], in0=qe[:, :n], scalar1=121, scalar2=-130, op0=ALU.max, op1=ALU.add), reads=[k + "e"], writes=[k + "e"])
    p.op("DVE", lambda e: e.tensor_scalar(out=qs[:, :n], in0=qe[:, :n], scalar1=-1, scalar2=127, op0=ALU.mult, op1=ALU.add), reads=[k + "e"], writes=[k + "s"])
    p.op("DVE", lambda e: e.tensor_single_scalar(out=qs[:, :n], in_=qs[:, :n], scalar=23, op=ALU.logical_shift_left), reads=[k + "s"], writes=[k + "s"])
    p.op("DVE", lambda e: e.tensor_mul(out=qa[:, :n], in0=qa[:, :n], in1=qs[:, :n].bitcast(F32)), reads=[k + "a", k + "s"], writes=[k + "a"])
    p.op("DVE", lambda e: e.tensor_scalar(out=qa[:, :n], in0=qa[:, :n], scalar1=8388608.0, scalar2=-8388608.0, op0=ALU.add, op1=ALU.add), reads=[k + "a"], writes=[k + "a"])
    p.op("DVE", lambda e: e.tensor_single_scalar(out=qs[:, :n], in_=qe[:, :n], scalar=127, op=ALU.add), reads=[k + "e"], writes=[k + "s"])
    p.op("DVE", lambda e: e.tensor_single_scalar(out=qs[:, :n], in_=qs[:, :n], scalar=23, op=ALU.logical_shift_left), reads=[k + "s"], writes=[k + "s"])
    p.op("DVE", lambda e: e.tensor_mul(out=qa[:, :n], in0=qa[:, :n], in1=qs[:, :n].bitcast(F32)), reads=[k + "a", k + "s"], writes=[k + "a"])
    p.op("DVE", lambda e: e.tensor_scalar_min(out=qa[:, :n], in0=qa[:, :n], scalar1=448.0), reads=[k + "a"], writes=[k + "a"])
    p.op("DVE", lambda e: e.tensor_single_scalar(out=qs[:, :n].bitcast(F32), in_=v_ap, scalar=0.0, op=ALU.is_ge), reads=[v_buf], writes=[k + "s"])
    p.op("DVE", lambda e: e.tensor_scalar(out=qs[:, :n].bitcast(F32), in0=qs[:, :n].bitcast(F32), scalar1=2.0, scalar2=-1.0, op0=ALU.mult, op1=ALU.add), reads=[k + "s"], writes=[k + "s"])
    p.op("DVE", lambda e: e.tensor_mul(out=out16_ap, in0=qa[:, :n], in1=qs[:, :n].bitcast(F32)), reads=[k + "a", k + "s"], writes=[out_buf])


def common_consts(p, C):
    eye32 = p.sb("eye32", 128, 128, F32)
    p.dma("SP", eye32[:], C["eye32"], writes=["eye32"])
    ones_f32 = p.sb("ones_f32", 1, 128, F32)
    p.op("DVE", lambda e: e.memset(ones_f32[:], 1.0), writes=["ones_f32"])
    ones_bf16 = p.sb("ones_bf16", 1, 128, BF16)
    p.op("DVE", lambda e: e.memset(ones_bf16[:], 1.0), writes=["ones_bf16"])
    ones_col = p.sb("ones_col", 128, 1, F32)
    p.op("DVE", lambda e: e.memset(ones_col[:], 1.0), writes=["ones_col"])
    return eye32, ones_f32, ones_bf16, ones_col


def emit_l0attn(p, xin, xout, W, C):
    eye32, ones_f32, ones_bf16, ones_col = p.cc
    (pA, kA), (pB, kB), (pC, kC), (pD, kD), (pE, kE), (pF, kF) = p.pool[:6]
    triu = p.sb("triu", 64, 64, F32)
    p.dma("SP", triu[:], C["triu64"], writes=["triu"])

    xld = [p.sb(f"xld_{j}", 128, D, F32) for j in range(2)]
    xn_t = [p.sb(f"xn_{i}", 128, D, F32) for i in range(NTC)]
    scr2l = [p.sb(f"scr{j}", 128, D, F32) for j in range(2)]
    cols2l = [p.sb(f"cols{j}", 128, 8, F32) for j in range(2)]
    colsb2 = [p.sb(f"cols2{j}", 128, 8, F32) for j in range(2)]
    scr = scr2l[0]
    cols = cols2l[0]

    for i in range(NTC):
        j = i % 2
        p.dma("SP", xld[j][:], xin[i * 128:(i + 1) * 128, :], writes=[f"xld_{j}"])
        emit_ln(p, xld[j], f"xld_{j}", xn_t[i][:], f"xn_{i}",
                (scr2l[j], f"scr{j}"), (cols2l[j], f"cols{j}"))

    for dch in range(NDC):
        def mmh(e, dch=dch):
            last = None
            for i in range(NTC):
                last = e.matmul(pB[:, dch:dch + 1],
                                xn_t[i][:, dch * 128:(dch + 1) * 128],
                                ones_col[:], start=(i == 0), stop=(i == NTC - 1))
            return last
        p.op("PE", mmh, reads=[f"xn_{i}" for i in range(NTC)] + ["ones_col"],
             writes=[kB])
    hm_col = p.sb("hm_col", 128, 8, F32)
    c_invT, ck_invT = p.constf(1.0 / T)
    p.op("ACT", lambda e: e.activation(out=hm_col[:], in_=pB[:, 0:8], func=AF.Identity,
                                       bias=0.0, scale=c_invT),
         reads=[kB, ck_invT], writes=["hm_col"])
    hm_colb = p.sb("hm_colb", 128, 8, BF16)
    p.op("ACT", lambda e: e.copy(out=hm_colb[:], in_=hm_col[:]),
         reads=["hm_col"], writes=["hm_colb"])

    tr8 = p.sb("tr8", 8, 128, F32)
    hm_row = p.sb("hm_row", 1, D, F32)
    p.op("PE", lambda e: e.transpose(pC[:8, :128], hm_col[:], eye32[:]),
         reads=["hm_col", "eye32"], writes=[kC])
    p.op("ACT", lambda e: e.copy(out=tr8[:], in_=pC[:8, :128]), reads=[kC],
         writes=["tr8"])
    p.dma("POOL", hm_row[:], tr8[:], reads=["tr8"], writes=["hm_row"])
    hm_bc = bcast_row(p, "hm_bc", hm_row, "hm_row", D, ones_f32, pA, kA)

    NPAR = 2016
    wrow = [p.sb(f"wrow_{i}", 128, D, F32) for i in range(2)]
    prod = p.sb("prod", 128, D, F32)
    params_col = p.sb("params_col", 128, 16, F32)
    for rc in range(16):
        rows = 128 if rc < 15 else NPAR - 15 * 128
        i = rc % 2
        p.dma("SP", wrow[i][:rows, :],
              W["L0_attn_omega_proj_w"][rc * 128:rc * 128 + rows, :],
              writes=[f"wrow_{i}"])
        p.op("DVE", lambda e, i=i, rows=rows: e.tensor_mul(
            out=prod[:rows, :], in0=wrow[i][:rows, :], in1=hm_bc[:rows, :]),
            reads=[f"wrow_{i}", "hm_bc"], writes=["prod"])
        p.op("DVE", lambda e, rc=rc, rows=rows: e.reduce_sum(
            out=params_col[:rows, rc:rc + 1], in_=prod[:rows, :], axis=AX.X),
            reads=["prod"], writes=["params_col"])
    tr16 = p.sb("tr16", 16, 128, F32)
    params_row = p.sb("params_row", 1, 2048, F32)
    p.op("PE", lambda e: e.transpose(pC[:16, :128], params_col[:], eye32[:]),
         reads=["params_col", "eye32"], writes=[kC])
    p.op("ACT", lambda e: e.copy(out=tr16[:], in_=pC[:16, :128]), reads=[kC],
         writes=["tr16"])
    p.dma("POOL", params_row[:], tr16[:], reads=["tr16"], writes=["params_row"])
    brow = load_row(p, "ob_row", W["L0_attn_omega_proj_b"], NPAR)
    p.op("DVE", lambda e: e.tensor_add(out=params_row[:1, :NPAR],
                                       in0=params_row[:1, :NPAR], in1=brow[:]),
         reads=["ob_row", "params_row"], writes=["params_row"])

    stage = p.sb("om_stage", 1, 2304, F32)
    p.op("DVE", lambda e: e.memset(stage[:], 0.0), writes=["om_stage"])
    p.op("DVE", lambda e: e.tensor_copy(out=stage[:1, 64:64 + NPAR],
                                        in_=params_row[:1, :NPAR]),
         reads=["params_row"], writes=["om_stage"])
    om_raw = p.sb("om_raw", 64, 64, F32)
    p.op("DVE", lambda e: e.memset(om_raw[:], 0.0), writes=["om_raw"])
    row_keys = []
    for r in range(63):
        start_r = 63 * r - r * (r - 1) // 2
        off = 64 + start_r - (r + 1)
        rk = f"om_raw_r{r}"
        row_keys.append(rk)
        p.dma("SP", om_raw[r:r + 1, :], stage[:1, off:off + 64],
              reads=["om_stage", "om_raw"], writes=[rk])
    om_u = p.sb("om_u", 64, 64, F32)
    p.op("DVE", lambda e: e.tensor_mul(out=om_u[:], in0=om_raw[:], in1=triu[:]),
         reads=["om_raw", "triu"] + row_keys, writes=["om_u"])
    p.op("PE", lambda e: e.transpose(pC[:64, :64], om_u[:], eye32[:64, :64]),
         reads=["om_u", "eye32"], writes=[kC])
    omega_b = p.sb("omega_b", 64, 64, BF16)
    omg = p.sb("omg", 64, 64, F32)
    p.op("DVE", lambda e: e.tensor_sub(out=omg[:], in0=om_u[:], in1=pC[:64, :64]),
         reads=["om_u", kC], writes=["omg"])
    p.op("ACT", lambda e: e.copy(out=omega_b[:], in_=omg[:]),
         reads=["omg"], writes=["omega_b"])

    mix_o64 = p.sb("mix_o64", 64, 1, F32)
    p.dma("SP", mix_o64[:], W["L0_attn_omega_mix"].broadcast_to([64, 1]),
          writes=["mix_o64"])
    mix_v = p.sb("mix_v", 1, 1, F32)
    p.dma("SP", mix_v[:], W["L0_attn_vq_mix"], writes=["mix_v"])
    mix_o1 = p.sb("mix_o1", 1, 1, F32)
    p.dma("SP", mix_o1[:], W["L0_attn_omega_mix"], writes=["mix_o1"])

    xhead_b = p.sb("xhead_b", 64, T, BF16)
    xheadm_b = p.sb("xheadm_b", 64, T, BF16)
    for i in range(NTC):
        p.op("PE", lambda e, i=i: e.transpose(pC[:64, :128], xn_t[i][:, 0:64], eye32[:]),
             reads=[f"xn_{i}", "eye32"], writes=[kC])
        p.op("ACT", lambda e, i=i: e.copy(
            out=xhead_b[:, i * 128:(i + 1) * 128], in_=pC[:64, :128]),
            reads=[kC], writes=["xhead_b"])
        p.op("ACT", lambda e, i=i: e.activation(
            out=xheadm_b[:, i * 128:(i + 1) * 128], in_=pC[:64, :128], func=AF.Identity,
            bias=0.0, scale=mix_o64[:]),
            reads=[kC, "mix_o64"], writes=["xheadm_b"])

    u_b = [p.sb(f"u{k}_b", 64, T, BF16) for k in range(3)]
    prev, prev_buf = xheadm_b, "xheadm_b"
    scales = [1.0, 0.5, 1.0 / 3.0]
    for k in range(3):
        c_s, ck_s = p.constf(scales[k])
        for q in range(4):
            p.op("PE", lambda e, q=q, prev=prev: e.matmul(
                pD[:64, :512], omega_b[:], prev[:, q * 512:(q + 1) * 512],
                start=True, stop=True),
                reads=["omega_b", prev_buf], writes=[kD])
            p.op("ACT", lambda e, k=k, q=q, c_s=c_s: e.activation(
                out=u_b[k][:, q * 512:(q + 1) * 512], in_=pD[:64, :512],
                func=AF.Identity, bias=0.0, scale=c_s[:64]),
                reads=[kD, ck_s], writes=[f"u{k}_b"])
        prev, prev_buf = u_b[k], f"u{k}_b"

    def load_64T(tag, w_ap):
        s32 = p.sb(tag + "_s32", 128, 64, F32)
        out16 = p.sb(tag, 64, D, BF16)
        for rc in range(8):
            p.dma("SP", s32[:], w_ap[rc * 128:(rc + 1) * 128, :],
                  writes=[tag + "_s32"])
            p.op("PE", lambda e: e.transpose(pD[:64, :128], s32[:], eye32[:]),
                 reads=[tag + "_s32", "eye32"], writes=[kD])
            p.op("ACT", lambda e, rc=rc: e.copy(
                out=out16[:, rc * 128:(rc + 1) * 128], in_=pD[:64, :128]),
                 reads=[kD], writes=[tag])
        return out16
    WoT = load_64T("WoT", W["L0_attn_omega_out_w"])
    VqoT = load_64T("VqoT", W["L0_attn_vq_proj_out_w"])

    vq_in32 = p.sb("vq_in32", 64, D, F32)
    p.dma("SP", vq_in32[:], W["L0_attn_vq_proj_in_w"], writes=["vq_in32"])
    vqinT = p.sb("vqinT", 128, 512, F32)
    for c in range(8):
        p.op("PE", lambda e, c=c: e.transpose(
            pE[:128, :64], vq_in32[:, c * 128:(c + 1) * 128], eye32[:64, :64]),
            reads=["vq_in32", "eye32"], writes=[kE])
        p.op("ACT", lambda e, c=c: e.copy(out=vqinT[:, c * 64:(c + 1) * 64],
                                          in_=pE[:128, :64]),
             reads=[kE], writes=["vqinT"])
    def mmvq(e):
        last = None
        for c in range(8):
            last = e.matmul(pE[:1, :64], hm_col[:, c:c + 1],
                            vqinT[:, c * 64:(c + 1) * 64],
                            start=(c == 0), stop=(c == 7))
        return last
    p.op("PE", mmvq, reads=["hm_col", "vqinT"], writes=[kE])
    vq_row = p.sb("vq_row", 1, 64, F32)
    bvq = load_row(p, "bvq", W["L0_attn_vq_proj_in_b"], 64)
    p.op("DVE", lambda e: e.tensor_add(out=vq_row[:], in0=pE[:1, :64], in1=bvq[:]),
         reads=[kE, "bvq"], writes=["vq_row"])

    sc1 = p.sb("sc1", 1, 8, F32)
    vsq = p.sb("vsq", 1, 64, F32)
    p.op("DVE", lambda e: e.tensor_mul(out=vsq[:], in0=vq_row[:], in1=vq_row[:]),
         reads=["vq_row"], writes=["vsq"])
    p.op("DVE", lambda e: e.reduce_sum(out=sc1[:1, 0:1], in_=vsq[:], axis=AX.X),
         reads=["vsq"], writes=["sc1"])
    p.op("ACT", lambda e: e.activation(out=sc1[:1, 1:2], in_=sc1[:1, 0:1],
                                       func=AF.Sqrt, bias=0.0, scale=1.0),
         reads=["sc1"], writes=["sc1"])
    p.op("DVE", lambda e: e.tensor_scalar_max(out=sc1[:1, 1:2], in0=sc1[:1, 1:2],
                                              scalar1=1e-12),
         reads=["sc1"], writes=["sc1"])
    p.op("DVE", lambda e: e.reciprocal(out=sc1[:1, 2:3], in_=sc1[:1, 1:2]),
         reads=["sc1"], writes=["sc1"])
    nvq = p.sb("nvq", 1, 64, F32)
    p.op("DVE", lambda e: e.tensor_single_scalar(out=nvq[:], in_=vq_row[:],
                                                 scalar=sc1[:1, 2:3], op=ALU.mult),
         reads=["vq_row", "sc1"], writes=["nvq"])

    cb = [p.sb(f"cb_{c}", 128, 64, F32) for c in range(2)]
    ncb = [p.sb(f"ncb_{c}", 128, 64, F32) for c in range(2)]
    cbc = p.sb("cbc", 128, 4, F32)
    for c in range(2):
        p.dma("SP", cb[c][:], W["L0_attn_codebook"][c * 128:(c + 1) * 128, :],
              writes=[f"cb_{c}"])
        p.op("DVE", lambda e, c=c: e.tensor_mul(out=ncb[c][:], in0=cb[c][:],
                                                in1=cb[c][:]),
             reads=[f"cb_{c}"], writes=[f"ncb_{c}"])
        p.op("DVE", lambda e, c=c: e.reduce_sum(out=cbc[:, 0:1], in_=ncb[c][:],
                                                axis=AX.X),
             reads=[f"ncb_{c}"], writes=["cbc"])
        p.op("ACT", lambda e, c=c: e.activation(out=cbc[:, 1:2], in_=cbc[:, 0:1],
                                                func=AF.Sqrt, bias=0.0, scale=1.0),
             reads=["cbc"], writes=["cbc"])
        p.op("DVE", lambda e, c=c: e.tensor_scalar_max(out=cbc[:, 1:2],
                                                       in0=cbc[:, 1:2], scalar1=1e-12),
             reads=["cbc"], writes=["cbc"])
        p.op("DVE", lambda e, c=c: e.reciprocal(out=cbc[:, 2:3], in_=cbc[:, 1:2]),
             reads=["cbc"], writes=["cbc"])
        p.op("DVE", lambda e, c=c: e.tensor_scalar(out=ncb[c][:], in0=cb[c][:],
                                                   scalar1=cbc[:, 2:3], scalar2=None,
                                                   op0=ALU.mult),
             reads=[f"cb_{c}", "cbc"], writes=[f"ncb_{c}"])

    nvq_col = p.sb("nvq_col", 64, 1, F32)
    p.op("PE", lambda e: e.transpose(pE[:64, :1], nvq[:1, :], eye32[:1, :1]),
         reads=["nvq", "eye32"], writes=[kE])
    p.op("ACT", lambda e: e.copy(out=nvq_col[:], in_=pE[:64, :1]),
         reads=[kE], writes=["nvq_col"])
    ncbT = p.sb("ncbT", 64, 256, F32)
    for c in range(2):
        p.op("PE", lambda e, c=c: e.transpose(pE[:64, :128], ncb[c][:], eye32[:]),
             reads=[f"ncb_{c}", "eye32"], writes=[kE])
        p.op("ACT", lambda e, c=c: e.copy(out=ncbT[:, c * 128:(c + 1) * 128],
                                          in_=pE[:64, :128]),
             reads=[kE], writes=["ncbT"])
    p.op("PE", lambda e: e.matmul(pE[:1, :256], nvq_col[:], ncbT[:],
                                  start=True, stop=True),
         reads=["nvq_col", "ncbT"], writes=[kE])
    scores = p.sb("scores", 1, 256, F32)
    p.op("ACT", lambda e: e.copy(out=scores[:], in_=pE[:1, :256]), reads=[kE],
         writes=["scores"])
    p.op("DVE", lambda e: e.reduce_max(out=sc1[:1, 3:4], in_=scores[:], axis=AX.X),
         reads=["scores"], writes=["sc1"])
    oh = p.sb("oh", 1, 256, F32)
    p.op("DVE", lambda e: e.tensor_single_scalar(out=oh[:], in_=scores[:],
                                                 scalar=sc1[:1, 3:4],
                                                 op=ALU.is_equal),
         reads=["scores", "sc1"], writes=["oh"])
    oh_col = p.sb("oh_col", 128, 2, F32)
    for c in range(2):
        p.op("PE", lambda e, c=c: e.transpose(pE[:128, c:c + 1],
                                              oh[:1, c * 128:(c + 1) * 128],
                                              eye32[:1, :1]),
             reads=["oh", "eye32"], writes=[kE])
    p.op("ACT", lambda e: e.copy(out=oh_col[:], in_=pE[:128, :2]), reads=[kE],
         writes=["oh_col"])
    def mmcode(e):
        last = None
        for c in range(2):
            last = e.matmul(pE[:1, :64], oh_col[:, c:c + 1], cb[c][:],
                            start=(c == 0), stop=(c == 1))
        return last
    p.op("PE", mmcode, reads=["oh_col", "cb_0", "cb_1"], writes=[kE])
    vqo = p.sb("vqo", 1, 64, F32)
    p.op("DVE", lambda e: e.tensor_sub(out=vqo[:], in0=pE[:1, :64], in1=vq_row[:]),
         reads=[kE, "vq_row"], writes=["vqo"])
    p.op("DVE", lambda e: e.tensor_add(out=vqo[:], in0=vq_row[:], in1=vqo[:]),
         reads=["vq_row", "vqo"], writes=["vqo"])
    vqo_col = p.sb("vqo_col", 64, 1, BF16)
    p.op("PE", lambda e: e.transpose(pE[:64, :1], vqo[:1, :], eye32[:1, :1]),
         reads=["vqo", "eye32"], writes=[kE])
    p.op("ACT", lambda e: e.copy(out=vqo_col[:], in_=pE[:64, :1]),
         reads=[kE], writes=["vqo_col"])
    vqc = p.sb("vqc", 1, D, F32)
    for h in range(2):
        p.op("PE", lambda e, h=h: e.matmul(pE[:1, :512], vqo_col[:],
                                           VqoT[:, h * 512:(h + 1) * 512],
                                           start=True, stop=True),
             reads=["vqo_col", "VqoT"], writes=[kE])
        p.op("ACT", lambda e, h=h: e.copy(out=vqc[:1, h * 512:(h + 1) * 512],
                                          in_=pE[:1, :512]),
             reads=[kE], writes=["vqc"])
    bvqo = load_row(p, "bvqo", W["L0_attn_vq_proj_out_b"], D)
    p.op("DVE", lambda e: e.tensor_add(out=vqc[:], in0=vqc[:], in1=bvqo[:]),
         reads=["vqc", "bvqo"], writes=["vqc"])
    p.op("DVE", lambda e: e.tensor_single_scalar(out=vqc[:], in_=vqc[:],
                                                 scalar=mix_v[:1, :1], op=ALU.mult),
         reads=["vqc", "mix_v"], writes=["vqc"])
    bo_row = load_row(p, "bo_row", W["L0_attn_omega_out_b"], D)
    p.op("DVE", lambda e: e.tensor_single_scalar(out=bo_row[:], in_=bo_row[:],
                                                 scalar=mix_o1[:1, :1], op=ALU.mult),
         reads=["bo_row", "mix_o1"], writes=["bo_row"])
    combo = p.sb("combo", 1, D, BF16)
    p.op("DVE", lambda e: e.tensor_add(out=combo[:], in0=vqc[:], in1=bo_row[:]),
         reads=["vqc", "bo_row"], writes=["combo"])

    nw_row = load_row(p, "nw_row", W["L0_attn_norm_w"], D)
    nw_bc = bcast_row(p, "nw_bc", nw_row, "nw_row", D, ones_f32, pF, kF)

    if getattr(p, "dbg_l0", False):
        d_xn = p.dram_out("d_xn", (128, D), F32)
        p.dma("POOL", d_xn, xn_t[0][:], reads=["xn_0"], writes=["d_xn"])
        d_hm = p.dram_out("d_hm", (128, 8), F32)
        p.dma("POOL", d_hm, hm_col[:], reads=["hm_col"], writes=["d_hm"])
        d_par = p.dram_out("d_par", (1, 2048), F32)
        p.dma("POOL", d_par, params_row[:], reads=["params_row"], writes=["d_par"])
        d_om = p.dram_out("d_om", (64, 64), F32)
        p.dma("POOL", d_om, omg[:], reads=["omg"], writes=["d_om"])
        d_u1 = p.dram_out("d_u1", (64, T), F32)
        u1f = p.sb("u1f", 64, T, F32)
        p.op("ACT", lambda e: e.copy(out=u1f[:], in_=u_b[0][:]), reads=["u0_b"],
             writes=["u1f"])
        p.dma("POOL", d_u1, u1f[:], reads=["u1f"], writes=["d_u1"])
        d_vqc = p.dram_out("d_vqc", (1, D), F32)
        p.dma("POOL", d_vqc, vqc[:], reads=["vqc"], writes=["d_vqc"])
        d_sc = p.dram_out("d_sc", (1, 256), F32)
        p.dma("POOL", d_sc, scores[:], reads=["scores"], writes=["d_sc"])

    bo_t2 = [p.sb(f"bo_t{j}", 128, D, F32) for j in range(2)]
    comps = [(xheadm_b, "xheadm_b"), (u_b[0], "u0_b"), (u_b[1], "u1_b"),
             (u_b[2], "u2_b")]
    comp_bufs = [cb_ for _, cb_ in comps]
    for i in range(NTC):
        jpar = i % 2
        bo_t, kbo = bo_t2[jpar], f"bo_t{jpar}"
        scr, kscr = scr2l[jpar], f"scr{jpar}"
        cols, kcols = cols2l[jpar], f"cols{jpar}"
        cols2, kcols2 = colsb2[jpar], f"cols2{jpar}"
        ppair = ((pA, kA), (pB, kB)) if jpar == 0 else ((pC, kC), (pD, kD))
        for h, (pl, kl) in enumerate(ppair):
            def mml(e, i=i, h=h, pl=pl):
                for k, (ct, _) in enumerate(comps):
                    e.matmul(pl[:, :512], ct[:, i * 128:(i + 1) * 128],
                             WoT[:, h * 512:(h + 1) * 512],
                             start=(k == 0), stop=False)
                return e.matmul(pl[:, :512], ones_bf16[:1, :],
                                combo[:1, h * 512:(h + 1) * 512],
                                start=False, stop=True)
            p.op("PE", mml, reads=comp_bufs + ["WoT", "combo", "ones_bf16"],
                 writes=[kl])
        p.op("DVE", lambda e, i=i, bo_t=bo_t, pl0=ppair[0][0]: e.tensor_add(
            out=bo_t[:, 0:512], in0=xn_t[i][:, 0:512], in1=pl0[:, :512]),
             reads=[f"xn_{i}", ppair[0][1]], writes=[kbo])
        p.op("DVE", lambda e, i=i, bo_t=bo_t, pl1=ppair[1][0]: e.tensor_add(
            out=bo_t[:, 512:1024], in0=xn_t[i][:, 512:1024], in1=pl1[:, :512]),
             reads=[f"xn_{i}", ppair[1][1]], writes=[kbo])
        if getattr(p, "dbg_l0", False) and i == 0:
            d_bo = p.dram_out("d_bo", (128, D), F32)
            p.dma("POOL", d_bo, bo_t[:], reads=[kbo], writes=["d_bo"])
        p.op("ACT", lambda e, scr=scr, bo_t=bo_t, cols2=cols2: e.activation(
            out=scr[:], in_=bo_t[:], func=AF.Square, bias=0.0, scale=1.0,
            accum_out=cols2[:, 0:1]),
             reads=[kbo], writes=[kscr, kcols2])
        c_e6, ck_e6 = p.constf(1e-6)
        c_iD, ck_iD = p.constf(1.0 / D)
        p.op("ACT", lambda e, cols2=cols2: e.activation(
            out=cols2[:, 1:2], in_=cols2[:, 0:1],
            func=AF.Sqrt, bias=c_e6, scale=c_iD),
             reads=[kcols2, ck_e6, ck_iD], writes=[kcols2])
        p.op("DVE", lambda e, cols2=cols2: e.reciprocal(
            out=cols2[:, 2:3], in_=cols2[:, 1:2]),
             reads=[kcols2], writes=[kcols2])
        p.op("ACT", lambda e, bo_t=bo_t, cols2=cols2: e.activation(
            out=bo_t[:], in_=bo_t[:], func=AF.Identity,
            bias=0.0, scale=cols2[:, 2:3]),
             reads=[kbo, kcols2], writes=[kbo])
        p.op("DVE", lambda e, bo_t=bo_t: e.tensor_mul(
            out=bo_t[:], in0=bo_t[:], in1=nw_bc[:]),
             reads=[kbo, "nw_bc"], writes=[kbo])
        emit_ln(p, bo_t, kbo, scr[:], kscr, (xn_t[i], f"xn_{i}"),
                (cols, kcols))
        j = i % 2
        p.dma("SP", xld[j][:], xin[i * 128:(i + 1) * 128, :], writes=[f"xld_{j}"])
        p.op("DVE", lambda e, i=i, j=j, bo_t=bo_t, scr=scr: e.tensor_add(
            out=bo_t[:], in0=xld[j][:], in1=scr[:]),
             reads=[f"xld_{j}", kscr], writes=[kbo])
        p.dma("POOL", xout[i * 128:(i + 1) * 128, :], bo_t[:],
              reads=[kbo], writes=[f"xo_{i}"])




def emit_l0ffn(p, xin, xout, W, C):
    eye32, ones_f32, ones_bf16, ones_col = p.cc
    (pA, kA), (pB, kB), (pC, kC), (pD, kD), (pE, kE), (pF, kF) = p.pool[:6]
    p.PK_w1b = p.dram_tmp("moe_w1b", (8, 1024, 2048), BF16)
    p.PK_w2b = p.dram_tmp("moe_w2b", (8, 2048, 1024), BF16)
    wst32 = [p.sb(f"f0_ws32_{i}", 128, 2048, F32) for i in range(2)]
    wc16 = [p.sb(f"f0_wc16_{i}", 128, 2048, BF16) for i in range(2)]
    cast_jobs = []
    for e_i in range(8):
        for rc in range(8):
            cast_jobs.append((W["L1_ffn_w1"][e_i, rc * 128:(rc + 1) * 128, :],
                              p.PK_w1b[e_i, rc * 128:(rc + 1) * 128, :], 2048))
        for rc in range(16):
            cast_jobs.append((W["L1_ffn_w2"][e_i, rc * 128:(rc + 1) * 128, :],
                              p.PK_w2b[e_i, rc * 128:(rc + 1) * 128, :], 1024))
    cast_i = [0]

    def emit_casts(k):
        for _ in range(k):
            if cast_i[0] >= len(cast_jobs):
                return
            src_ap, dst_ap, w = cast_jobs[cast_i[0]]
            par = cast_i[0] % 2
            cast_i[0] += 1
            p.dma("SP", wst32[par][:, :w], src_ap, writes=[f"f0_ws32_{par}"])
            p.op("POOL", lambda e, par=par, w=w: e.tensor_copy(
                out=wc16[par][:, :w], in_=wst32[par][:, :w]),
                reads=[f"f0_ws32_{par}"], writes=[f"f0_wc16_{par}"])
            p.dma("POOL", dst_ap, wc16[par][:, :w],
                  reads=[f"f0_wc16_{par}"], writes=["pk_wb"])

    qWT, qWT_bufs = load_wT_e4m3(p, "f0_qwt", W["L0_ffn_weight"], D, D)
    wsb = p.sb("f0_wsb", 128, 1, F32)
    p.dma("SP", wsb[:], W["L0_ffn_weight_scale"].broadcast_to([128, 1]),
          writes=["f0_wsb"])
    b_row = load_row(p, "f0_brow", W["L0_ffn_bias"], D)
    b_bc = bcast_row(p, "f0_bbc", b_row, "f0_brow", D, ones_f32, pF, kF)

    xb = [p.sb(f"f0_x{j}", 128, D, F32) for j in range(2)]
    xnb_2 = [p.sb(f"f0_xn{j}", 128, D, F32) for j in range(2)]
    scr_2 = [p.sb(f"f0_scr{j}", 128, D, F32) for j in range(2)]
    cols_2 = [p.sb(f"f0_cols{j}", 128, 8, F32) for j in range(2)]
    q16_2 = [p.sb(f"f0_q16{j}", 128, D, BF16) for j in range(2)]
    qa_2 = [p.sb(f"f0_qa{j}", 128, D, F32) for j in range(2)]
    qe_2 = [p.sb(f"f0_qe{j}", 128, D, I32) for j in range(2)]
    qs_2 = [p.sb(f"f0_qs{j}", 128, D, I32) for j in range(2)]
    qxT_2 = [p.sb(f"f0_qxT{j}", 128, D, BF16) for j in range(2)]
    ot_2 = [p.sb(f"f0_o{j}", 128, D, F32) for j in range(2)]

    for i in range(NTC):
        j = i % 2
        xk = f"f0_x{j}"
        xnb, scr, cols = xnb_2[j], scr_2[j], cols_2[j]
        q16, qxT, ot = q16_2[j], qxT_2[j], ot_2[j]
        qtmp = (qa_2[j], qe_2[j], qs_2[j], f"f0_q{j}")
        p.dma("SP", xb[j][:], xin[i * 128:(i + 1) * 128, :], writes=[xk])
        emit_ln(p, xb[j], xk, xnb[:], f"f0_xn{j}", (scr, f"f0_scr{j}"),
                (cols, f"f0_cols{j}"))
        am = cols[:, 6:7]
        xs = cols[:, 7:8]
        sinv = cols[:, 5:6]
        s2 = cols[:, 4:5]
        p.op("ACT", lambda e, scr=scr, xnb=xnb: e.activation(
            out=scr[:], in_=xnb[:], func=AF.Abs, bias=0.0, scale=1.0),
             reads=[f"f0_xn{j}"], writes=[f"f0_scr{j}"])
        p.op("DVE", lambda e, am=am, scr=scr: e.tensor_reduce(
            out=am, in_=scr[:], axis=AX.X, op=ALU.max),
             reads=[f"f0_scr{j}"], writes=[f"f0_cols{j}"])
        p.op("DVE", lambda e, am=am: e.tensor_scalar_max(
            out=am, in0=am, scalar1=1e-12),
             reads=[f"f0_cols{j}"], writes=[f"f0_cols{j}"])
        p.op("DVE", lambda e, xs=xs, am=am: e.tensor_scalar_mul(
            out=xs, in0=am, scalar1=1.0 / 448.0),
             reads=[f"f0_cols{j}"], writes=[f"f0_cols{j}"])
        p.op("DVE", lambda e, sinv=sinv, am=am: e.reciprocal(out=sinv, in_=am),
             reads=[f"f0_cols{j}"], writes=[f"f0_cols{j}"])
        p.op("DVE", lambda e, sinv=sinv: e.tensor_scalar_mul(
            out=sinv, in0=sinv, scalar1=448.0),
             reads=[f"f0_cols{j}"], writes=[f"f0_cols{j}"])
        p.op("DVE", lambda e, scr=scr, xnb=xnb, sinv=sinv: e.tensor_scalar(
            out=scr[:], in0=xnb[:], scalar1=sinv, scalar2=None, op0=ALU.mult),
             reads=[f"f0_xn{j}", f"f0_cols{j}"], writes=[f"f0_scr{j}"])
        emit_quant_e4m3(p, scr[:], f"f0_scr{j}", q16[:], f"f0_q16{j}", qtmp)
        for kc in range(NDC):
            p.dma_tr("SP", qxT[:, kc * 128:(kc + 1) * 128],
                     q16[:, kc * 128:(kc + 1) * 128],
                     reads=[f"f0_q16{j}"], writes=[f"f0_qxT{j}"])
        pspair = ((pA, kA), (pB, kB)) if j == 0 else ((pC, kC), (pD, kD))
        for h, (pl, kl) in enumerate(pspair):
            def mmf(e, h=h, pl=pl, qxT=qxT):
                last = None
                for kc in range(NDC):
                    last = e.matmul(pl[:, :512], qxT[:, kc * 128:(kc + 1) * 128],
                                    qWT[kc][:, h * 512:(h + 1) * 512],
                                    start=(kc == 0), stop=(kc == NDC - 1))
                return last
            p.op("PE", mmf, reads=[f"f0_qxT{j}"] + qWT_bufs, writes=[kl],
                 lhsT_reads=[f"f0_qxT{j}"] + qWT_bufs)
        p.op("DVE", lambda e, s2=s2, xs=xs: e.tensor_mul(out=s2, in0=xs, in1=wsb[:]),
             reads=[f"f0_cols{j}", "f0_wsb"], writes=[f"f0_cols{j}"])
        for h, (pl, kl) in enumerate(pspair):
            p.op("ACT", lambda e, h=h, pl=pl, ot=ot, s2=s2: e.activation(
                out=ot[:, h * 512:(h + 1) * 512], in_=pl[:, :512],
                func=AF.Identity, bias=0.0, scale=s2),
                reads=[kl, f"f0_cols{j}"], writes=[f"f0_o{j}"])
        p.op("DVE", lambda e, ot=ot: e.tensor_add(out=ot[:], in0=ot[:], in1=b_bc[:]),
             reads=[f"f0_o{j}", "f0_bbc"], writes=[f"f0_o{j}"])
        if getattr(p, "dbg_f0", False) and i == 0:
            for nm, tile, buf in (("dq_xn", xnb, f"f0_xn{j}"), ("dq_o", ot, f"f0_o{j}"),
                                  ("dq_scr", scr, f"f0_scr{j}")):
                dd = p.dram_out(nm, (128, D), F32)
                p.dma("POOL", dd, tile[:], reads=[buf], writes=[nm])
            dq16 = p.dram_out("dq_q16", (128, D), F32)
            q16f = p.sb("f0_q16f", 128, D, F32)
            p.op("ACT", lambda e, q16f=q16f, q16=q16: e.copy(out=q16f[:], in_=q16[:]),
                 reads=[f"f0_q16{j}"], writes=["f0_q16f"])
            p.dma("POOL", dq16, q16f[:], reads=["f0_q16f"], writes=["dq_q16"])
            dcols = p.dram_out("dq_cols", (128, 8), F32)
            p.dma("POOL", dcols, cols[:], reads=[f"f0_cols{j}"], writes=["dq_cols"])
        emit_ln(p, ot, f"f0_o{j}", scr[:], f"f0_scr{j}", (xnb, f"f0_xn{j}"),
                (cols, f"f0_cols{j}"))
        p.op("DVE", lambda e, j=j, ot=ot, scr=scr: e.tensor_add(
            out=ot[:], in0=xb[j][:], in1=scr[:]),
             reads=[xk, f"f0_scr{j}"], writes=[f"f0_o{j}"])
        p.dma("POOL", xout[i * 128:(i + 1) * 128, :], ot[:],
              reads=[f"f0_o{j}"], writes=[f"f0_xo_{i}"])
        emit_casts(12)




def emit_l1attn(p, xin, xout, W, C):
    eye32, ones_f32, ones_bf16, ones_col = p.cc
    (pA, kA), (pB, kB), (pC, kC), (pD, kD), (pE, kE), (pF, kF) = p.pool[:6]
    E = 2048
    NJ = 16

    PK_xcb = p.dram_tmp("a1_pk_xcb", (NJ, 128, T), BF16)
    PK_xcf = p.dram_tmp("a1_pk_xcf", (NJ, 128, T), F32)
    PK_xn2 = p.dram_tmp("a1_pk_xn2", (NDC, 128, T), BF16)
    PK_y = p.dram_tmp("a1_pk_y", (NJ, 128, T), F32)
    PK_xo = p.dram_tmp("a1_pk_xo", (T, D), F32)

    p.push_region()
    ina_row = load_row(p, "a1_ina", W["L1_attn_in_norm_alpha"], D)
    inb_row = load_row(p, "a1_inb", W["L1_attn_in_norm_bias"], D)
    ina_bc = bcast_row(p, "a1_inabc", ina_row, "a1_ina", D, ones_f32, pE, kE)
    inb_bc = bcast_row(p, "a1_inbbc", inb_row, "a1_inb", D, ones_f32, pE, kE)
    WxT, WxT_bufs = load_wT_bf16(p, "a1_wxt", W["L1_attn_in_proj_w"][0:E, :],
                                 E, D)
    ipb_row = load_row(p, "a1_ipb", W["L1_attn_in_proj_b"], 2 * E)
    bx_cols = bias_cols(p, "a1_bx", ipb_row[:1, 0:E], "a1_ipb", E, eye32, pE, kE)
    cb_row = load_row(p, "a1_cbr", W["L1_attn_conv_b"], E)
    cb_cols = bias_cols(p, "a1_cb", cb_row, "a1_cbr", E, eye32, pE, kE)
    wk_cols = p.sb("a1_wk", 128, 64, F32)
    for j in range(NJ):
        p.dma("SP", wk_cols[:, j * 4:(j + 1) * 4],
              W["L1_attn_conv_w"][j * 128:(j + 1) * 128, :], writes=["a1_wk"])

    xld = [p.sb(f"a1_x{i}", 128, D, F32) for i in range(2)]
    xo_t = p.sb("a1_xo", 128, D, F32)
    xn2_t = p.sb("a1_xn2", 128, D, BF16)
    scr = p.sb("a1_scr", 128, D, F32)
    cols = p.sb("a1_cols", 128, 8, F32)
    xn2FM = [p.sb(f"a1_xn2fm{kc}", 128, T, BF16) for kc in range(NDC)]

    for i in range(NTC):
        jj = i % 2
        xk = f"a1_x{jj}"
        p.dma("SP", xld[jj][:], xin[i * 128:(i + 1) * 128, :], writes=[xk])
        emit_ln(p, xld[jj], xk, xo_t[:], "a1_xo", (scr, "a1_scr"),
                (cols, "a1_cols"))
        p.dma("POOL", PK_xo[i * 128:(i + 1) * 128, :], xo_t[:],
              reads=["a1_xo"], writes=["pk_xo"])
        emit_ln(p, xo_t, "a1_xo", xn2_t[:], "a1_xn2", (scr, "a1_scr"),
                (cols, "a1_cols"), kind="inner", alpha_bc=ina_bc, beta_bc=inb_bc,
                ab_bufs=("a1_inabc", "a1_inbbc"))
        for kc in range(NDC):
            p.dma_tr("SP", xn2FM[kc][:, i * 128:(i + 1) * 128],
                     xn2_t[:, kc * 128:(kc + 1) * 128],
                     reads=["a1_xn2"], writes=[f"a1_xn2fm{kc}"])
    for kc in range(NDC):
        p.dma("POOL", PK_xn2[kc], xn2FM[kc][:],
              reads=[f"a1_xn2fm{kc}"], writes=["pk_xn2"])

    xin_j = [p.sb(f"a1_xinj{i}", 128, T + 3, F32) for i in range(2)]
    xcs_j = [p.sb(f"a1_xcsj{i}", 128, T, F32) for i in range(2)]
    xcs16 = [p.sb(f"a1_xcs16{i}", 128, T, BF16) for i in range(2)]
    for i in range(2):
        p.op("DVE", lambda e, i=i: e.memset(xin_j[i][:, 0:3], 0.0),
             writes=[f"a1_xinj{i}"])
    for j in range(NJ):
        par = j % 2
        xi, xc, x16 = xin_j[par], xcs_j[par], xcs16[par]
        ki, kx, k16 = f"a1_xinj{par}", f"a1_xcsj{par}", f"a1_xcs16{par}"
        for q in range(4):
            pl, kl = (pA, kA) if q % 2 == 0 else (pB, kB)
            def mmx(e, j=j, q=q, pl=pl):
                last = None
                for kc in range(NDC):
                    last = e.matmul(pl[:, :512],
                                    WxT[kc][:, j * 128:(j + 1) * 128],
                                    xn2FM[kc][:, q * 512:(q + 1) * 512],
                                    start=(kc == 0), stop=(kc == NDC - 1))
                return last
            p.op("PE", mmx, reads=WxT_bufs + [f"a1_xn2fm{kc}" for kc in range(NDC)],
                 writes=[kl], lhsT_reads=WxT_bufs)
            p.op("ACT", lambda e, j=j, q=q, pl=pl, xi=xi: e.activation(
                out=xi[:, 3 + q * 512:3 + (q + 1) * 512], in_=pl[:, :512],
                func=AF.Identity, bias=bx_cols[:, j:j + 1], scale=1.0),
                reads=[kl, "a1_bx"], writes=[ki])
        p.op("DVE", lambda e, j=j, xi=xi, xc=xc: e.tensor_scalar(
            out=xc[:], in0=xi[:, 0:T], scalar1=wk_cols[:, j * 4:j * 4 + 1],
            scalar2=None, op0=ALU.mult),
            reads=[ki, "a1_wk"], writes=[kx])
        for kk in range(1, 4):
            p.op("DVE", lambda e, j=j, kk=kk, xi=xi, xc=xc: e.scalar_tensor_tensor(
                out=xc[:], in0=xi[:, kk:kk + T],
                scalar=wk_cols[:, j * 4 + kk:j * 4 + kk + 1], in1=xc[:],
                op0=ALU.mult, op1=ALU.add),
                reads=[ki, "a1_wk", kx], writes=[kx])
        p.op("ACT", lambda e, j=j, xc=xc: e.activation(
            out=xc[:], in_=xc[:], func=AF.Silu,
            bias=cb_cols[:, j:j + 1], scale=1.0),
            reads=[kx, "a1_cb"], writes=[kx])
        p.dma("POOL", PK_xcf[j], xc[:], reads=[kx], writes=["pk_xcf"])
        p.op("ACT", lambda e, xc=xc, x16=x16: e.copy(out=x16[:], in_=xc[:]),
             reads=[kx], writes=[k16])
        p.dma("POOL", PK_xcb[j], x16[:], reads=[k16], writes=["pk_xcb"])
    p.pop_region()

    p.push_region()
    TS_B = 256
    NSL = T // TS_B
    TC = 8
    NCH = TS_B // TC
    dtWT, dtWT_bufs = load_wT_bf16(p, "a1_dtw", W["L1_attn_dt_proj_w"], E, E)
    xpT = p.sb("a1_xpt", 128, 512, BF16)
    p.push_region()
    xp32 = p.sb("a1_xp32", 32, E, F32)
    xp16 = p.sb("a1_xp16", 32, E, BF16)
    p.dma("SP", xp32[:], W["L1_attn_x_proj_w"], writes=["a1_xp32"])
    p.op("ACT", lambda e: e.copy(out=xp16[:], in_=xp32[:]),
         reads=["a1_xp32"], writes=["a1_xp16"])
    for kc in range(NJ):
        p.dma_tr("SP", xpT[:, kc * 32:(kc + 1) * 32],
                 xp16[:, kc * 128:(kc + 1) * 128],
                 reads=["a1_xp16"], writes=["a1_xpt"])
    p.pop_region()
    dtb_row = load_row(p, "a1_dtbr", W["L1_attn_dt_proj_b"], E)
    dtb_cols = bias_cols(p, "a1_dtb", dtb_row, "a1_dtbr", E, eye32, pE, kE)
    xpb_row = load_row(p, "a1_xpbr", W["L1_attn_x_proj_b"], 32)
    xpb_cols = p.sb("a1_xpb", 32, 2, F32)
    for half in range(2):
        p.op("PE", lambda e, half=half: e.transpose(
            pE[:16, :1], xpb_row[:1, half * 16:(half + 1) * 16], eye32[:1, :1]),
            reads=["a1_xpbr", "eye32"], writes=[kE])
        p.op("ACT", lambda e, half=half: e.copy(out=xpb_cols[:16, half:half + 1],
                                                in_=pE[:16, :1]),
             reads=[kE], writes=["a1_xpb"])
    D_row = load_row(p, "a1_dr", W["L1_attn_D_param"], E)
    D_cols = bias_cols(p, "a1_dc", D_row, "a1_dr", E, eye32, pE, kE)
    A_tile = p.sb("a1_at", 128, 256, F32)
    for j in range(NJ):
        p.dma("SP", A_tile[:, j * 16:(j + 1) * 16],
              W["L1_attn_A_log"][j * 128:(j + 1) * 128, :], writes=["a1_at"])
    p.op("ACT", lambda e: e.activation(out=A_tile[:], in_=A_tile[:], func=AF.Exp,
                                       bias=0.0, scale=1.0),
         reads=["a1_at"], writes=["a1_at"])
    p.op("DVE", lambda e: e.tensor_scalar_mul(out=A_tile[:], in0=A_tile[:],
                                              scalar1=-1.0),
         reads=["a1_at"], writes=["a1_at"])

    xcb_sl = p.sb("a1_xcbs", 128, NJ * TS_B, BF16)
    xcf_sl = p.sb("a1_xcfs", 128, NJ * TS_B, F32)
    dl_sl = p.sb("a1_dls", 128, NJ * TS_B, F32)
    y_sl = p.sb("a1_ys", 128, NJ * TS_B, F32)
    _zb1 = p.sb("a1_zb0", 128, NJ * 16 * TC, F32)
    Zb2 = [_zb1, _zb1]
    dAb2 = [p.sb(f"a1_dab{i}", 128, NJ * 16 * TC, F32) for i in range(2)]
    Ub2 = [p.sb(f"a1_ub{i}", 128, NJ * 16 * TC, F32) for i in range(2)]
    Hb2 = [p.sb(f"a1_hb{i}", 128, NJ * 16 * TC, F32) for i in range(2)]
    sq2 = [p.sb("a1_sq2", 128, 256, F32)] * 2
    ssj2 = [p.sb("a1_ssj", 128, 16, F32)] * 2
    sBrep2 = [p.sb(f"a1_brep{i}", 128, 16 * TC, F32) for i in range(2)]
    sCrep2 = [p.sb(f"a1_crep{i}", 128, 16 * TC, F32) for i in range(2)]
    _brow = p.sb("a1_brow0", 1, 16 * TC, F32)
    _crow = p.sb("a1_crow0", 1, 16 * TC, F32)
    Brow2 = [_brow, _brow]
    Crow2 = [_crow, _crow]
    wch2 = [p.sb(f"a1_wch{i}", 128, NJ * TC, F32) for i in range(2)]
    B_sl = p.sb("a1_bsl", 16, TS_B, F32)
    C_sl = p.sb("a1_csl", 16, TS_B, F32)
    _spa = p.sb("a1_spa0", 128, TS_B, F32)
    spa = [_spa, _spa]
    _spb = p.sb("a1_spb0", 128, TS_B, F32)
    spb = [_spb, _spb]
    c_neg1 = p.constf(-1.0)
    c_e10 = float(np.exp(np.float32(-10.0)))

    def vch(t, j_s, n_s, t_s, off=0):
        return bass.AP(t.tensor if hasattr(t, "tensor") else None, 0, [])

    for ts in range(NSL):
        for j in range(NJ):
            p.dma("SP", xcb_sl[:, j * TS_B:(j + 1) * TS_B],
                  PK_xcb[j][:, ts * TS_B:(ts + 1) * TS_B],
                  reads=["pk_xcb"], writes=["a1_xcbs"])
            p.dma("SP", xcf_sl[:, j * TS_B:(j + 1) * TS_B],
                  PK_xcf[j][:, ts * TS_B:(ts + 1) * TS_B],
                  reads=["pk_xcf"], writes=["a1_xcfs"])
        for m in range(NJ):
            par = m % 2
            pl, kl = (pA, kA) if par == 0 else (pB, kB)
            sa, sb = spa[par][:], spb[par][:]
            ka_, kb_ = "a1_spa0", "a1_spb0"
            def mmd(e, m=m, pl=pl):
                last = None
                for kc in range(NJ):
                    last = e.matmul(pl[:, :TS_B],
                                    dtWT[kc][:, m * 128:(m + 1) * 128],
                                    xcb_sl[:, kc * TS_B:(kc + 1) * TS_B],
                                    start=(kc == 0), stop=(kc == NJ - 1))
                return last
            p.op("PE", mmd, reads=dtWT_bufs + ["a1_xcbs"], writes=[kl],
                 lhsT_reads=dtWT_bufs)
            p.op("DVE", lambda e, m=m, pl=pl, sa=sa: e.tensor_scalar(
                out=sa, in0=pl[:, :TS_B], scalar1=dtb_cols[:, m:m + 1],
                scalar2=15.0, op0=ALU.add, op1=ALU.min),
                reads=[kl, "a1_dtb"], writes=[ka_])
            p.op("ACT", lambda e, sa=sa, sb=sb: e.activation(
                out=sb, in_=sa, func=AF.Exp, bias=0.0, scale=1.0),
                reads=[ka_], writes=[kb_])
            p.op("ACT", lambda e, sa=sa, sb=sb: e.activation(
                out=sa, in_=sb, func=AF.Ln, bias=1.0, scale=1.0),
                reads=[kb_], writes=[ka_])
            p.op("DVE", lambda e, m=m, sa=sa: e.tensor_scalar(
                out=dl_sl[:, m * TS_B:(m + 1) * TS_B], in0=sa, scalar1=10.0,
                scalar2=1e-4, op0=ALU.min, op1=ALU.max),
                reads=[ka_], writes=["a1_dls"])
        for half, (dst, dk) in enumerate(((B_sl, "a1_bsl"), (C_sl, "a1_csl"))):
            def mmbc(e, half=half):
                last = None
                for kc in range(NJ):
                    last = e.matmul(pD[:16, :TS_B],
                                    xpT[:, kc * 32 + half * 16:kc * 32 + half * 16 + 16],
                                    xcb_sl[:, kc * TS_B:(kc + 1) * TS_B],
                                    start=(kc == 0), stop=(kc == NJ - 1))
                return last
            p.op("PE", mmbc, reads=["a1_xpt", "a1_xcbs"], writes=[kD],
                 lhsT_reads=["a1_xpt"])
            p.op("ACT", lambda e, half=half, dst=dst: e.activation(
                out=dst[:], in_=pD[:16, :TS_B], func=AF.Identity,
                bias=xpb_cols[:16, half:half + 1], scale=1.0),
                reads=[kD, "a1_xpb"], writes=[dk])
        def scan_pre(ch):
            t0 = ch * TC
            gch = ts * NCH + ch
            cpar = gch % 2
            Zb, dAb, Ub, Hb = Zb2[cpar], dAb2[cpar], Ub2[cpar], Hb2[cpar]
            sBrep, sCrep = sBrep2[cpar], sCrep2[cpar]
            Brow, Crow = Brow2[cpar], Crow2[cpar]
            kzb, kdab, kub = "a1_zb0", f"a1_dab{cpar}", f"a1_ub{cpar}"
            kbrep, kcrep = f"a1_brep{cpar}", f"a1_crep{cpar}"
            kbrow, kcrow = "a1_brow0", "a1_crow0"
            p.dma("SP", Brow[:].rearrange("o (n t) -> o n t", n=16),
                  B_sl[:, t0:t0 + TC], reads=["a1_bsl"], writes=[kbrow])
            p.dma("SP", Crow[:].rearrange("o (n t) -> o n t", n=16),
                  C_sl[:, t0:t0 + TC], reads=["a1_csl"], writes=[kcrow])
            p.op("PE", lambda e, Brow=Brow: e.matmul(pE[:, :16 * TC], ones_f32[:1, :],
                                          Brow[:], start=True, stop=True),
                 reads=[kbrow, "ones_f32"], writes=[kE],
                 lhsT_reads=["ones_f32"])
            p.op("ACT", lambda e, sBrep=sBrep: e.copy(out=sBrep[:], in_=pE[:, :16 * TC]),
                 reads=[kE], writes=[kbrep])
            p.op("PE", lambda e, Crow=Crow: e.matmul(pF[:, :16 * TC], ones_f32[:1, :],
                                          Crow[:], start=True, stop=True),
                 reads=[kcrow, "ones_f32"], writes=[kF],
                 lhsT_reads=["ones_f32"])
            p.op("ACT", lambda e, sCrep=sCrep: e.copy(out=sCrep[:], in_=pF[:, :16 * TC]),
                 reads=[kF], writes=[kcrep])
            zv = Zb[:].rearrange("p (j n t) -> p j n t", j=NJ, n=16)
            dl_ap = dl_sl[:].rearrange("p (j t) -> p j t", j=NJ)[:, :, t0:t0 + TC]
            dl_b = dl_ap.unsqueeze(2).broadcast_to([128, NJ, 16, TC])
            A_ap = A_tile[:].rearrange("p (j n) -> p j n", j=NJ).unsqueeze(3).broadcast_to([128, NJ, 16, TC])
            p.op("POOL", lambda e, zv=zv, dl_b=dl_b, A_ap=A_ap: e.tensor_tensor(
                out=zv, in0=dl_b, in1=A_ap, op=ALU.mult),
                reads=["a1_dls", "a1_at"], writes=[kzb])
            c10, ck10 = p.constf(10.0)
            cn10, ckn10 = p.constf(-10.0)
            p.op("ACT", lambda e, Zb=Zb, c10=c10: e.activation(
                out=Zb[:], in_=Zb[:], func=AF.Relu, bias=c10, scale=1.0),
                 reads=[kzb, ck10], writes=[kzb])
            p.op("ACT", lambda e, dAb=dAb, Zb=Zb, cn10=cn10: e.activation(
                out=dAb[:], in_=Zb[:], func=AF.Exp, bias=cn10, scale=1.0),
                 reads=[kzb, ckn10], writes=[kdab])
            wcht, kwch = wch2[cpar], f"a1_wch{cpar}"
            wv = wcht[:].rearrange("p (j t) -> p j t", j=NJ)
            xcf_ap = xcf_sl[:].rearrange("p (j t) -> p j t", j=NJ)[:, :, t0:t0 + TC]
            p.op("POOL", lambda e, wv=wv, dl_ap=dl_ap, xcf_ap=xcf_ap: e.tensor_tensor(
                out=wv, in0=dl_ap, in1=xcf_ap, op=ALU.mult),
                reads=["a1_dls", "a1_xcfs"], writes=[kwch])
            uv = Ub[:].rearrange("p (j n t) -> p j n t", j=NJ, n=16)
            w_b = wv.unsqueeze(2).broadcast_to([128, NJ, 16, TC])
            br_b = sBrep[:].rearrange("p (n t) -> p n t", n=16).unsqueeze(1).broadcast_to([128, NJ, 16, TC])
            p.op("POOL", lambda e, uv=uv, w_b=w_b, br_b=br_b: e.tensor_tensor(
                out=uv, in0=w_b, in1=br_b, op=ALU.mult),
                reads=[kwch, kbrep], writes=[kub])

        def scan_back(ch):
            t0 = ch * TC
            gch = ts * NCH + ch
            cpar = gch % 2
            Zb, dAb, Ub, Hb = Zb2[cpar], dAb2[cpar], Ub2[cpar], Hb2[cpar]
            sq256, ssj = sq2[0], ssj2[0]
            sCrep = sCrep2[cpar]
            kzb, kdab, kub, khb = ("a1_zb0", f"a1_dab{cpar}",
                                   f"a1_ub{cpar}", f"a1_hb{cpar}")
            ksq, kssj = "a1_sq2", "a1_ssj"
            kcrep = f"a1_crep{cpar}"
            da0v = dAb[:].rearrange("p (j n t) -> p j n t", j=NJ, n=16)[:, :, :, 0:1]
            if gch > 0:
                u0 = Ub[:].rearrange("p (j n t) -> p j n t", j=NJ, n=16)[:, :, :, 0:1]
                Hprev = Hb2[(gch - 1) % 2]
                khprev = f"a1_hb{(gch - 1) % 2}"
                hp = Hprev[:].rearrange("p (j n t) -> p j n t", j=NJ, n=16)[:, :, :, TC - 1:TC]
                p.op("DVE", lambda e, da0v=da0v, hp=hp, sq256=sq256: e.tensor_tensor(
                    out=sq256[:].rearrange("p (j n) -> p j n", j=NJ).unsqueeze(3),
                    in0=da0v, in1=hp, op=ALU.mult),
                    reads=[kdab, khprev], writes=[ksq])
                p.op("DVE", lambda e, u0=u0, sq256=sq256: e.tensor_tensor(
                    out=u0, in0=u0,
                    in1=sq256[:].rearrange("p (j n) -> p j n", j=NJ).unsqueeze(3),
                    op=ALU.add),
                    reads=[kub, ksq], writes=[kub])
            p.op("DVE", lambda e, da0v=da0v: e.memset(da0v, 0.0),
                 reads=[], writes=[kdab])
            p.op("DVE", lambda e, Hb=Hb, dAb=dAb, Ub=Ub: e.tensor_tensor_scan(
                out=Hb[:], data0=dAb[:], data1=Ub[:], initial=0.0,
                op0=ALU.mult, op1=ALU.add),
                reads=[kdab, kub], writes=[khb])
            h15 = Hb[:].rearrange("p (j n t) -> p j n t", j=NJ, n=16)[:, :, :, TC - 1:TC]
            if gch % 2 == 1:
                p.op("DVE", lambda e, h15=h15, sq256=sq256: e.tensor_tensor(
                    out=sq256[:].rearrange("p (j n) -> p j n", j=NJ).unsqueeze(3),
                    in0=h15, in1=h15, op=ALU.mult),
                    reads=[khb], writes=[ksq])
                p.op("DVE", lambda e, sq256=sq256, ssj=ssj: e.tensor_reduce(
                    out=ssj[:], in_=sq256[:].rearrange("p (j n) -> p j n", j=NJ),
                    axis=AX.X, op=ALU.add),
                    reads=[ksq], writes=[kssj])
                p.op("ACT", lambda e, ssj=ssj: e.activation(out=ssj[:], in_=ssj[:],
                                                   func=AF.Ln, bias=0.0,
                                                   scale=1.0),
                     reads=[kssj], writes=[kssj])
                c_nh, ck_nh = p.constf(-0.5)
                p.op("ACT", lambda e, ssj=ssj, c_nh=c_nh: e.activation(
                    out=ssj[:], in_=ssj[:], func=AF.Exp, bias=0.0, scale=c_nh),
                     reads=[kssj, ck_nh], writes=[kssj])
                p.op("DVE", lambda e, ssj=ssj: e.tensor_scalar(out=ssj[:], in0=ssj[:],
                                                      scalar1=10.0, scalar2=1.0,
                                                      op0=ALU.mult, op1=ALU.min),
                     reads=[kssj], writes=[kssj])
                sc_b = ssj[:].unsqueeze(2).broadcast_to([128, NJ, 16]).unsqueeze(3)
                p.op("DVE", lambda e, h15=h15, sc_b=sc_b: e.tensor_tensor(
                    out=h15, in0=h15, in1=sc_b, op=ALU.mult),
                    reads=[khb, kssj], writes=[khb])
            sv = Ub[:].rearrange("p (j n t) -> p j n t", j=NJ, n=16)
            cr_b = sCrep[:].rearrange("p (n t) -> p n t", n=16).unsqueeze(1).broadcast_to([128, NJ, 16, TC])
            hv = Hb[:].rearrange("p (j n t) -> p j n t", j=NJ, n=16)
            p.op("POOL", lambda e, sv=sv, hv=hv, cr_b=cr_b: e.tensor_tensor(
                out=sv, in0=hv, in1=cr_b, op=ALU.mult),
                reads=[khb, kcrep], writes=[kub])

        def scan_red(ch):
            t0 = ch * TC
            gch = ts * NCH + ch
            cpar = gch % 2
            Ub = Ub2[cpar]
            kub = f"a1_ub{cpar}"
            y_ap = y_sl[:].rearrange("p (j t) -> p j t", j=NJ)[:, :, t0:t0 + TC]
            s_re = Ub[:].rearrange("p (j n t) -> p j t n", j=NJ, n=16)
            p.op("DVE", lambda e, y_ap=y_ap, s_re=s_re: e.tensor_reduce(
                out=y_ap, in_=s_re, axis=AX.X, op=ALU.add),
                reads=[kub], writes=["a1_ys"])

        for ch in range(NCH):
            if ch > 1:
                scan_red(ch - 2)
            scan_pre(ch)
            if ch > 0:
                scan_back(ch - 1)
        scan_back(NCH - 1)
        scan_red(NCH - 2)
        scan_red(NCH - 1)
        dcol_b = D_cols[:].unsqueeze(2).broadcast_to([128, 16, TS_B])
        xcf_v = xcf_sl[:].rearrange("p (j t) -> p j t", j=NJ)
        p.op("POOL", lambda e, dcol_b=dcol_b, xcf_v=xcf_v: e.tensor_tensor(
            out=xcf_v, in0=xcf_v, in1=dcol_b, op=ALU.mult),
            reads=["a1_xcfs", "a1_dc"], writes=["a1_xcfs"])
        p.op("DVE", lambda e: e.tensor_add(out=y_sl[:], in0=y_sl[:], in1=xcf_sl[:]),
             reads=["a1_ys", "a1_xcfs"], writes=["a1_ys"])
        for j in range(NJ):
            p.dma("POOL", PK_y[j][:, ts * TS_B:(ts + 1) * TS_B],
                  y_sl[:, j * TS_B:(j + 1) * TS_B],
                  reads=["a1_ys"], writes=["pk_y"])
    p.pop_region()

    p.push_region()
    WzT, WzT_bufs = load_wT_bf16(p, "a1_wzt", W["L1_attn_in_proj_w"][E:2 * E, :],
                                 E, D)
    WoT2, WoT2_bufs = load_wT_bf16(p, "a1_wot", W["L1_attn_out_proj_w"], D, E)
    ipb2_row = load_row(p, "a1_ipb2", W["L1_attn_in_proj_b"], 2 * E)
    bz_cols = bias_cols(p, "a1_bz", ipb2_row[:1, E:2 * E], "a1_ipb2", E, eye32,
                        pE, kE)
    ob_row = load_row(p, "a1_obr", W["L1_attn_out_proj_b"], D)
    ob_bc = bcast_row(p, "a1_obc", ob_row, "a1_obr", D, ones_f32, pE, kE)
    na_row = load_row(p, "a1_nar", W["L1_attn_norm_alpha"], D)
    nb_row = load_row(p, "a1_nbr", W["L1_attn_norm_bias"], D)
    na_bc = bcast_row(p, "a1_nabc", na_row, "a1_nar", D, ones_f32, pE, kE)
    nb_bc = bcast_row(p, "a1_nbbc", nb_row, "a1_nbr", D, ones_f32, pE, kE)

    TS_D = 256
    xn2_sl = p.sb("d1_xn2", 128, NDC * TS_D, BF16)
    yd_sl = p.sb("d1_y", 128, NJ * TS_D, F32)
    gate_sl = p.sb("d1_gate", 128, NJ * TS_D, BF16)
    zs_j = [p.sb(f"d1_zs{i}", 128, TS_D, F32) for i in range(2)]
    bo_t = p.sb("d1_bo", 128, D, F32)
    xo2_t = p.sb("d1_xo", 128, D, F32)
    x2_t = p.sb("d1_x2", 128, D, F32)
    scr2 = p.sb("d1_scr", 128, D, F32)
    cols2 = p.sb("d1_cols", 128, 8, F32)
    for ts in range(T // TS_D):
        for kc in range(NDC):
            p.dma("SP", xn2_sl[:, kc * TS_D:(kc + 1) * TS_D],
                  PK_xn2[kc][:, ts * TS_D:(ts + 1) * TS_D],
                  reads=["pk_xn2"], writes=["d1_xn2"])
        for j in range(NJ):
            p.dma("SP", yd_sl[:, j * TS_D:(j + 1) * TS_D],
                  PK_y[j][:, ts * TS_D:(ts + 1) * TS_D],
                  reads=["pk_y"], writes=["d1_y"])
        for j in range(NJ):
            par = j % 2
            pl, kl = (pA, kA) if par == 0 else (pB, kB)
            zs, kz = zs_j[par], f"d1_zs{par}"
            def mmz(e, j=j, pl=pl):
                last = None
                for kc in range(NDC):
                    last = e.matmul(pl[:, :TS_D],
                                    WzT[kc][:, j * 128:(j + 1) * 128],
                                    xn2_sl[:, kc * TS_D:(kc + 1) * TS_D],
                                    start=(kc == 0), stop=(kc == NDC - 1))
                return last
            p.op("PE", mmz, reads=WzT_bufs + ["d1_xn2"], writes=[kl],
                 lhsT_reads=WzT_bufs)
            p.op("ACT", lambda e, j=j, pl=pl, zs=zs: e.activation(
                out=zs[:], in_=pl[:, :TS_D], func=AF.Silu,
                bias=bz_cols[:, j:j + 1], scale=1.0),
                reads=[kl, "a1_bz"], writes=[kz])
            p.op("DVE", lambda e, j=j, zs=zs: e.tensor_mul(
                out=gate_sl[:, j * TS_D:(j + 1) * TS_D],
                in0=yd_sl[:, j * TS_D:(j + 1) * TS_D], in1=zs[:]),
                reads=["d1_y", kz], writes=["d1_gate"])
        for tc in range(TS_D // 128):
            g = ts * (TS_D // 128) + tc
            for h, (pl, kl) in enumerate(((pC, kC), (pD, kD))):
                def mmo(e, h=h, pl=pl, tc=tc):
                    last = None
                    for kc in range(NJ):
                        last = e.matmul(pl[:, :512],
                                        gate_sl[:, kc * TS_D + tc * 128:kc * TS_D + (tc + 1) * 128],
                                        WoT2[kc][:, h * 512:(h + 1) * 512],
                                        start=(kc == 0), stop=(kc == NJ - 1))
                    return last
                p.op("PE", mmo, reads=WoT2_bufs + ["d1_gate"], writes=[kl],
                     lhsT_reads=["d1_gate"])
                p.op("ACT", lambda e, h=h, pl=pl: e.activation(
                    out=bo_t[:, h * 512:(h + 1) * 512], in_=pl[:, :512],
                    func=AF.Identity, bias=0.0, scale=1.0),
                    reads=[kl], writes=["d1_bo"])
            p.op("DVE", lambda e: e.tensor_add(out=bo_t[:], in0=bo_t[:],
                                               in1=ob_bc[:]),
                 reads=["d1_bo", "a1_obc"], writes=["d1_bo"])
            p.dma("SP", xo2_t[:], PK_xo[g * 128:(g + 1) * 128, :],
                  reads=["pk_xo"], writes=["d1_xo"])
            p.op("DVE", lambda e: e.tensor_add(out=bo_t[:], in0=bo_t[:],
                                               in1=xo2_t[:]),
                 reads=["d1_bo", "d1_xo"], writes=["d1_bo"])
            emit_ln(p, bo_t, "d1_bo", scr2[:], "d1_scr", (xo2_t, "d1_xo"),
                    (cols2, "d1_cols"), kind="inner", alpha_bc=na_bc,
                    beta_bc=nb_bc, ab_bufs=("a1_nabc", "a1_nbbc"))
            emit_ln(p, scr2, "d1_scr", bo_t[:], "d1_bo", (xo2_t, "d1_xo"),
                    (cols2, "d1_cols"))
            p.dma("SP", x2_t[:], xin[g * 128:(g + 1) * 128, :],
                  reads=[], writes=["d1_x2"])
            p.op("DVE", lambda e: e.tensor_add(out=bo_t[:], in0=x2_t[:],
                                               in1=bo_t[:]),
                 reads=["d1_x2", "d1_bo"], writes=["d1_bo"])
            p.dma("POOL", xout[g * 128:(g + 1) * 128, :], bo_t[:],
                  reads=["d1_bo"], writes=["d1_out"])
    p.pop_region()




def emit_l1ffn(p, xin, xout, W, C):
    eye32, ones_f32, ones_bf16, ones_col = p.cc
    (pA, kA), (pB, kB), (pC, kC), (pD, kD), (pE, kE), (pF, kF) = p.pool[:6]
    NE, H = 8, 2048
    NIC = H // 128

    xn3FM = [p.sb(f"m_xn3fm{kc}", 128, T, BF16) for kc in range(NDC)]
    acc = [p.sb(f"m_acc{g}", 128, D, F32) for g in range(NTC)]
    Wgt = p.sb("m_wgt", 128, NTC * NE, F32)

    routerT = p.sb("m_rt", 128, 64, F32)
    p.push_region()
    rw32 = p.sb("m_rw32", 8, D, F32)
    p.dma("SP", rw32[:], W["L1_ffn_router_w"], writes=["m_rw32"])
    for kc in range(NDC):
        p.op("PE", lambda e, kc=kc: e.transpose(
            pC[:, :8], rw32[:, kc * 128:(kc + 1) * 128], eye32[:8, :8]),
            reads=["m_rw32", "eye32"], writes=[kC])
        p.op("ACT", lambda e, kc=kc: e.copy(out=routerT[:, kc * 8:(kc + 1) * 8],
                                            in_=pC[:, :8]),
             reads=[kC], writes=["m_rt"])
    p.pop_region()

    xld = [p.sb(f"m_x{i}", 128, D, F32) for i in range(2)]
    xn3_2 = [p.sb(f"m_xn3_{i}", 128, D, F32) for i in range(2)]
    _xn3T = p.sb("m_xn3T_0", 128, D, F32)
    xn3T_2 = [_xn3T, _xn3T]
    _mscr = p.sb("m_scr_0", 128, D, F32)
    scr_2 = [_mscr, _mscr]
    cols_2 = [p.sb(f"m_cols_{i}", 128, 8, F32) for i in range(2)]
    lg_2 = [p.sb(f"m_lg_{i}", 128, 8, F32) for i in range(2)]
    t8_2 = [p.sb(f"m_t8_{i}", 128, 8, F32) for i in range(2)]
    _pR = p.ps("m_psr0", 128, 8, F32)
    pR2 = [_pR, _pR]

    for i in range(NTC):
        jj = i % 2
        xk = f"m_x{jj}"
        xn3_t, kxn3 = xn3_2[jj], f"m_xn3_{jj}"
        xn3T, kxn3T = xn3T_2[jj], "m_xn3T_0"
        scr, kscr = scr_2[jj], "m_scr_0"
        cols, kcols = cols_2[jj], f"m_cols_{jj}"
        lg, klg = lg_2[jj], f"m_lg_{jj}"
        t8, kt8 = t8_2[jj], f"m_t8_{jj}"
        pR, kpR = pR2[jj], "m_psr0"
        pT, kpT = ((pC, kC) if jj == 0 else (pD, kD))
        p.dma("SP", xld[jj][:], xin[i * 128:(i + 1) * 128, :], writes=[xk])
        emit_ln(p, xld[jj], xk, xn3_t[:], kxn3, (scr, kscr), (cols, kcols))
        p.op("DVE", lambda e: e.memset(acc[i][:], 0.0), writes=[f"m_acc{i}"])
        for kc in range(NDC):
            p.op("PE", lambda e, kc=kc, xn3_t=xn3_t, pT=pT: e.transpose(
                pT[:, :128], xn3_t[:, kc * 128:(kc + 1) * 128], eye32[:]),
                reads=[kxn3, "eye32"], writes=[kpT])
            p.op("ACT", lambda e, kc=kc, i=i, pT=pT: e.copy(
                out=xn3FM[kc][:, i * 128:(i + 1) * 128], in_=pT[:, :128]),
                reads=[kpT], writes=[f"m_xn3fm{kc}"])
            p.op("ACT", lambda e, kc=kc, pT=pT, xn3T=xn3T: e.copy(
                out=xn3T[:, kc * 128:(kc + 1) * 128], in_=pT[:, :128]),
                reads=[kpT], writes=[kxn3T])
        def mmr(e, xn3T=xn3T, pR=pR):
            last = None
            for kc in range(NDC):
                last = e.matmul(pR[:], xn3T[:, kc * 128:(kc + 1) * 128],
                                routerT[:, kc * 8:(kc + 1) * 8],
                                start=(kc == 0), stop=(kc == NDC - 1))
            return last
        p.op("PE", mmr, reads=[kxn3T, "m_rt"], writes=[kpR])
        p.op("ACT", lambda e, lg=lg, pR=pR: e.copy(out=lg[:], in_=pR[:]),
             reads=[kpR], writes=[klg])
        m1 = cols[:, 0:1]
        m2 = cols[:, 1:2]
        dd = cols[:, 2:3]
        s1 = cols[:, 3:4]
        s2 = cols[:, 4:5]
        p.op("DVE", lambda e, lg=lg, m1=m1: e.reduce_max(out=m1, in_=lg[:], axis=AX.X),
             reads=[klg], writes=[kcols])
        p.op("DVE", lambda e, lg=lg, scr=scr, m1=m1: e.tensor_single_scalar(
            out=scr[:, 0:8], in_=lg[:], scalar=m1, op=ALU.is_equal),
             reads=[klg, kcols], writes=[kscr])
        p.op("DVE", lambda e, scr=scr, lg=lg, t8=t8: e.scalar_tensor_tensor(
            out=t8[:], in0=scr[:, 0:8], scalar=-1e30, in1=lg[:],
            op0=ALU.mult, op1=ALU.add),
             reads=[kscr, klg], writes=[kt8])
        p.op("DVE", lambda e, t8=t8, m2=m2: e.reduce_max(out=m2, in_=t8[:], axis=AX.X),
             reads=[kt8], writes=[kcols])
        p.op("DVE", lambda e, m1=m1, m2=m2, dd=dd: e.tensor_sub(out=dd, in0=m1, in1=m2),
             reads=[kcols], writes=[kcols])
        p.op("ACT", lambda e, dd=dd, s1=s1: e.activation(out=s1, in_=dd, func=AF.Sigmoid,
                                           bias=0.0, scale=1.0),
             reads=[kcols], writes=[kcols])
        p.op("DVE", lambda e, s1=s1, s2=s2: e.tensor_scalar(out=s2, in0=s1, scalar1=-1.0,
                                              scalar2=1.0, op0=ALU.mult,
                                              op1=ALU.add),
             reads=[kcols], writes=[kcols])
        p.op("DVE", lambda e, t8=t8, m2=m2: e.tensor_single_scalar(out=t8[:], in_=t8[:],
                                                     scalar=m2, op=ALU.is_equal),
             reads=[kt8, kcols], writes=[kt8])
        p.op("DVE", lambda e, t8=t8, s2=s2: e.tensor_scalar(out=t8[:], in0=t8[:], scalar1=s2,
                                              scalar2=None, op0=ALU.mult),
             reads=[kt8, kcols], writes=[kt8])
        p.op("DVE", lambda e, i=i, scr=scr, s1=s1, t8=t8: e.scalar_tensor_tensor(
            out=Wgt[:, i * NE:(i + 1) * NE], in0=scr[:, 0:8], scalar=s1,
            in1=t8[:], op0=ALU.mult, op1=ALU.add),
            reads=[kscr, kcols, kt8], writes=["m_wgt"])

    w1b = [p.sb(f"m_w1b{kc}", 128, H, BF16) for kc in range(NDC)]
    w2b = [p.sb(f"m_w2b{ic}", 128, D, BF16) for ic in range(NIC)]
    h2g = p.sb("m_h2g", 128, NIC * 512, BF16)
    hrt = [p.sb(f"m_hrt{i}", 128, 512, BF16) for i in range(2)]
    for e_i in range(NE):
        for kc in range(NDC):
            p.dma("SP", w1b[kc][:], p.PK_w1b[e_i, kc * 128:(kc + 1) * 128, :],
                  reads=["pk_wb"], writes=[f"m_w1b{kc}"])
        for ic in range(NIC):
            p.dma("POOL", w2b[ic][:], p.PK_w2b[e_i, ic * 128:(ic + 1) * 128, :],
                  reads=["pk_wb"], writes=[f"m_w2b{ic}"])

        for ts in range(4):
            for ic in range(NIC):
                par = ic % 2
                pl, kl = (pA, kA) if par == 0 else (pB, kB)
                hr, kh = hrt[par], f"m_hrt{par}"
                def mmh(e, ic=ic, ts=ts, pl=pl):
                    last = None
                    for kc in range(NDC):
                        last = e.matmul(pl[:, :512],
                                        w1b[kc][:, ic * 128:(ic + 1) * 128],
                                        xn3FM[kc][:, ts * 512:(ts + 1) * 512],
                                        start=(kc == 0), stop=(kc == NDC - 1))
                    return last
                p.op("PE", mmh,
                     reads=[f"m_w1b{kc}" for kc in range(NDC)] +
                           [f"m_xn3fm{kc}" for kc in range(NDC)],
                     writes=[kl], lhsT_reads=[f"m_w1b{kc}" for kc in range(NDC)])
                p.op("ACT", lambda e, pl=pl, hr=hr: e.activation(
                    out=hr[:], in_=pl[:, :512], func=AF.Relu, bias=0.0,
                    scale=1.0),
                     reads=[kl], writes=[kh])
                p.op("DVE", lambda e, ic=ic, hr=hr: e.tensor_mul(
                    out=h2g[:, ic * 512:(ic + 1) * 512], in0=hr[:], in1=hr[:]),
                    reads=[kh], writes=["m_h2g"])
            for tc in range(4):
                g = ts * 4 + tc
                for h, (pl, kl) in enumerate(((pC, kC), (pD, kD))):
                    def mmy(e, h=h, pl=pl, tc=tc):
                        last = None
                        for ic in range(NIC):
                            last = e.matmul(pl[:, :512],
                                            h2g[:, ic * 512 + tc * 128:ic * 512 + (tc + 1) * 128],
                                            w2b[ic][:, h * 512:(h + 1) * 512],
                                            start=(ic == 0), stop=(ic == NIC - 1))
                        return last
                    p.op("PE", mmy,
                         reads=["m_h2g"] + [f"m_w2b{ic}" for ic in range(NIC)],
                         writes=[kl], lhsT_reads=["m_h2g"])
                    p.op("DVE", lambda e, h=h, pl=pl, g=g, e_i=e_i: e.scalar_tensor_tensor(
                        out=acc[g][:, h * 512:(h + 1) * 512], in0=pl[:, :512],
                        scalar=Wgt[:, g * NE + e_i:g * NE + e_i + 1],
                        in1=acc[g][:, h * 512:(h + 1) * 512],
                        op0=ALU.mult, op1=ALU.add),
                        reads=[kl, "m_wgt", f"m_acc{g}"], writes=[f"m_acc{g}"])
    if getattr(p, "dbg_m", False):
        d_acc = p.dram_out("d_acc", (128, D), F32)
        p.dma("POOL", d_acc, acc[15][:], reads=["m_acc15"], writes=["d_acc"])
        d_acc0 = p.dram_out("d_acc0", (128, D), F32)
        p.dma("POOL", d_acc0, acc[0][:], reads=["m_acc0"], writes=["d_acc0"])
    for i in range(NTC):
        jj = i % 2
        xk = f"m_x{jj}"
        scr, kscr = scr_2[jj], "m_scr_0"
        cols, kcols = cols_2[jj], f"m_cols_{jj}"
        xsc, kxsc = xn3_2[jj], f"m_xn3_{jj}"
        p.dma("SP", xld[jj][:], xin[i * 128:(i + 1) * 128, :], writes=[xk])
        emit_ln(p, acc[i], f"m_acc{i}", scr[:], kscr, (xsc, kxsc),
                (cols, kcols))
        p.op("DVE", lambda e, i=i, jj=jj, scr=scr: e.tensor_add(
            out=scr[:], in0=xld[jj][:], in1=scr[:]),
             reads=[xk, kscr], writes=[kscr])
        p.dma("POOL", xout[i * 128:(i + 1) * 128, :], scr[:],
              reads=[kscr], writes=[f"m_out{i}"])



INPUT_SPECS = {
    "L0_attn_omega_proj_w": ((2016, 1024), F32),
    "L0_attn_omega_proj_b": ((2016,), F32),
    "L0_attn_omega_out_w": ((1024, 64), F32),
    "L0_attn_omega_out_b": ((1024,), F32),
    "L0_attn_vq_proj_in_w": ((64, 1024), F32),
    "L0_attn_vq_proj_in_b": ((64,), F32),
    "L0_attn_vq_proj_out_w": ((1024, 64), F32),
    "L0_attn_vq_proj_out_b": ((1024,), F32),
    "L0_attn_codebook": ((256, 64), F32),
    "L0_attn_omega_mix": ((1, 1), F32),
    "L0_attn_vq_mix": ((1, 1), F32),
    "L0_attn_norm_w": ((1024,), F32),
    "L0_ffn_weight": ((1024, 1024), U8),
    "L0_ffn_weight_scale": ((1, 1), F32),
    "L0_ffn_bias": ((1024,), F32),
    "L1_attn_in_proj_w": ((4096, 1024), F32),
    "L1_attn_in_proj_b": ((4096,), F32),
    "L1_attn_conv_w": ((2048, 4), F32),
    "L1_attn_conv_b": ((2048,), F32),
    "L1_attn_x_proj_w": ((32, 2048), F32),
    "L1_attn_x_proj_b": ((32,), F32),
    "L1_attn_dt_proj_w": ((2048, 2048), F32),
    "L1_attn_dt_proj_b": ((2048,), F32),
    "L1_attn_A_log": ((2048, 16), F32),
    "L1_attn_D_param": ((2048,), F32),
    "L1_attn_out_proj_w": ((1024, 2048), F32),
    "L1_attn_out_proj_b": ((1024,), F32),
    "L1_attn_norm_alpha": ((1024,), F32),
    "L1_attn_norm_bias": ((1024,), F32),
    "L1_attn_in_norm_alpha": ((1024,), F32),
    "L1_attn_in_norm_bias": ((1024,), F32),
    "L1_ffn_router_w": ((8, 1024), F32),
    "L1_ffn_w1": ((8, 1024, 2048), F32),
    "L1_ffn_w2": ((8, 2048, 1024), F32),
}

PHASE_WEIGHTS = {
    "l0attn": [k for k in INPUT_SPECS if k.startswith("L0_attn")],
    "l0ffn": [k for k in INPUT_SPECS if k.startswith("L0_ffn")] +
             ["L1_ffn_w1", "L1_ffn_w2"],
    "l1attn": [k for k in INPUT_SPECS if k.startswith("L1_attn")],
    "l1ffn": [k for k in INPUT_SPECS if k.startswith("L1_ffn")],
}

EMITTERS = {}


def build_program(phases):
    p = Prog()
    p.pool = [(p.ps(f"pool{i}", 128, 512, F32), f"pool{i}") for i in range(7)]
    wnames = sorted({w for ph in phases for w in PHASE_WEIGHTS[ph]})
    W = {n: p.dram_in(n, INPUT_SPECS[n][0], INPUT_SPECS[n][1]) for n in wnames}
    C = {
        "eye32": p.dram_in("c_eye32", (128, 128), F32),
        "triu64": p.dram_in("c_triu64", (64, 64), F32),
    }
    p.cc = common_consts(p, C)
    p.constf(0.0)
    xin = p.dram_in("x_in", (T, D), F32)
    cur = xin
    for i, ph in enumerate(phases):
        last = i == len(phases) - 1
        if last:
            nxt = p.dram_out("x_out", (T, D), F32)
        else:
            nxt = p.dram_tmp(f"x_mid{i}", (T, D), F32)
        p.push_region()
        EMITTERS[ph](p, cur, nxt, W, C)
        p.pop_region()
        cur = nxt
    nc = p.finalize()
    return nc, wnames


_CONSTS = None


def _consts():
    global _CONSTS
    if _CONSTS is None:
        eye = np.eye(128, dtype=np.float32)
        triu = np.triu(np.ones((64, 64), dtype=np.float32), k=1)
        _CONSTS = {"eye32": eye, "triu64": triu}
    return _CONSTS


LAUNCH_GROUPS = [["l0attn", "l0ffn", "l1attn", "l1ffn"]]


def _prep_inputs(inputs):
    arrs = {}
    for name, (shape, dt) in INPUT_SPECS.items():
        a = np.asarray(inputs[name])
        if dt == F32:
            a = np.ascontiguousarray(a, dtype=np.float32).reshape(shape)
        elif dt == U8:
            a = np.ascontiguousarray(a).view(np.uint8).reshape(shape)
        else:
            a = np.ascontiguousarray(a).reshape(shape)
        arrs[name] = a
    return arrs


def run_groups(inputs, groups):
    arrs = _prep_inputs(inputs)
    x = np.ascontiguousarray(np.asarray(inputs["L0_attn_x"], dtype=np.float32))
    cur = [np.ascontiguousarray(x[b]) for b in range(8)]
    consts = _consts()
    for phases in groups:
        nc, wnames = build_program(phases)
        in_maps = []
        for b in range(8):
            m = {"x_in": cur[b], "c_eye32": consts["eye32"],
                 "c_triu64": consts["triu64"]}
            for n in wnames:
                m[n] = arrs[n]
            in_maps.append(m)
        res = run_bass_kernel_spmd(nc, in_maps, core_ids=list(range(8)))
        cur = [np.asarray(res.results[b]["x_out"]) for b in range(8)]
    return np.stack(cur).astype(np.float32)


def kernel(**inputs):
    return run_groups(inputs, LAUNCH_GROUPS)


EMITTERS["l0attn"] = emit_l0attn
EMITTERS["l0ffn"] = emit_l0ffn
EMITTERS["l1attn"] = emit_l1attn
EMITTERS["l1ffn"] = emit_l1ffn


# version 20
# speedup vs baseline: 1.0294x; 1.0294x over previous
import contextlib
import math

import numpy as np
import ml_dtypes

import concourse.bass as bass
import concourse.mybir as mybir
from concourse.bass_utils import run_bass_kernel_spmd

F32 = mybir.dt.float32
BF16 = mybir.dt.bfloat16
FP8 = mybir.dt.float8e4
U8 = mybir.dt.uint8
I32 = mybir.dt.int32
AF = mybir.ActivationFunctionType
ALU = mybir.AluOpType
AX = mybir.AxisListType

T = 2048
D = 1024
NTC = T // 128
NDC = D // 128

ENGINES = ("PE", "ACT", "DVE", "POOL", "SP")
QUEUES = ("qSP", "qPOOL", "qACT", "qDVE")
Q_OF = {"SP": "qSP", "POOL": "qPOOL", "ACT": "qACT", "DVE": "qDVE"}
ENG_ATTR = {"PE": "tensor", "ACT": "scalar", "DVE": "vector", "POOL": "gpsimd",
            "SP": "sync"}
SAME_ENGINE_SYNC = True


class Prog:

    def __init__(self, name="prog"):
        self.nc = bass.Bass("TRN2", target_bir_lowering=False)
        self.stack = contextlib.ExitStack()
        self.regions = []
        self.stream = {e: [] for e in ENGINES}
        self.count = {e: 0 for e in ENGINES + QUEUES}
        self.clock = {e: {} for e in ENGINES}
        self.snap = {e: {} for e in ENGINES + QUEUES}
        self.buf = {}
        self.uid = 0

    def _cur_stack(self):
        return self.regions[-1] if self.regions else self.stack

    def push_region(self):
        self.regions.append(contextlib.ExitStack())

    def pop_region(self):
        self.barrier()
        self.regions.pop().close()

    def sb(self, name, p, f, dt):
        return self._cur_stack().enter_context(
            self.nc.sbuf_tensor(name, [p, f], dt))

    def ps(self, name, p, f, dt=F32):
        return self._cur_stack().enter_context(
            self.nc.psum_tensor(name, [p, f], dt))

    def dram_in(self, name, shape, dt):
        return self.nc.dram_tensor(name, list(shape), dt, kind="ExternalInput").ap()

    def dram_out(self, name, shape, dt):
        return self.nc.dram_tensor(name, list(shape), dt, kind="ExternalOutput").ap()

    def dram_tmp(self, name, shape, dt):
        return self.nc.dram_tensor(name, list(shape), dt, kind="Internal").ap()

    def sem(self, name):
        return self.stack.enter_context(self.nc.semaphore(name))

    def op(self, eng, emit, reads=(), writes=(), queue=False, lhsT_reads=None):
        import sys as _sys
        _f = _sys._getframe(1)
        _label = f"{_f.f_lineno}"
        if _f.f_code.co_name in ("dma", "dma_tr"):
            _f2 = _sys._getframe(2)
            _label = f"{_f2.f_lineno}"
        self._last_label = _label
        deps = {}

        def need(ev):
            e, c = ev
            if c > deps.get(e, -1):
                deps[e] = c

        need_fence = False
        if eng == "PE":
            if not hasattr(self, "fclk"):
                self.fclk = {}
            lb = reads if lhsT_reads is None else lhsT_reads
            for b in lb:
                rec = self.buf.get(b)
                if rec and rec["w"]:
                    de, dc = rec["w"]
                    if de != eng and self.fclk.get(de, 0) < dc:
                        need_fence = True
                        break
        for b in reads:
            rec = self.buf.get(b)
            if rec and rec["w"]:
                need(rec["w"])
        for b in writes:
            rec = self.buf.get(b)
            if rec:
                if rec["w"]:
                    need(rec["w"])
                for r in rec["r"]:
                    need(r)
        clk = self.clock[eng]
        for de in sorted(deps):
            dc = deps[de]
            if de == eng and not SAME_ENGINE_SYNC:
                continue
            if clk.get(de, 0) >= dc:
                continue
            self.stream[eng].append(("wait", de, dc))
            snap = self.snap[de].get(dc, {})
            for k, v in snap.items():
                if v > clk.get(k, 0):
                    clk[k] = v
            if dc > clk.get(de, 0):
                clk[de] = dc

        if queue:
            ev_eng = Q_OF[eng]
            self.count[ev_eng] += 16
        else:
            ev_eng = eng
            self.count[ev_eng] += 1
        cnt = self.count[ev_eng]
        if need_fence:
            self.fclk = dict(self.clock["PE"])
            emit0 = emit

            def emit(e, emit0=emit0):
                for _ in range(70):
                    e.nop()
                return emit0(e)
        self.stream[eng].append(("emit", emit, ev_eng, cnt, self._last_label))
        evclk = dict(clk)
        evclk[eng] = self.count[eng]
        evclk[ev_eng] = cnt
        self.snap[ev_eng][cnt] = evclk
        ev = (ev_eng, cnt)
        for b in reads:
            self.buf.setdefault(b, {"w": None, "r": []})["r"].append(ev)
        for b in writes:
            self.buf[b] = {"w": ev, "r": []}
        return ev

    SERIAL_DMA = False

    def dma(self, eng, dst, src, reads=(), writes=(), **kw):
        tok = ([f"__q_{eng}"] if self.SERIAL_DMA else [])
        return self.op(eng, lambda e: e.dma_start(out=dst, in_=src, **kw),
                       reads=reads, writes=list(writes) + tok, queue=True)

    def dma_tr(self, eng, dst, src, reads=(), writes=()):
        tok = ([f"__q_{eng}"] if self.SERIAL_DMA else [])
        return self.op(eng, lambda e: e.dma_start_transpose(dst, src),
                       reads=reads, writes=list(writes) + tok, queue=True)

    def barrier(self):
        counts = dict(self.count)
        for eng in ENGINES:
            clk = self.clock[eng]
            for de in ENGINES + QUEUES:
                if de == eng:
                    continue
                dc = counts[de]
                if dc > 0 and clk.get(de, 0) < dc:
                    self.stream[eng].append(("wait", de, dc))
                    clk[de] = dc
        for b in self.buf:
            self.buf[b] = {"w": None, "r": []}

    def finalize(self):
        nc = self.nc
        sems = {e: self.stack.enter_context(nc.semaphore("sem_" + e))
                for e in ENGINES + QUEUES}
        block = self.stack.enter_context(nc.Block())

        def runner(eng_name):
            def run(eng_obj):
                for cmd in self.stream[eng_name]:
                    if cmd[0] == "wait":
                        eng_obj.wait_ge(sems[cmd[1]], cmd[2])
                    else:
                        _, emit, ev_eng, cnt = cmd[:4]
                        inst = emit(eng_obj)
                        inc = 16 if ev_eng in QUEUES else 1
                        inst.then_inc(sems[ev_eng], inc)
            return run

        block.tensor(runner("PE"))
        block.scalar(runner("ACT"))
        block.vector(runner("DVE"))
        block.gpsimd(runner("POOL"))
        block.sync(runner("SP"))
        self.stack.close()
        return nc

    def fresh(self, tag):
        self.uid += 1
        return f"{tag}_{self.uid}"

    def constf(self, val):
        if not hasattr(self, "_consts"):
            self._consts = {}
            self._ctile = self.stack.enter_context(
                self.nc.sbuf_tensor("cconsts", [128, 16], F32))
            self._cn = 0
        val = float(val)
        if val not in self._consts:
            i = self._cn
            self._cn += 1
            self.op("DVE", lambda e, i=i, val=val: e.memset(
                self._ctile[:, i:i + 1], val), writes=["cconsts"])
            self._consts[val] = i
        i = self._consts[val]
        return self._ctile[:, i:i + 1], "cconsts"



def load_row(p, name, src_ap, n, dt=F32):
    t = p.sb(name, 1, n, dt)
    p.dma("SP", t[:], src_ap.rearrange("(o n) -> o n", o=1), writes=[name])
    return t


def bcast_row(p, name, row_t, row_buf, n, ones_f32, pb, pbk, dt=F32):
    out = p.sb(name, 128, n, dt)
    for c0 in range(0, n, 512):
        w = min(512, n - c0)
        p.op("PE", lambda e, c0=c0, w=w: e.matmul(
            pb[:, :w], ones_f32[:1, :], row_t[:1, c0:c0 + w], start=True, stop=True),
            reads=[row_buf, "ones_f32"], writes=[pbk], lhsT_reads=["ones_f32"])
        p.op("ACT", lambda e, c0=c0, w=w: e.copy(out=out[:, c0:c0 + w], in_=pb[:, :w]),
             reads=[pbk], writes=[name])
    return out


def bias_cols(p, name, row_t, row_buf, n, eye32, pb, pbk):
    ncol = (n + 127) // 128
    out = p.sb(name, 128, ncol, F32)
    for c in range(ncol):
        w = min(128, n - c * 128)
        p.op("PE", lambda e, c=c, w=w: e.transpose(
            pb[:w, :1], row_t[:1, c * 128:c * 128 + w], eye32[:1, :1]),
            reads=[row_buf, "eye32"], writes=[pbk])
        p.op("ACT", lambda e, c=c, w=w: e.copy(out=out[:w, c:c + 1], in_=pb[:w, :1]),
             reads=[pbk], writes=[name])
    return out


def emit_ln(p, x_t, x_buf, out_ap, out_buf, scratch, cols, kind="outer",
            alpha_bc=None, beta_bc=None, ab_bufs=()):
    scr_t, scr_buf = scratch
    cols_t, cols_buf = cols
    s_sum = cols_t[:, 0:1]
    s_mneg = cols_t[:, 1:2]
    s_ss = cols_t[:, 2:3]
    s_std = cols_t[:, 3:4]
    s_r = cols_t[:, 4:5]
    s_bneg = cols_t[:, 5:6]
    p.op("DVE", lambda e: e.reduce_sum(out=s_sum, in_=x_t[:], axis=AX.X),
         reads=[x_buf], writes=[cols_buf])
    p.op("DVE", lambda e: e.tensor_scalar_mul(out=s_mneg, in0=s_sum, scalar1=-1.0 / D),
         reads=[cols_buf], writes=[cols_buf])
    p.op("ACT", lambda e: e.activation(out=scr_t[:], in_=x_t[:], func=AF.Square,
                                       bias=s_mneg, scale=1.0, accum_out=s_ss),
         reads=[x_buf, cols_buf], writes=[scr_buf, cols_buf])
    if kind == "outer":
        b_eps, bk = p.constf(1e-5)
        s_inv, sk = p.constf(1.0 / D)
        p.op("ACT", lambda e: e.activation(out=s_std, in_=s_ss, func=AF.Sqrt,
                                           bias=b_eps, scale=s_inv),
             reads=[cols_buf, bk, sk], writes=[cols_buf])
        p.op("DVE", lambda e: e.reciprocal(out=s_r, in_=s_std),
             reads=[cols_buf], writes=[cols_buf])
    else:
        s_inv1, sk1 = p.constf(1.0 / (D - 1))
        p.op("ACT", lambda e: e.activation(out=s_std, in_=s_ss, func=AF.Sqrt,
                                           bias=0.0, scale=s_inv1),
             reads=[cols_buf, sk1], writes=[cols_buf])
        p.op("DVE", lambda e: e.tensor_scalar_add(out=s_std, in0=s_std, scalar1=1e-6),
             reads=[cols_buf], writes=[cols_buf])
        p.op("DVE", lambda e: e.reciprocal(out=s_r, in_=s_std),
             reads=[cols_buf], writes=[cols_buf])
    p.op("DVE", lambda e: e.tensor_mul(out=s_bneg, in0=s_mneg, in1=s_r),
         reads=[cols_buf], writes=[cols_buf])
    if alpha_bc is None:
        p.op("ACT", lambda e: e.activation(out=out_ap, in_=x_t[:], func=AF.Identity,
                                           bias=s_bneg, scale=s_r),
             reads=[x_buf, cols_buf], writes=[out_buf])
    else:
        p.op("ACT", lambda e: e.activation(out=scr_t[:], in_=x_t[:], func=AF.Identity,
                                           bias=s_bneg, scale=s_r),
             reads=[x_buf, cols_buf], writes=[scr_buf])
        p.op("DVE", lambda e: e.tensor_mul(out=scr_t[:], in0=scr_t[:], in1=alpha_bc[:]),
             reads=[scr_buf, ab_bufs[0]], writes=[scr_buf])
        p.op("DVE", lambda e: e.tensor_add(out=out_ap, in0=scr_t[:], in1=beta_bc[:]),
             reads=[scr_buf, ab_bufs[1]], writes=[out_buf])


def load_wT_bf16(p, tag, w_ap, rows, cols, in_dt=F32):
    nrc = rows // 128
    nkc = cols // 128
    tiles = [p.sb(f"{tag}_k{kc}", 128, rows, BF16) for kc in range(nkc)]
    bufs = [f"{tag}_k{kc}" for kc in range(nkc)]
    p.push_region()
    stg32 = [p.sb(f"{tag}_s32_{i}", 128, cols, in_dt) for i in range(2)]
    stg16 = [p.sb(f"{tag}_s16_{i}", 128, cols, BF16) for i in range(2)]
    for rc in range(nrc):
        i = rc % 2
        b32, b16 = f"{tag}_s32_{i}", f"{tag}_s16_{i}"
        p.dma("SP", stg32[i][:], w_ap[rc * 128:(rc + 1) * 128, :], writes=[b32])
        p.op("ACT", lambda e, i=i: e.copy(out=stg16[i][:], in_=stg32[i][:]),
             reads=[b32], writes=[b16])
        for kc in range(nkc):
            p.dma_tr("SP", tiles[kc][:, rc * 128:(rc + 1) * 128],
                     stg16[i][:, kc * 128:(kc + 1) * 128],
                     reads=[b16], writes=[bufs[kc]])
    p.pop_region()
    return tiles, bufs




def emit_decode_e4m3(p, u8_t, u8_buf, out16_ap, out_buf, tmp, n):
    ti, tb, tf, tf2, k = tmp
    p.op("DVE", lambda e: e.tensor_copy(out=ti[:, :n], in_=u8_t), reads=[u8_buf], writes=[k + "i"])
    p.op("DVE", lambda e: e.tensor_single_scalar(out=tb[:, :n], in_=ti[:, :n], scalar=3, op=ALU.arith_shift_right), reads=[k + "i"], writes=[k + "b"])
    p.op("DVE", lambda e: e.tensor_single_scalar(out=tb[:, :n], in_=tb[:, :n], scalar=15, op=ALU.bitwise_and), reads=[k + "b"], writes=[k + "b"])
    p.op("DVE", lambda e: e.tensor_single_scalar(out=tf2[:, :n].bitcast(I32), in_=ti[:, :n], scalar=7, op=ALU.bitwise_and), reads=[k + "i"], writes=[k + "g"])
    p.op("DVE", lambda e: e.tensor_single_scalar(out=tf[:, :n].bitcast(I32), in_=tb[:, :n], scalar=0, op=ALU.is_gt), reads=[k + "b"], writes=[k + "f"])
    p.op("DVE", lambda e: e.tensor_single_scalar(out=tf[:, :n].bitcast(I32), in_=tf[:, :n].bitcast(I32), scalar=3, op=ALU.logical_shift_left), reads=[k + "f"], writes=[k + "f"])
    p.op("DVE", lambda e: e.tensor_tensor(out=tf[:, :n].bitcast(I32), in0=tf[:, :n].bitcast(I32), in1=tf2[:, :n].bitcast(I32), op=ALU.add), reads=[k + "f", k + "g"], writes=[k + "f"])
    p.op("DVE", lambda e: e.tensor_copy(out=tf[:, :n], in_=tf[:, :n].bitcast(I32)), reads=[k + "f"], writes=[k + "f"])
    p.op("DVE", lambda e: e.tensor_scalar(out=tb[:, :n], in0=tb[:, :n], scalar1=1, scalar2=117, op0=ALU.max, op1=ALU.add), reads=[k + "b"], writes=[k + "b"])
    p.op("DVE", lambda e: e.tensor_single_scalar(out=tb[:, :n], in_=tb[:, :n], scalar=23, op=ALU.logical_shift_left), reads=[k + "b"], writes=[k + "b"])
    p.op("DVE", lambda e: e.tensor_mul(out=tf[:, :n], in0=tf[:, :n], in1=tb[:, :n].bitcast(F32)), reads=[k + "f", k + "b"], writes=[k + "f"])
    p.op("DVE", lambda e: e.tensor_single_scalar(out=tb[:, :n], in_=ti[:, :n], scalar=7, op=ALU.arith_shift_right), reads=[k + "i"], writes=[k + "b"])
    p.op("DVE", lambda e: e.tensor_copy(out=tf2[:, :n], in_=tb[:, :n]), reads=[k + "b"], writes=[k + "g"])
    p.op("DVE", lambda e: e.tensor_scalar(out=tf2[:, :n], in0=tf2[:, :n], scalar1=-2.0, scalar2=1.0, op0=ALU.mult, op1=ALU.add), reads=[k + "g"], writes=[k + "g"])
    p.op("DVE", lambda e: e.tensor_mul(out=out16_ap, in0=tf[:, :n], in1=tf2[:, :n]), reads=[k + "f", k + "g"], writes=[out_buf])


def load_wT_e4m3(p, tag, w_ap, rows, cols):
    nrc = rows // 128
    nkc = cols // 128
    tiles = [p.sb(f"{tag}_k{kc}", 128, rows, BF16) for kc in range(nkc)]
    bufs = [f"{tag}_k{kc}" for kc in range(nkc)]
    stg8 = p.sb(f"{tag}_s8", 128, cols, U8)
    stg16 = p.sb(f"{tag}_s16", 128, cols, BF16)
    ti = p.sb(f"{tag}_ti", 128, cols, I32)
    tb = p.sb(f"{tag}_tb", 128, cols, I32)
    tf = p.sb(f"{tag}_tf", 128, cols, F32)
    tf2 = p.sb(f"{tag}_tf2", 128, cols, F32)
    tmp = (ti, tb, tf, tf2, tag + "_t")
    for rc in range(nrc):
        p.dma("SP", stg8[:], w_ap[rc * 128:(rc + 1) * 128, :], writes=[tag + "_s8"])
        emit_decode_e4m3(p, stg8[:], tag + "_s8", stg16[:], tag + "_s16", tmp, cols)
        for kc in range(nkc):
            p.dma_tr("SP", tiles[kc][:, rc * 128:(rc + 1) * 128],
                     stg16[:, kc * 128:(kc + 1) * 128],
                     reads=[tag + "_s16"], writes=[bufs[kc]])
    return tiles, bufs


def emit_quant_e4m3(p, v_ap, v_buf, out16_ap, out_buf, tmp, n=D):
    qa, qe, qs, k = tmp
    p.op("ACT", lambda e: e.activation(out=qa[:, :n], in_=v_ap, func=AF.Abs,
                                       bias=0.0, scale=1.0),
         reads=[v_buf], writes=[k + "a"])
    p.op("DVE", lambda e: e.tensor_single_scalar(out=qe[:, :n], in_=qa[:, :n].bitcast(I32), scalar=23, op=ALU.arith_shift_right), reads=[k + "a"], writes=[k + "e"])
    p.op("DVE", lambda e: e.tensor_scalar(out=qe[:, :n], in0=qe[:, :n], scalar1=121, scalar2=-130, op0=ALU.max, op1=ALU.add), reads=[k + "e"], writes=[k + "e"])
    p.op("DVE", lambda e: e.tensor_scalar(out=qs[:, :n], in0=qe[:, :n], scalar1=-1, scalar2=127, op0=ALU.mult, op1=ALU.add), reads=[k + "e"], writes=[k + "s"])
    p.op("DVE", lambda e: e.tensor_single_scalar(out=qs[:, :n], in_=qs[:, :n], scalar=23, op=ALU.logical_shift_left), reads=[k + "s"], writes=[k + "s"])
    p.op("DVE", lambda e: e.tensor_mul(out=qa[:, :n], in0=qa[:, :n], in1=qs[:, :n].bitcast(F32)), reads=[k + "a", k + "s"], writes=[k + "a"])
    p.op("DVE", lambda e: e.tensor_scalar(out=qa[:, :n], in0=qa[:, :n], scalar1=8388608.0, scalar2=-8388608.0, op0=ALU.add, op1=ALU.add), reads=[k + "a"], writes=[k + "a"])
    p.op("DVE", lambda e: e.tensor_single_scalar(out=qs[:, :n], in_=qe[:, :n], scalar=127, op=ALU.add), reads=[k + "e"], writes=[k + "s"])
    p.op("DVE", lambda e: e.tensor_single_scalar(out=qs[:, :n], in_=qs[:, :n], scalar=23, op=ALU.logical_shift_left), reads=[k + "s"], writes=[k + "s"])
    p.op("DVE", lambda e: e.tensor_mul(out=qa[:, :n], in0=qa[:, :n], in1=qs[:, :n].bitcast(F32)), reads=[k + "a", k + "s"], writes=[k + "a"])
    p.op("DVE", lambda e: e.tensor_scalar_min(out=qa[:, :n], in0=qa[:, :n], scalar1=448.0), reads=[k + "a"], writes=[k + "a"])
    p.op("DVE", lambda e: e.tensor_single_scalar(out=qs[:, :n].bitcast(F32), in_=v_ap, scalar=0.0, op=ALU.is_ge), reads=[v_buf], writes=[k + "s"])
    p.op("DVE", lambda e: e.tensor_scalar(out=qs[:, :n].bitcast(F32), in0=qs[:, :n].bitcast(F32), scalar1=2.0, scalar2=-1.0, op0=ALU.mult, op1=ALU.add), reads=[k + "s"], writes=[k + "s"])
    p.op("DVE", lambda e: e.tensor_mul(out=out16_ap, in0=qa[:, :n], in1=qs[:, :n].bitcast(F32)), reads=[k + "a", k + "s"], writes=[out_buf])


def common_consts(p, C):
    eye32 = p.sb("eye32", 128, 128, F32)
    p.dma("SP", eye32[:], C["eye32"], writes=["eye32"])
    ones_f32 = p.sb("ones_f32", 1, 128, F32)
    p.op("DVE", lambda e: e.memset(ones_f32[:], 1.0), writes=["ones_f32"])
    ones_bf16 = p.sb("ones_bf16", 1, 128, BF16)
    p.op("DVE", lambda e: e.memset(ones_bf16[:], 1.0), writes=["ones_bf16"])
    ones_col = p.sb("ones_col", 128, 1, F32)
    p.op("DVE", lambda e: e.memset(ones_col[:], 1.0), writes=["ones_col"])
    return eye32, ones_f32, ones_bf16, ones_col


def emit_l0attn(p, xin, xout, W, C):
    eye32, ones_f32, ones_bf16, ones_col = p.cc
    (pA, kA), (pB, kB), (pC, kC), (pD, kD), (pE, kE), (pF, kF) = p.pool[:6]
    triu = p.sb("triu", 64, 64, F32)
    p.dma("SP", triu[:], C["triu64"], writes=["triu"])

    xld = [p.sb(f"xld_{j}", 128, D, F32) for j in range(2)]
    xn_t = [p.sb(f"xn_{i}", 128, D, F32) for i in range(NTC)]
    scr2l = [p.sb(f"scr{j}", 128, D, F32) for j in range(2)]
    cols2l = [p.sb(f"cols{j}", 128, 8, F32) for j in range(2)]
    colsb2 = [p.sb(f"cols2{j}", 128, 8, F32) for j in range(2)]
    scr = scr2l[0]
    cols = cols2l[0]

    for i in range(NTC):
        j = i % 2
        p.dma("SP", xld[j][:], xin[i * 128:(i + 1) * 128, :], writes=[f"xld_{j}"])
        emit_ln(p, xld[j], f"xld_{j}", xn_t[i][:], f"xn_{i}",
                (scr2l[j], f"scr{j}"), (cols2l[j], f"cols{j}"))

    for dch in range(NDC):
        def mmh(e, dch=dch):
            last = None
            for i in range(NTC):
                last = e.matmul(pB[:, dch:dch + 1],
                                xn_t[i][:, dch * 128:(dch + 1) * 128],
                                ones_col[:], start=(i == 0), stop=(i == NTC - 1))
            return last
        p.op("PE", mmh, reads=[f"xn_{i}" for i in range(NTC)] + ["ones_col"],
             writes=[kB])
    hm_col = p.sb("hm_col", 128, 8, F32)
    c_invT, ck_invT = p.constf(1.0 / T)
    p.op("ACT", lambda e: e.activation(out=hm_col[:], in_=pB[:, 0:8], func=AF.Identity,
                                       bias=0.0, scale=c_invT),
         reads=[kB, ck_invT], writes=["hm_col"])
    hm_colb = p.sb("hm_colb", 128, 8, BF16)
    p.op("ACT", lambda e: e.copy(out=hm_colb[:], in_=hm_col[:]),
         reads=["hm_col"], writes=["hm_colb"])

    tr8 = p.sb("tr8", 8, 128, F32)
    hm_row = p.sb("hm_row", 1, D, F32)
    p.op("PE", lambda e: e.transpose(pC[:8, :128], hm_col[:], eye32[:]),
         reads=["hm_col", "eye32"], writes=[kC])
    p.op("ACT", lambda e: e.copy(out=tr8[:], in_=pC[:8, :128]), reads=[kC],
         writes=["tr8"])
    p.dma("POOL", hm_row[:], tr8[:], reads=["tr8"], writes=["hm_row"])
    hm_bc = bcast_row(p, "hm_bc", hm_row, "hm_row", D, ones_f32, pA, kA)

    NPAR = 2016
    wrow = [p.sb(f"wrow_{i}", 128, D, F32) for i in range(2)]
    prod = p.sb("prod", 128, D, F32)
    params_col = p.sb("params_col", 128, 16, F32)
    for rc in range(16):
        rows = 128 if rc < 15 else NPAR - 15 * 128
        i = rc % 2
        p.dma("SP", wrow[i][:rows, :],
              W["L0_attn_omega_proj_w"][rc * 128:rc * 128 + rows, :],
              writes=[f"wrow_{i}"])
        p.op("DVE", lambda e, i=i, rows=rows: e.tensor_mul(
            out=prod[:rows, :], in0=wrow[i][:rows, :], in1=hm_bc[:rows, :]),
            reads=[f"wrow_{i}", "hm_bc"], writes=["prod"])
        p.op("DVE", lambda e, rc=rc, rows=rows: e.reduce_sum(
            out=params_col[:rows, rc:rc + 1], in_=prod[:rows, :], axis=AX.X),
            reads=["prod"], writes=["params_col"])
    tr16 = p.sb("tr16", 16, 128, F32)
    params_row = p.sb("params_row", 1, 2048, F32)
    p.op("PE", lambda e: e.transpose(pC[:16, :128], params_col[:], eye32[:]),
         reads=["params_col", "eye32"], writes=[kC])
    p.op("ACT", lambda e: e.copy(out=tr16[:], in_=pC[:16, :128]), reads=[kC],
         writes=["tr16"])
    p.dma("POOL", params_row[:], tr16[:], reads=["tr16"], writes=["params_row"])
    brow = load_row(p, "ob_row", W["L0_attn_omega_proj_b"], NPAR)
    p.op("DVE", lambda e: e.tensor_add(out=params_row[:1, :NPAR],
                                       in0=params_row[:1, :NPAR], in1=brow[:]),
         reads=["ob_row", "params_row"], writes=["params_row"])

    stage = p.sb("om_stage", 1, 2304, F32)
    p.op("DVE", lambda e: e.memset(stage[:], 0.0), writes=["om_stage"])
    p.op("DVE", lambda e: e.tensor_copy(out=stage[:1, 64:64 + NPAR],
                                        in_=params_row[:1, :NPAR]),
         reads=["params_row"], writes=["om_stage"])
    om_raw = p.sb("om_raw", 64, 64, F32)
    p.op("DVE", lambda e: e.memset(om_raw[:], 0.0), writes=["om_raw"])
    row_keys = []
    for r in range(63):
        start_r = 63 * r - r * (r - 1) // 2
        off = 64 + start_r - (r + 1)
        rk = f"om_raw_r{r}"
        row_keys.append(rk)
        p.dma("SP", om_raw[r:r + 1, :], stage[:1, off:off + 64],
              reads=["om_stage", "om_raw"], writes=[rk])
    om_u = p.sb("om_u", 64, 64, F32)
    p.op("DVE", lambda e: e.tensor_mul(out=om_u[:], in0=om_raw[:], in1=triu[:]),
         reads=["om_raw", "triu"] + row_keys, writes=["om_u"])
    p.op("PE", lambda e: e.transpose(pC[:64, :64], om_u[:], eye32[:64, :64]),
         reads=["om_u", "eye32"], writes=[kC])
    omega_b = p.sb("omega_b", 64, 64, BF16)
    omg = p.sb("omg", 64, 64, F32)
    p.op("DVE", lambda e: e.tensor_sub(out=omg[:], in0=om_u[:], in1=pC[:64, :64]),
         reads=["om_u", kC], writes=["omg"])
    p.op("ACT", lambda e: e.copy(out=omega_b[:], in_=omg[:]),
         reads=["omg"], writes=["omega_b"])

    mix_o64 = p.sb("mix_o64", 64, 1, F32)
    p.dma("SP", mix_o64[:], W["L0_attn_omega_mix"].broadcast_to([64, 1]),
          writes=["mix_o64"])
    mix_v = p.sb("mix_v", 1, 1, F32)
    p.dma("SP", mix_v[:], W["L0_attn_vq_mix"], writes=["mix_v"])
    mix_o1 = p.sb("mix_o1", 1, 1, F32)
    p.dma("SP", mix_o1[:], W["L0_attn_omega_mix"], writes=["mix_o1"])

    xhead_b = p.sb("xhead_b", 64, T, BF16)
    xheadm_b = p.sb("xheadm_b", 64, T, BF16)
    for i in range(NTC):
        p.op("PE", lambda e, i=i: e.transpose(pC[:64, :128], xn_t[i][:, 0:64], eye32[:]),
             reads=[f"xn_{i}", "eye32"], writes=[kC])
        p.op("ACT", lambda e, i=i: e.copy(
            out=xhead_b[:, i * 128:(i + 1) * 128], in_=pC[:64, :128]),
            reads=[kC], writes=["xhead_b"])
        p.op("ACT", lambda e, i=i: e.activation(
            out=xheadm_b[:, i * 128:(i + 1) * 128], in_=pC[:64, :128], func=AF.Identity,
            bias=0.0, scale=mix_o64[:]),
            reads=[kC, "mix_o64"], writes=["xheadm_b"])

    u_b = [p.sb(f"u{k}_b", 64, T, BF16) for k in range(3)]
    prev, prev_buf = xheadm_b, "xheadm_b"
    scales = [1.0, 0.5, 1.0 / 3.0]
    for k in range(3):
        c_s, ck_s = p.constf(scales[k])
        for q in range(4):
            p.op("PE", lambda e, q=q, prev=prev: e.matmul(
                pD[:64, :512], omega_b[:], prev[:, q * 512:(q + 1) * 512],
                start=True, stop=True),
                reads=["omega_b", prev_buf], writes=[kD])
            p.op("ACT", lambda e, k=k, q=q, c_s=c_s: e.activation(
                out=u_b[k][:, q * 512:(q + 1) * 512], in_=pD[:64, :512],
                func=AF.Identity, bias=0.0, scale=c_s[:64]),
                reads=[kD, ck_s], writes=[f"u{k}_b"])
        prev, prev_buf = u_b[k], f"u{k}_b"

    def load_64T(tag, w_ap):
        s32 = p.sb(tag + "_s32", 128, 64, F32)
        out16 = p.sb(tag, 64, D, BF16)
        for rc in range(8):
            p.dma("SP", s32[:], w_ap[rc * 128:(rc + 1) * 128, :],
                  writes=[tag + "_s32"])
            p.op("PE", lambda e: e.transpose(pD[:64, :128], s32[:], eye32[:]),
                 reads=[tag + "_s32", "eye32"], writes=[kD])
            p.op("ACT", lambda e, rc=rc: e.copy(
                out=out16[:, rc * 128:(rc + 1) * 128], in_=pD[:64, :128]),
                 reads=[kD], writes=[tag])
        return out16
    WoT = load_64T("WoT", W["L0_attn_omega_out_w"])
    VqoT = load_64T("VqoT", W["L0_attn_vq_proj_out_w"])

    vq_in32 = p.sb("vq_in32", 64, D, F32)
    p.dma("SP", vq_in32[:], W["L0_attn_vq_proj_in_w"], writes=["vq_in32"])
    vqinT = p.sb("vqinT", 128, 512, F32)
    for c in range(8):
        p.op("PE", lambda e, c=c: e.transpose(
            pE[:128, :64], vq_in32[:, c * 128:(c + 1) * 128], eye32[:64, :64]),
            reads=["vq_in32", "eye32"], writes=[kE])
        p.op("ACT", lambda e, c=c: e.copy(out=vqinT[:, c * 64:(c + 1) * 64],
                                          in_=pE[:128, :64]),
             reads=[kE], writes=["vqinT"])
    def mmvq(e):
        last = None
        for c in range(8):
            last = e.matmul(pE[:1, :64], hm_col[:, c:c + 1],
                            vqinT[:, c * 64:(c + 1) * 64],
                            start=(c == 0), stop=(c == 7))
        return last
    p.op("PE", mmvq, reads=["hm_col", "vqinT"], writes=[kE])
    vq_row = p.sb("vq_row", 1, 64, F32)
    bvq = load_row(p, "bvq", W["L0_attn_vq_proj_in_b"], 64)
    p.op("DVE", lambda e: e.tensor_add(out=vq_row[:], in0=pE[:1, :64], in1=bvq[:]),
         reads=[kE, "bvq"], writes=["vq_row"])

    sc1 = p.sb("sc1", 1, 8, F32)
    vsq = p.sb("vsq", 1, 64, F32)
    p.op("DVE", lambda e: e.tensor_mul(out=vsq[:], in0=vq_row[:], in1=vq_row[:]),
         reads=["vq_row"], writes=["vsq"])
    p.op("DVE", lambda e: e.reduce_sum(out=sc1[:1, 0:1], in_=vsq[:], axis=AX.X),
         reads=["vsq"], writes=["sc1"])
    p.op("ACT", lambda e: e.activation(out=sc1[:1, 1:2], in_=sc1[:1, 0:1],
                                       func=AF.Sqrt, bias=0.0, scale=1.0),
         reads=["sc1"], writes=["sc1"])
    p.op("DVE", lambda e: e.tensor_scalar_max(out=sc1[:1, 1:2], in0=sc1[:1, 1:2],
                                              scalar1=1e-12),
         reads=["sc1"], writes=["sc1"])
    p.op("DVE", lambda e: e.reciprocal(out=sc1[:1, 2:3], in_=sc1[:1, 1:2]),
         reads=["sc1"], writes=["sc1"])
    nvq = p.sb("nvq", 1, 64, F32)
    p.op("DVE", lambda e: e.tensor_single_scalar(out=nvq[:], in_=vq_row[:],
                                                 scalar=sc1[:1, 2:3], op=ALU.mult),
         reads=["vq_row", "sc1"], writes=["nvq"])

    cb = [p.sb(f"cb_{c}", 128, 64, F32) for c in range(2)]
    ncb = [p.sb(f"ncb_{c}", 128, 64, F32) for c in range(2)]
    cbc = p.sb("cbc", 128, 4, F32)
    for c in range(2):
        p.dma("SP", cb[c][:], W["L0_attn_codebook"][c * 128:(c + 1) * 128, :],
              writes=[f"cb_{c}"])
        p.op("DVE", lambda e, c=c: e.tensor_mul(out=ncb[c][:], in0=cb[c][:],
                                                in1=cb[c][:]),
             reads=[f"cb_{c}"], writes=[f"ncb_{c}"])
        p.op("DVE", lambda e, c=c: e.reduce_sum(out=cbc[:, 0:1], in_=ncb[c][:],
                                                axis=AX.X),
             reads=[f"ncb_{c}"], writes=["cbc"])
        p.op("ACT", lambda e, c=c: e.activation(out=cbc[:, 1:2], in_=cbc[:, 0:1],
                                                func=AF.Sqrt, bias=0.0, scale=1.0),
             reads=["cbc"], writes=["cbc"])
        p.op("DVE", lambda e, c=c: e.tensor_scalar_max(out=cbc[:, 1:2],
                                                       in0=cbc[:, 1:2], scalar1=1e-12),
             reads=["cbc"], writes=["cbc"])
        p.op("DVE", lambda e, c=c: e.reciprocal(out=cbc[:, 2:3], in_=cbc[:, 1:2]),
             reads=["cbc"], writes=["cbc"])
        p.op("DVE", lambda e, c=c: e.tensor_scalar(out=ncb[c][:], in0=cb[c][:],
                                                   scalar1=cbc[:, 2:3], scalar2=None,
                                                   op0=ALU.mult),
             reads=[f"cb_{c}", "cbc"], writes=[f"ncb_{c}"])

    nvq_col = p.sb("nvq_col", 64, 1, F32)
    p.op("PE", lambda e: e.transpose(pE[:64, :1], nvq[:1, :], eye32[:1, :1]),
         reads=["nvq", "eye32"], writes=[kE])
    p.op("ACT", lambda e: e.copy(out=nvq_col[:], in_=pE[:64, :1]),
         reads=[kE], writes=["nvq_col"])
    ncbT = p.sb("ncbT", 64, 256, F32)
    for c in range(2):
        p.op("PE", lambda e, c=c: e.transpose(pE[:64, :128], ncb[c][:], eye32[:]),
             reads=[f"ncb_{c}", "eye32"], writes=[kE])
        p.op("ACT", lambda e, c=c: e.copy(out=ncbT[:, c * 128:(c + 1) * 128],
                                          in_=pE[:64, :128]),
             reads=[kE], writes=["ncbT"])
    p.op("PE", lambda e: e.matmul(pE[:1, :256], nvq_col[:], ncbT[:],
                                  start=True, stop=True),
         reads=["nvq_col", "ncbT"], writes=[kE])
    scores = p.sb("scores", 1, 256, F32)
    p.op("ACT", lambda e: e.copy(out=scores[:], in_=pE[:1, :256]), reads=[kE],
         writes=["scores"])
    p.op("DVE", lambda e: e.reduce_max(out=sc1[:1, 3:4], in_=scores[:], axis=AX.X),
         reads=["scores"], writes=["sc1"])
    oh = p.sb("oh", 1, 256, F32)
    p.op("DVE", lambda e: e.tensor_single_scalar(out=oh[:], in_=scores[:],
                                                 scalar=sc1[:1, 3:4],
                                                 op=ALU.is_equal),
         reads=["scores", "sc1"], writes=["oh"])
    oh_col = p.sb("oh_col", 128, 2, F32)
    for c in range(2):
        p.op("PE", lambda e, c=c: e.transpose(pE[:128, c:c + 1],
                                              oh[:1, c * 128:(c + 1) * 128],
                                              eye32[:1, :1]),
             reads=["oh", "eye32"], writes=[kE])
    p.op("ACT", lambda e: e.copy(out=oh_col[:], in_=pE[:128, :2]), reads=[kE],
         writes=["oh_col"])
    def mmcode(e):
        last = None
        for c in range(2):
            last = e.matmul(pE[:1, :64], oh_col[:, c:c + 1], cb[c][:],
                            start=(c == 0), stop=(c == 1))
        return last
    p.op("PE", mmcode, reads=["oh_col", "cb_0", "cb_1"], writes=[kE])
    vqo = p.sb("vqo", 1, 64, F32)
    p.op("DVE", lambda e: e.tensor_sub(out=vqo[:], in0=pE[:1, :64], in1=vq_row[:]),
         reads=[kE, "vq_row"], writes=["vqo"])
    p.op("DVE", lambda e: e.tensor_add(out=vqo[:], in0=vq_row[:], in1=vqo[:]),
         reads=["vq_row", "vqo"], writes=["vqo"])
    vqo_col = p.sb("vqo_col", 64, 1, BF16)
    p.op("PE", lambda e: e.transpose(pE[:64, :1], vqo[:1, :], eye32[:1, :1]),
         reads=["vqo", "eye32"], writes=[kE])
    p.op("ACT", lambda e: e.copy(out=vqo_col[:], in_=pE[:64, :1]),
         reads=[kE], writes=["vqo_col"])
    vqc = p.sb("vqc", 1, D, F32)
    for h in range(2):
        p.op("PE", lambda e, h=h: e.matmul(pE[:1, :512], vqo_col[:],
                                           VqoT[:, h * 512:(h + 1) * 512],
                                           start=True, stop=True),
             reads=["vqo_col", "VqoT"], writes=[kE])
        p.op("ACT", lambda e, h=h: e.copy(out=vqc[:1, h * 512:(h + 1) * 512],
                                          in_=pE[:1, :512]),
             reads=[kE], writes=["vqc"])
    bvqo = load_row(p, "bvqo", W["L0_attn_vq_proj_out_b"], D)
    p.op("DVE", lambda e: e.tensor_add(out=vqc[:], in0=vqc[:], in1=bvqo[:]),
         reads=["vqc", "bvqo"], writes=["vqc"])
    p.op("DVE", lambda e: e.tensor_single_scalar(out=vqc[:], in_=vqc[:],
                                                 scalar=mix_v[:1, :1], op=ALU.mult),
         reads=["vqc", "mix_v"], writes=["vqc"])
    bo_row = load_row(p, "bo_row", W["L0_attn_omega_out_b"], D)
    p.op("DVE", lambda e: e.tensor_single_scalar(out=bo_row[:], in_=bo_row[:],
                                                 scalar=mix_o1[:1, :1], op=ALU.mult),
         reads=["bo_row", "mix_o1"], writes=["bo_row"])
    combo = p.sb("combo", 1, D, BF16)
    p.op("DVE", lambda e: e.tensor_add(out=combo[:], in0=vqc[:], in1=bo_row[:]),
         reads=["vqc", "bo_row"], writes=["combo"])

    nw_row = load_row(p, "nw_row", W["L0_attn_norm_w"], D)
    nw_bc = bcast_row(p, "nw_bc", nw_row, "nw_row", D, ones_f32, pF, kF)

    if getattr(p, "dbg_l0", False):
        d_xn = p.dram_out("d_xn", (128, D), F32)
        p.dma("POOL", d_xn, xn_t[0][:], reads=["xn_0"], writes=["d_xn"])
        d_hm = p.dram_out("d_hm", (128, 8), F32)
        p.dma("POOL", d_hm, hm_col[:], reads=["hm_col"], writes=["d_hm"])
        d_par = p.dram_out("d_par", (1, 2048), F32)
        p.dma("POOL", d_par, params_row[:], reads=["params_row"], writes=["d_par"])
        d_om = p.dram_out("d_om", (64, 64), F32)
        p.dma("POOL", d_om, omg[:], reads=["omg"], writes=["d_om"])
        d_u1 = p.dram_out("d_u1", (64, T), F32)
        u1f = p.sb("u1f", 64, T, F32)
        p.op("ACT", lambda e: e.copy(out=u1f[:], in_=u_b[0][:]), reads=["u0_b"],
             writes=["u1f"])
        p.dma("POOL", d_u1, u1f[:], reads=["u1f"], writes=["d_u1"])
        d_vqc = p.dram_out("d_vqc", (1, D), F32)
        p.dma("POOL", d_vqc, vqc[:], reads=["vqc"], writes=["d_vqc"])
        d_sc = p.dram_out("d_sc", (1, 256), F32)
        p.dma("POOL", d_sc, scores[:], reads=["scores"], writes=["d_sc"])

    bo_t2 = [p.sb(f"bo_t{j}", 128, D, F32) for j in range(2)]
    comps = [(xheadm_b, "xheadm_b"), (u_b[0], "u0_b"), (u_b[1], "u1_b"),
             (u_b[2], "u2_b")]
    comp_bufs = [cb_ for _, cb_ in comps]
    for i in range(NTC):
        jpar = i % 2
        bo_t, kbo = bo_t2[jpar], f"bo_t{jpar}"
        scr, kscr = scr2l[jpar], f"scr{jpar}"
        cols, kcols = cols2l[jpar], f"cols{jpar}"
        cols2, kcols2 = colsb2[jpar], f"cols2{jpar}"
        ppair = ((pA, kA), (pB, kB)) if jpar == 0 else ((pC, kC), (pD, kD))
        for h, (pl, kl) in enumerate(ppair):
            def mml(e, i=i, h=h, pl=pl):
                for k, (ct, _) in enumerate(comps):
                    e.matmul(pl[:, :512], ct[:, i * 128:(i + 1) * 128],
                             WoT[:, h * 512:(h + 1) * 512],
                             start=(k == 0), stop=False)
                return e.matmul(pl[:, :512], ones_bf16[:1, :],
                                combo[:1, h * 512:(h + 1) * 512],
                                start=False, stop=True)
            p.op("PE", mml, reads=comp_bufs + ["WoT", "combo", "ones_bf16"],
                 writes=[kl])
        p.op("DVE", lambda e, i=i, bo_t=bo_t, pl0=ppair[0][0]: e.tensor_add(
            out=bo_t[:, 0:512], in0=xn_t[i][:, 0:512], in1=pl0[:, :512]),
             reads=[f"xn_{i}", ppair[0][1]], writes=[kbo])
        p.op("DVE", lambda e, i=i, bo_t=bo_t, pl1=ppair[1][0]: e.tensor_add(
            out=bo_t[:, 512:1024], in0=xn_t[i][:, 512:1024], in1=pl1[:, :512]),
             reads=[f"xn_{i}", ppair[1][1]], writes=[kbo])
        if getattr(p, "dbg_l0", False) and i == 0:
            d_bo = p.dram_out("d_bo", (128, D), F32)
            p.dma("POOL", d_bo, bo_t[:], reads=[kbo], writes=["d_bo"])
        p.op("ACT", lambda e, scr=scr, bo_t=bo_t, cols2=cols2: e.activation(
            out=scr[:], in_=bo_t[:], func=AF.Square, bias=0.0, scale=1.0,
            accum_out=cols2[:, 0:1]),
             reads=[kbo], writes=[kscr, kcols2])
        c_e6, ck_e6 = p.constf(1e-6)
        c_iD, ck_iD = p.constf(1.0 / D)
        p.op("ACT", lambda e, cols2=cols2: e.activation(
            out=cols2[:, 1:2], in_=cols2[:, 0:1],
            func=AF.Sqrt, bias=c_e6, scale=c_iD),
             reads=[kcols2, ck_e6, ck_iD], writes=[kcols2])
        p.op("DVE", lambda e, cols2=cols2: e.reciprocal(
            out=cols2[:, 2:3], in_=cols2[:, 1:2]),
             reads=[kcols2], writes=[kcols2])
        p.op("ACT", lambda e, bo_t=bo_t, cols2=cols2: e.activation(
            out=bo_t[:], in_=bo_t[:], func=AF.Identity,
            bias=0.0, scale=cols2[:, 2:3]),
             reads=[kbo, kcols2], writes=[kbo])
        p.op("DVE", lambda e, bo_t=bo_t: e.tensor_mul(
            out=bo_t[:], in0=bo_t[:], in1=nw_bc[:]),
             reads=[kbo, "nw_bc"], writes=[kbo])
        emit_ln(p, bo_t, kbo, scr[:], kscr, (xn_t[i], f"xn_{i}"),
                (cols, kcols))
        j = i % 2
        p.dma("SP", xld[j][:], xin[i * 128:(i + 1) * 128, :], writes=[f"xld_{j}"])
        p.op("DVE", lambda e, i=i, j=j, bo_t=bo_t, scr=scr: e.tensor_add(
            out=bo_t[:], in0=xld[j][:], in1=scr[:]),
             reads=[f"xld_{j}", kscr], writes=[kbo])
        p.dma("POOL", xout[i * 128:(i + 1) * 128, :], bo_t[:],
              reads=[kbo], writes=[f"xo_{i}"])




def emit_l0ffn(p, xin, xout, W, C):
    eye32, ones_f32, ones_bf16, ones_col = p.cc
    (pA, kA), (pB, kB), (pC, kC), (pD, kD), (pE, kE), (pF, kF) = p.pool[:6]
    p.PK_w1b = p.dram_tmp("moe_w1b", (8, 1024, 2048), BF16)
    p.PK_w2b = p.dram_tmp("moe_w2b", (8, 2048, 1024), BF16)
    wst32 = [p.sb(f"f0_ws32_{i}", 128, 2048, F32) for i in range(2)]
    wc16 = [p.sb(f"f0_wc16_{i}", 128, 2048, BF16) for i in range(2)]
    cast_jobs = []
    for e_i in range(8):
        for rc in range(8):
            cast_jobs.append((W["L1_ffn_w1"][e_i, rc * 128:(rc + 1) * 128, :],
                              p.PK_w1b[e_i, rc * 128:(rc + 1) * 128, :], 2048))
        for rc in range(16):
            cast_jobs.append((W["L1_ffn_w2"][e_i, rc * 128:(rc + 1) * 128, :],
                              p.PK_w2b[e_i, rc * 128:(rc + 1) * 128, :], 1024))
    cast_i = [0]

    def emit_casts(k):
        for _ in range(k):
            if cast_i[0] >= len(cast_jobs):
                return
            src_ap, dst_ap, w = cast_jobs[cast_i[0]]
            par = cast_i[0] % 2
            cast_i[0] += 1
            p.dma("SP", wst32[par][:, :w], src_ap, writes=[f"f0_ws32_{par}"])
            p.op("ACT", lambda e, par=par, w=w: e.activation(
                out=wc16[par][:, :w], in_=wst32[par][:, :w], func=AF.Identity,
                bias=0.0, scale=1.0),
                reads=[f"f0_ws32_{par}"], writes=[f"f0_wc16_{par}"])
            p.dma("POOL", dst_ap, wc16[par][:, :w],
                  reads=[f"f0_wc16_{par}"], writes=["pk_wb"])

    qWT, qWT_bufs = load_wT_e4m3(p, "f0_qwt", W["L0_ffn_weight"], D, D)
    wsb = p.sb("f0_wsb", 128, 1, F32)
    p.dma("SP", wsb[:], W["L0_ffn_weight_scale"].broadcast_to([128, 1]),
          writes=["f0_wsb"])
    b_row = load_row(p, "f0_brow", W["L0_ffn_bias"], D)
    b_bc = bcast_row(p, "f0_bbc", b_row, "f0_brow", D, ones_f32, pF, kF)

    xb = [p.sb(f"f0_x{j}", 128, D, F32) for j in range(2)]
    xnb_2 = [p.sb(f"f0_xn{j}", 128, D, F32) for j in range(2)]
    scr_2 = [p.sb(f"f0_scr{j}", 128, D, F32) for j in range(2)]
    cols_2 = [p.sb(f"f0_cols{j}", 128, 8, F32) for j in range(2)]
    q16_2 = [p.sb(f"f0_q16{j}", 128, D, BF16) for j in range(2)]
    qa_2 = [p.sb(f"f0_qa{j}", 128, D, F32) for j in range(2)]
    qe_2 = [p.sb(f"f0_qe{j}", 128, D, I32) for j in range(2)]
    qs_2 = [p.sb(f"f0_qs{j}", 128, D, I32) for j in range(2)]
    qxT_2 = [p.sb(f"f0_qxT{j}", 128, D, BF16) for j in range(2)]
    ot_2 = [p.sb(f"f0_o{j}", 128, D, F32) for j in range(2)]

    for i in range(NTC):
        j = i % 2
        xk = f"f0_x{j}"
        xnb, scr, cols = xnb_2[j], scr_2[j], cols_2[j]
        q16, qxT, ot = q16_2[j], qxT_2[j], ot_2[j]
        qtmp = (qa_2[j], qe_2[j], qs_2[j], f"f0_q{j}")
        p.dma("SP", xb[j][:], xin[i * 128:(i + 1) * 128, :], writes=[xk])
        emit_ln(p, xb[j], xk, xnb[:], f"f0_xn{j}", (scr, f"f0_scr{j}"),
                (cols, f"f0_cols{j}"))
        am = cols[:, 6:7]
        xs = cols[:, 7:8]
        sinv = cols[:, 5:6]
        s2 = cols[:, 4:5]
        p.op("ACT", lambda e, scr=scr, xnb=xnb: e.activation(
            out=scr[:], in_=xnb[:], func=AF.Abs, bias=0.0, scale=1.0),
             reads=[f"f0_xn{j}"], writes=[f"f0_scr{j}"])
        p.op("DVE", lambda e, am=am, scr=scr: e.tensor_reduce(
            out=am, in_=scr[:], axis=AX.X, op=ALU.max),
             reads=[f"f0_scr{j}"], writes=[f"f0_cols{j}"])
        p.op("DVE", lambda e, am=am: e.tensor_scalar_max(
            out=am, in0=am, scalar1=1e-12),
             reads=[f"f0_cols{j}"], writes=[f"f0_cols{j}"])
        p.op("DVE", lambda e, xs=xs, am=am: e.tensor_scalar_mul(
            out=xs, in0=am, scalar1=1.0 / 448.0),
             reads=[f"f0_cols{j}"], writes=[f"f0_cols{j}"])
        p.op("DVE", lambda e, sinv=sinv, am=am: e.reciprocal(out=sinv, in_=am),
             reads=[f"f0_cols{j}"], writes=[f"f0_cols{j}"])
        p.op("DVE", lambda e, sinv=sinv: e.tensor_scalar_mul(
            out=sinv, in0=sinv, scalar1=448.0),
             reads=[f"f0_cols{j}"], writes=[f"f0_cols{j}"])
        p.op("DVE", lambda e, scr=scr, xnb=xnb, sinv=sinv: e.tensor_scalar(
            out=scr[:], in0=xnb[:], scalar1=sinv, scalar2=None, op0=ALU.mult),
             reads=[f"f0_xn{j}", f"f0_cols{j}"], writes=[f"f0_scr{j}"])
        emit_quant_e4m3(p, scr[:], f"f0_scr{j}", q16[:], f"f0_q16{j}", qtmp)
        for kc in range(NDC):
            p.dma_tr("SP", qxT[:, kc * 128:(kc + 1) * 128],
                     q16[:, kc * 128:(kc + 1) * 128],
                     reads=[f"f0_q16{j}"], writes=[f"f0_qxT{j}"])
        pspair = ((pA, kA), (pB, kB)) if j == 0 else ((pC, kC), (pD, kD))
        for h, (pl, kl) in enumerate(pspair):
            def mmf(e, h=h, pl=pl, qxT=qxT):
                last = None
                for kc in range(NDC):
                    last = e.matmul(pl[:, :512], qxT[:, kc * 128:(kc + 1) * 128],
                                    qWT[kc][:, h * 512:(h + 1) * 512],
                                    start=(kc == 0), stop=(kc == NDC - 1))
                return last
            p.op("PE", mmf, reads=[f"f0_qxT{j}"] + qWT_bufs, writes=[kl],
                 lhsT_reads=[f"f0_qxT{j}"] + qWT_bufs)
        p.op("DVE", lambda e, s2=s2, xs=xs: e.tensor_mul(out=s2, in0=xs, in1=wsb[:]),
             reads=[f"f0_cols{j}", "f0_wsb"], writes=[f"f0_cols{j}"])
        for h, (pl, kl) in enumerate(pspair):
            p.op("ACT", lambda e, h=h, pl=pl, ot=ot, s2=s2: e.activation(
                out=ot[:, h * 512:(h + 1) * 512], in_=pl[:, :512],
                func=AF.Identity, bias=0.0, scale=s2),
                reads=[kl, f"f0_cols{j}"], writes=[f"f0_o{j}"])
        p.op("DVE", lambda e, ot=ot: e.tensor_add(out=ot[:], in0=ot[:], in1=b_bc[:]),
             reads=[f"f0_o{j}", "f0_bbc"], writes=[f"f0_o{j}"])
        if getattr(p, "dbg_f0", False) and i == 0:
            for nm, tile, buf in (("dq_xn", xnb, f"f0_xn{j}"), ("dq_o", ot, f"f0_o{j}"),
                                  ("dq_scr", scr, f"f0_scr{j}")):
                dd = p.dram_out(nm, (128, D), F32)
                p.dma("POOL", dd, tile[:], reads=[buf], writes=[nm])
            dq16 = p.dram_out("dq_q16", (128, D), F32)
            q16f = p.sb("f0_q16f", 128, D, F32)
            p.op("ACT", lambda e, q16f=q16f, q16=q16: e.copy(out=q16f[:], in_=q16[:]),
                 reads=[f"f0_q16{j}"], writes=["f0_q16f"])
            p.dma("POOL", dq16, q16f[:], reads=["f0_q16f"], writes=["dq_q16"])
            dcols = p.dram_out("dq_cols", (128, 8), F32)
            p.dma("POOL", dcols, cols[:], reads=[f"f0_cols{j}"], writes=["dq_cols"])
        emit_ln(p, ot, f"f0_o{j}", scr[:], f"f0_scr{j}", (xnb, f"f0_xn{j}"),
                (cols, f"f0_cols{j}"))
        p.op("DVE", lambda e, j=j, ot=ot, scr=scr: e.tensor_add(
            out=ot[:], in0=xb[j][:], in1=scr[:]),
             reads=[xk, f"f0_scr{j}"], writes=[f"f0_o{j}"])
        p.dma("POOL", xout[i * 128:(i + 1) * 128, :], ot[:],
              reads=[f"f0_o{j}"], writes=[f"f0_xo_{i}"])
        emit_casts(12)




def emit_l1attn(p, xin, xout, W, C):
    eye32, ones_f32, ones_bf16, ones_col = p.cc
    (pA, kA), (pB, kB), (pC, kC), (pD, kD), (pE, kE), (pF, kF) = p.pool[:6]
    E = 2048
    NJ = 16

    PK_xcb = p.dram_tmp("a1_pk_xcb", (NJ, 128, T), BF16)
    PK_xcf = p.dram_tmp("a1_pk_xcf", (NJ, 128, T), F32)
    PK_xn2 = p.dram_tmp("a1_pk_xn2", (NDC, 128, T), BF16)
    PK_y = p.dram_tmp("a1_pk_y", (NJ, 128, T), F32)
    PK_xo = p.dram_tmp("a1_pk_xo", (T, D), F32)

    p.push_region()
    ina_row = load_row(p, "a1_ina", W["L1_attn_in_norm_alpha"], D)
    inb_row = load_row(p, "a1_inb", W["L1_attn_in_norm_bias"], D)
    ina_bc = bcast_row(p, "a1_inabc", ina_row, "a1_ina", D, ones_f32, pE, kE)
    inb_bc = bcast_row(p, "a1_inbbc", inb_row, "a1_inb", D, ones_f32, pE, kE)
    WxT, WxT_bufs = load_wT_bf16(p, "a1_wxt", W["L1_attn_in_proj_w"][0:E, :],
                                 E, D)
    ipb_row = load_row(p, "a1_ipb", W["L1_attn_in_proj_b"], 2 * E)
    bx_cols = bias_cols(p, "a1_bx", ipb_row[:1, 0:E], "a1_ipb", E, eye32, pE, kE)
    cb_row = load_row(p, "a1_cbr", W["L1_attn_conv_b"], E)
    cb_cols = bias_cols(p, "a1_cb", cb_row, "a1_cbr", E, eye32, pE, kE)
    wk_cols = p.sb("a1_wk", 128, 64, F32)
    for j in range(NJ):
        p.dma("SP", wk_cols[:, j * 4:(j + 1) * 4],
              W["L1_attn_conv_w"][j * 128:(j + 1) * 128, :], writes=["a1_wk"])

    xld = [p.sb(f"a1_x{i}", 128, D, F32) for i in range(2)]
    xo_t = p.sb("a1_xo", 128, D, F32)
    xn2_t = p.sb("a1_xn2", 128, D, BF16)
    scr = p.sb("a1_scr", 128, D, F32)
    cols = p.sb("a1_cols", 128, 8, F32)
    xn2FM = [p.sb(f"a1_xn2fm{kc}", 128, T, BF16) for kc in range(NDC)]

    for i in range(NTC):
        jj = i % 2
        xk = f"a1_x{jj}"
        p.dma("SP", xld[jj][:], xin[i * 128:(i + 1) * 128, :], writes=[xk])
        emit_ln(p, xld[jj], xk, xo_t[:], "a1_xo", (scr, "a1_scr"),
                (cols, "a1_cols"))
        p.dma("POOL", PK_xo[i * 128:(i + 1) * 128, :], xo_t[:],
              reads=["a1_xo"], writes=["pk_xo"])
        emit_ln(p, xo_t, "a1_xo", xn2_t[:], "a1_xn2", (scr, "a1_scr"),
                (cols, "a1_cols"), kind="inner", alpha_bc=ina_bc, beta_bc=inb_bc,
                ab_bufs=("a1_inabc", "a1_inbbc"))
        for kc in range(NDC):
            p.dma_tr("SP", xn2FM[kc][:, i * 128:(i + 1) * 128],
                     xn2_t[:, kc * 128:(kc + 1) * 128],
                     reads=["a1_xn2"], writes=[f"a1_xn2fm{kc}"])
    for kc in range(NDC):
        p.dma("POOL", PK_xn2[kc], xn2FM[kc][:],
              reads=[f"a1_xn2fm{kc}"], writes=["pk_xn2"])

    xin_j = [p.sb(f"a1_xinj{i}", 128, T + 3, F32) for i in range(2)]
    xcs_j = [p.sb(f"a1_xcsj{i}", 128, T, F32) for i in range(2)]
    xcs16 = [p.sb(f"a1_xcs16{i}", 128, T, BF16) for i in range(2)]
    for i in range(2):
        p.op("DVE", lambda e, i=i: e.memset(xin_j[i][:, 0:3], 0.0),
             writes=[f"a1_xinj{i}"])
    for j in range(NJ):
        par = j % 2
        xi, xc, x16 = xin_j[par], xcs_j[par], xcs16[par]
        ki, kx, k16 = f"a1_xinj{par}", f"a1_xcsj{par}", f"a1_xcs16{par}"
        for q in range(4):
            pl, kl = (pA, kA) if q % 2 == 0 else (pB, kB)
            def mmx(e, j=j, q=q, pl=pl):
                last = None
                for kc in range(NDC):
                    last = e.matmul(pl[:, :512],
                                    WxT[kc][:, j * 128:(j + 1) * 128],
                                    xn2FM[kc][:, q * 512:(q + 1) * 512],
                                    start=(kc == 0), stop=(kc == NDC - 1))
                return last
            p.op("PE", mmx, reads=WxT_bufs + [f"a1_xn2fm{kc}" for kc in range(NDC)],
                 writes=[kl], lhsT_reads=WxT_bufs)
            p.op("ACT", lambda e, j=j, q=q, pl=pl, xi=xi: e.activation(
                out=xi[:, 3 + q * 512:3 + (q + 1) * 512], in_=pl[:, :512],
                func=AF.Identity, bias=bx_cols[:, j:j + 1], scale=1.0),
                reads=[kl, "a1_bx"], writes=[ki])
        p.op("DVE", lambda e, j=j, xi=xi, xc=xc: e.tensor_scalar(
            out=xc[:], in0=xi[:, 0:T], scalar1=wk_cols[:, j * 4:j * 4 + 1],
            scalar2=None, op0=ALU.mult),
            reads=[ki, "a1_wk"], writes=[kx])
        for kk in range(1, 4):
            p.op("DVE", lambda e, j=j, kk=kk, xi=xi, xc=xc: e.scalar_tensor_tensor(
                out=xc[:], in0=xi[:, kk:kk + T],
                scalar=wk_cols[:, j * 4 + kk:j * 4 + kk + 1], in1=xc[:],
                op0=ALU.mult, op1=ALU.add),
                reads=[ki, "a1_wk", kx], writes=[kx])
        p.op("ACT", lambda e, j=j, xc=xc: e.activation(
            out=xc[:], in_=xc[:], func=AF.Silu,
            bias=cb_cols[:, j:j + 1], scale=1.0),
            reads=[kx, "a1_cb"], writes=[kx])
        p.dma("POOL", PK_xcf[j], xc[:], reads=[kx], writes=["pk_xcf"])
        p.op("ACT", lambda e, xc=xc, x16=x16: e.copy(out=x16[:], in_=xc[:]),
             reads=[kx], writes=[k16])
        p.dma("POOL", PK_xcb[j], x16[:], reads=[k16], writes=["pk_xcb"])
    p.pop_region()

    p.push_region()
    TS_B = 256
    NSL = T // TS_B
    TC = 8
    NCH = TS_B // TC
    dtWT, dtWT_bufs = load_wT_bf16(p, "a1_dtw", W["L1_attn_dt_proj_w"], E, E)
    xpT = p.sb("a1_xpt", 128, 512, BF16)
    p.push_region()
    xp32 = p.sb("a1_xp32", 32, E, F32)
    xp16 = p.sb("a1_xp16", 32, E, BF16)
    p.dma("SP", xp32[:], W["L1_attn_x_proj_w"], writes=["a1_xp32"])
    p.op("ACT", lambda e: e.copy(out=xp16[:], in_=xp32[:]),
         reads=["a1_xp32"], writes=["a1_xp16"])
    for kc in range(NJ):
        p.dma_tr("SP", xpT[:, kc * 32:(kc + 1) * 32],
                 xp16[:, kc * 128:(kc + 1) * 128],
                 reads=["a1_xp16"], writes=["a1_xpt"])
    p.pop_region()
    dtb_row = load_row(p, "a1_dtbr", W["L1_attn_dt_proj_b"], E)
    dtb_cols = bias_cols(p, "a1_dtb", dtb_row, "a1_dtbr", E, eye32, pE, kE)
    xpb_row = load_row(p, "a1_xpbr", W["L1_attn_x_proj_b"], 32)
    xpb_cols = p.sb("a1_xpb", 32, 2, F32)
    for half in range(2):
        p.op("PE", lambda e, half=half: e.transpose(
            pE[:16, :1], xpb_row[:1, half * 16:(half + 1) * 16], eye32[:1, :1]),
            reads=["a1_xpbr", "eye32"], writes=[kE])
        p.op("ACT", lambda e, half=half: e.copy(out=xpb_cols[:16, half:half + 1],
                                                in_=pE[:16, :1]),
             reads=[kE], writes=["a1_xpb"])
    D_row = load_row(p, "a1_dr", W["L1_attn_D_param"], E)
    D_cols = bias_cols(p, "a1_dc", D_row, "a1_dr", E, eye32, pE, kE)
    A_tile = p.sb("a1_at", 128, 256, F32)
    for j in range(NJ):
        p.dma("SP", A_tile[:, j * 16:(j + 1) * 16],
              W["L1_attn_A_log"][j * 128:(j + 1) * 128, :], writes=["a1_at"])
    p.op("ACT", lambda e: e.activation(out=A_tile[:], in_=A_tile[:], func=AF.Exp,
                                       bias=0.0, scale=1.0),
         reads=["a1_at"], writes=["a1_at"])
    p.op("DVE", lambda e: e.tensor_scalar_mul(out=A_tile[:], in0=A_tile[:],
                                              scalar1=-1.0),
         reads=["a1_at"], writes=["a1_at"])

    xcb_sl = p.sb("a1_xcbs", 128, NJ * TS_B, BF16)
    xcf_sl = p.sb("a1_xcfs", 128, NJ * TS_B, F32)
    dl_sl = p.sb("a1_dls", 128, NJ * TS_B, F32)
    y_sl = p.sb("a1_ys", 128, NJ * TS_B, F32)
    _zb1 = p.sb("a1_zb0", 128, NJ * 16 * TC, F32)
    Zb2 = [_zb1, _zb1]
    dAb2 = [p.sb(f"a1_dab{i}", 128, NJ * 16 * TC, F32) for i in range(2)]
    Ub2 = [p.sb(f"a1_ub{i}", 128, NJ * 16 * TC, F32) for i in range(2)]
    Hb2 = [p.sb(f"a1_hb{i}", 128, NJ * 16 * TC, F32) for i in range(2)]
    sq2 = [p.sb("a1_sq2", 128, 256, F32)] * 2
    ssj2 = [p.sb("a1_ssj", 128, 16, F32)] * 2
    sBrep2 = [p.sb(f"a1_brep{i}", 128, 16 * TC, F32) for i in range(2)]
    sCrep2 = [p.sb(f"a1_crep{i}", 128, 16 * TC, F32) for i in range(2)]
    _brow = p.sb("a1_brow0", 1, 16 * TC, F32)
    _crow = p.sb("a1_crow0", 1, 16 * TC, F32)
    Brow2 = [_brow, _brow]
    Crow2 = [_crow, _crow]
    wch2 = [p.sb(f"a1_wch{i}", 128, NJ * TC, F32) for i in range(2)]
    B_sl = p.sb("a1_bsl", 16, TS_B, F32)
    C_sl = p.sb("a1_csl", 16, TS_B, F32)
    _spa = p.sb("a1_spa0", 128, TS_B, F32)
    spa = [_spa, _spa]
    _spb = p.sb("a1_spb0", 128, TS_B, F32)
    spb = [_spb, _spb]
    c_neg1 = p.constf(-1.0)
    c_e10 = float(np.exp(np.float32(-10.0)))

    def vch(t, j_s, n_s, t_s, off=0):
        return bass.AP(t.tensor if hasattr(t, "tensor") else None, 0, [])

    for ts in range(NSL):
        for j in range(NJ):
            p.dma("SP", xcb_sl[:, j * TS_B:(j + 1) * TS_B],
                  PK_xcb[j][:, ts * TS_B:(ts + 1) * TS_B],
                  reads=["pk_xcb"], writes=["a1_xcbs"])
            p.dma("SP", xcf_sl[:, j * TS_B:(j + 1) * TS_B],
                  PK_xcf[j][:, ts * TS_B:(ts + 1) * TS_B],
                  reads=["pk_xcf"], writes=["a1_xcfs"])
        for m in range(NJ):
            par = m % 2
            pl, kl = (pA, kA) if par == 0 else (pB, kB)
            sa, sb = spa[par][:], spb[par][:]
            ka_, kb_ = "a1_spa0", "a1_spb0"
            def mmd(e, m=m, pl=pl):
                last = None
                for kc in range(NJ):
                    last = e.matmul(pl[:, :TS_B],
                                    dtWT[kc][:, m * 128:(m + 1) * 128],
                                    xcb_sl[:, kc * TS_B:(kc + 1) * TS_B],
                                    start=(kc == 0), stop=(kc == NJ - 1))
                return last
            p.op("PE", mmd, reads=dtWT_bufs + ["a1_xcbs"], writes=[kl],
                 lhsT_reads=dtWT_bufs)
            p.op("DVE", lambda e, m=m, pl=pl, sa=sa: e.tensor_scalar(
                out=sa, in0=pl[:, :TS_B], scalar1=dtb_cols[:, m:m + 1],
                scalar2=15.0, op0=ALU.add, op1=ALU.min),
                reads=[kl, "a1_dtb"], writes=[ka_])
            p.op("ACT", lambda e, sa=sa, sb=sb: e.activation(
                out=sb, in_=sa, func=AF.Exp, bias=0.0, scale=1.0),
                reads=[ka_], writes=[kb_])
            p.op("ACT", lambda e, sa=sa, sb=sb: e.activation(
                out=sa, in_=sb, func=AF.Ln, bias=1.0, scale=1.0),
                reads=[kb_], writes=[ka_])
            p.op("DVE", lambda e, m=m, sa=sa: e.tensor_scalar(
                out=dl_sl[:, m * TS_B:(m + 1) * TS_B], in0=sa, scalar1=10.0,
                scalar2=1e-4, op0=ALU.min, op1=ALU.max),
                reads=[ka_], writes=["a1_dls"])
        for half, (dst, dk) in enumerate(((B_sl, "a1_bsl"), (C_sl, "a1_csl"))):
            def mmbc(e, half=half):
                last = None
                for kc in range(NJ):
                    last = e.matmul(pD[:16, :TS_B],
                                    xpT[:, kc * 32 + half * 16:kc * 32 + half * 16 + 16],
                                    xcb_sl[:, kc * TS_B:(kc + 1) * TS_B],
                                    start=(kc == 0), stop=(kc == NJ - 1))
                return last
            p.op("PE", mmbc, reads=["a1_xpt", "a1_xcbs"], writes=[kD],
                 lhsT_reads=["a1_xpt"])
            p.op("ACT", lambda e, half=half, dst=dst: e.activation(
                out=dst[:], in_=pD[:16, :TS_B], func=AF.Identity,
                bias=xpb_cols[:16, half:half + 1], scale=1.0),
                reads=[kD, "a1_xpb"], writes=[dk])
        def scan_pre(ch):
            t0 = ch * TC
            gch = ts * NCH + ch
            cpar = gch % 2
            Zb, dAb, Ub, Hb = Zb2[cpar], dAb2[cpar], Ub2[cpar], Hb2[cpar]
            sBrep, sCrep = sBrep2[cpar], sCrep2[cpar]
            Brow, Crow = Brow2[cpar], Crow2[cpar]
            kzb, kdab, kub = "a1_zb0", f"a1_dab{cpar}", f"a1_ub{cpar}"
            kbrep, kcrep = f"a1_brep{cpar}", f"a1_crep{cpar}"
            kbrow, kcrow = "a1_brow0", "a1_crow0"
            p.dma("SP", Brow[:].rearrange("o (n t) -> o n t", n=16),
                  B_sl[:, t0:t0 + TC], reads=["a1_bsl"], writes=[kbrow])
            p.dma("SP", Crow[:].rearrange("o (n t) -> o n t", n=16),
                  C_sl[:, t0:t0 + TC], reads=["a1_csl"], writes=[kcrow])
            p.op("PE", lambda e, Brow=Brow: e.matmul(pE[:, :16 * TC], ones_f32[:1, :],
                                          Brow[:], start=True, stop=True),
                 reads=[kbrow, "ones_f32"], writes=[kE],
                 lhsT_reads=["ones_f32"])
            p.op("ACT", lambda e, sBrep=sBrep: e.copy(out=sBrep[:], in_=pE[:, :16 * TC]),
                 reads=[kE], writes=[kbrep])
            p.op("PE", lambda e, Crow=Crow: e.matmul(pF[:, :16 * TC], ones_f32[:1, :],
                                          Crow[:], start=True, stop=True),
                 reads=[kcrow, "ones_f32"], writes=[kF],
                 lhsT_reads=["ones_f32"])
            p.op("ACT", lambda e, sCrep=sCrep: e.copy(out=sCrep[:], in_=pF[:, :16 * TC]),
                 reads=[kF], writes=[kcrep])
            zv = Zb[:].rearrange("p (j n t) -> p j n t", j=NJ, n=16)
            dl_ap = dl_sl[:].rearrange("p (j t) -> p j t", j=NJ)[:, :, t0:t0 + TC]
            dl_b = dl_ap.unsqueeze(2).broadcast_to([128, NJ, 16, TC])
            A_ap = A_tile[:].rearrange("p (j n) -> p j n", j=NJ).unsqueeze(3).broadcast_to([128, NJ, 16, TC])
            p.op("POOL", lambda e, zv=zv, dl_b=dl_b, A_ap=A_ap: e.tensor_tensor(
                out=zv, in0=dl_b, in1=A_ap, op=ALU.mult),
                reads=["a1_dls", "a1_at"], writes=[kzb])
            c10, ck10 = p.constf(10.0)
            cn10, ckn10 = p.constf(-10.0)
            p.op("ACT", lambda e, Zb=Zb, c10=c10: e.activation(
                out=Zb[:], in_=Zb[:], func=AF.Relu, bias=c10, scale=1.0),
                 reads=[kzb, ck10], writes=[kzb])
            p.op("ACT", lambda e, dAb=dAb, Zb=Zb, cn10=cn10: e.activation(
                out=dAb[:], in_=Zb[:], func=AF.Exp, bias=cn10, scale=1.0),
                 reads=[kzb, ckn10], writes=[kdab])
            wcht, kwch = wch2[cpar], f"a1_wch{cpar}"
            wv = wcht[:].rearrange("p (j t) -> p j t", j=NJ)
            xcf_ap = xcf_sl[:].rearrange("p (j t) -> p j t", j=NJ)[:, :, t0:t0 + TC]
            p.op("POOL", lambda e, wv=wv, dl_ap=dl_ap, xcf_ap=xcf_ap: e.tensor_tensor(
                out=wv, in0=dl_ap, in1=xcf_ap, op=ALU.mult),
                reads=["a1_dls", "a1_xcfs"], writes=[kwch])
            uv = Ub[:].rearrange("p (j n t) -> p j n t", j=NJ, n=16)
            w_b = wv.unsqueeze(2).broadcast_to([128, NJ, 16, TC])
            br_b = sBrep[:].rearrange("p (n t) -> p n t", n=16).unsqueeze(1).broadcast_to([128, NJ, 16, TC])
            p.op("POOL", lambda e, uv=uv, w_b=w_b, br_b=br_b: e.tensor_tensor(
                out=uv, in0=w_b, in1=br_b, op=ALU.mult),
                reads=[kwch, kbrep], writes=[kub])

        def scan_back(ch):
            t0 = ch * TC
            gch = ts * NCH + ch
            cpar = gch % 2
            Zb, dAb, Ub, Hb = Zb2[cpar], dAb2[cpar], Ub2[cpar], Hb2[cpar]
            sq256, ssj = sq2[0], ssj2[0]
            sCrep = sCrep2[cpar]
            kzb, kdab, kub, khb = ("a1_zb0", f"a1_dab{cpar}",
                                   f"a1_ub{cpar}", f"a1_hb{cpar}")
            ksq, kssj = "a1_sq2", "a1_ssj"
            kcrep = f"a1_crep{cpar}"
            da0v = dAb[:].rearrange("p (j n t) -> p j n t", j=NJ, n=16)[:, :, :, 0:1]
            if gch > 0:
                u0 = Ub[:].rearrange("p (j n t) -> p j n t", j=NJ, n=16)[:, :, :, 0:1]
                Hprev = Hb2[(gch - 1) % 2]
                khprev = f"a1_hb{(gch - 1) % 2}"
                hp = Hprev[:].rearrange("p (j n t) -> p j n t", j=NJ, n=16)[:, :, :, TC - 1:TC]
                p.op("DVE", lambda e, da0v=da0v, hp=hp, sq256=sq256: e.tensor_tensor(
                    out=sq256[:].rearrange("p (j n) -> p j n", j=NJ).unsqueeze(3),
                    in0=da0v, in1=hp, op=ALU.mult),
                    reads=[kdab, khprev], writes=[ksq])
                p.op("DVE", lambda e, u0=u0, sq256=sq256: e.tensor_tensor(
                    out=u0, in0=u0,
                    in1=sq256[:].rearrange("p (j n) -> p j n", j=NJ).unsqueeze(3),
                    op=ALU.add),
                    reads=[kub, ksq], writes=[kub])
            p.op("DVE", lambda e, da0v=da0v: e.memset(da0v, 0.0),
                 reads=[], writes=[kdab])
            p.op("DVE", lambda e, Hb=Hb, dAb=dAb, Ub=Ub: e.tensor_tensor_scan(
                out=Hb[:], data0=dAb[:], data1=Ub[:], initial=0.0,
                op0=ALU.mult, op1=ALU.add),
                reads=[kdab, kub], writes=[khb])
            h15 = Hb[:].rearrange("p (j n t) -> p j n t", j=NJ, n=16)[:, :, :, TC - 1:TC]
            if gch % 2 == 1:
                p.op("DVE", lambda e, h15=h15, sq256=sq256: e.tensor_tensor(
                    out=sq256[:].rearrange("p (j n) -> p j n", j=NJ).unsqueeze(3),
                    in0=h15, in1=h15, op=ALU.mult),
                    reads=[khb], writes=[ksq])
                p.op("DVE", lambda e, sq256=sq256, ssj=ssj: e.tensor_reduce(
                    out=ssj[:], in_=sq256[:].rearrange("p (j n) -> p j n", j=NJ),
                    axis=AX.X, op=ALU.add),
                    reads=[ksq], writes=[kssj])
                p.op("ACT", lambda e, ssj=ssj: e.activation(out=ssj[:], in_=ssj[:],
                                                   func=AF.Ln, bias=0.0,
                                                   scale=1.0),
                     reads=[kssj], writes=[kssj])
                c_nh, ck_nh = p.constf(-0.5)
                p.op("ACT", lambda e, ssj=ssj, c_nh=c_nh: e.activation(
                    out=ssj[:], in_=ssj[:], func=AF.Exp, bias=0.0, scale=c_nh),
                     reads=[kssj, ck_nh], writes=[kssj])
                p.op("DVE", lambda e, ssj=ssj: e.tensor_scalar(out=ssj[:], in0=ssj[:],
                                                      scalar1=10.0, scalar2=1.0,
                                                      op0=ALU.mult, op1=ALU.min),
                     reads=[kssj], writes=[kssj])
                sc_b = ssj[:].unsqueeze(2).broadcast_to([128, NJ, 16]).unsqueeze(3)
                p.op("DVE", lambda e, h15=h15, sc_b=sc_b: e.tensor_tensor(
                    out=h15, in0=h15, in1=sc_b, op=ALU.mult),
                    reads=[khb, kssj], writes=[khb])
            sv = Ub[:].rearrange("p (j n t) -> p j n t", j=NJ, n=16)
            cr_b = sCrep[:].rearrange("p (n t) -> p n t", n=16).unsqueeze(1).broadcast_to([128, NJ, 16, TC])
            hv = Hb[:].rearrange("p (j n t) -> p j n t", j=NJ, n=16)
            p.op("POOL", lambda e, sv=sv, hv=hv, cr_b=cr_b: e.tensor_tensor(
                out=sv, in0=hv, in1=cr_b, op=ALU.mult),
                reads=[khb, kcrep], writes=[kub])

        def scan_red(ch):
            t0 = ch * TC
            gch = ts * NCH + ch
            cpar = gch % 2
            Ub = Ub2[cpar]
            kub = f"a1_ub{cpar}"
            y_ap = y_sl[:].rearrange("p (j t) -> p j t", j=NJ)[:, :, t0:t0 + TC]
            s_re = Ub[:].rearrange("p (j n t) -> p j t n", j=NJ, n=16)
            p.op("DVE", lambda e, y_ap=y_ap, s_re=s_re: e.tensor_reduce(
                out=y_ap, in_=s_re, axis=AX.X, op=ALU.add),
                reads=[kub], writes=["a1_ys"])

        for ch in range(NCH):
            if ch > 1:
                scan_red(ch - 2)
            scan_pre(ch)
            if ch > 0:
                scan_back(ch - 1)
        scan_back(NCH - 1)
        scan_red(NCH - 2)
        scan_red(NCH - 1)
        dcol_b = D_cols[:].unsqueeze(2).broadcast_to([128, 16, TS_B])
        xcf_v = xcf_sl[:].rearrange("p (j t) -> p j t", j=NJ)
        p.op("POOL", lambda e, dcol_b=dcol_b, xcf_v=xcf_v: e.tensor_tensor(
            out=xcf_v, in0=xcf_v, in1=dcol_b, op=ALU.mult),
            reads=["a1_xcfs", "a1_dc"], writes=["a1_xcfs"])
        p.op("DVE", lambda e: e.tensor_add(out=y_sl[:], in0=y_sl[:], in1=xcf_sl[:]),
             reads=["a1_ys", "a1_xcfs"], writes=["a1_ys"])
        for j in range(NJ):
            p.dma("POOL", PK_y[j][:, ts * TS_B:(ts + 1) * TS_B],
                  y_sl[:, j * TS_B:(j + 1) * TS_B],
                  reads=["a1_ys"], writes=["pk_y"])
    p.pop_region()

    p.push_region()
    WzT, WzT_bufs = load_wT_bf16(p, "a1_wzt", W["L1_attn_in_proj_w"][E:2 * E, :],
                                 E, D)
    WoT2, WoT2_bufs = load_wT_bf16(p, "a1_wot", W["L1_attn_out_proj_w"], D, E)
    ipb2_row = load_row(p, "a1_ipb2", W["L1_attn_in_proj_b"], 2 * E)
    bz_cols = bias_cols(p, "a1_bz", ipb2_row[:1, E:2 * E], "a1_ipb2", E, eye32,
                        pE, kE)
    ob_row = load_row(p, "a1_obr", W["L1_attn_out_proj_b"], D)
    ob_bc = bcast_row(p, "a1_obc", ob_row, "a1_obr", D, ones_f32, pE, kE)
    na_row = load_row(p, "a1_nar", W["L1_attn_norm_alpha"], D)
    nb_row = load_row(p, "a1_nbr", W["L1_attn_norm_bias"], D)
    na_bc = bcast_row(p, "a1_nabc", na_row, "a1_nar", D, ones_f32, pE, kE)
    nb_bc = bcast_row(p, "a1_nbbc", nb_row, "a1_nbr", D, ones_f32, pE, kE)

    TS_D = 256
    xn2_sl = p.sb("d1_xn2", 128, NDC * TS_D, BF16)
    yd_sl = p.sb("d1_y", 128, NJ * TS_D, F32)
    gate_sl = p.sb("d1_gate", 128, NJ * TS_D, BF16)
    zs_j = [p.sb(f"d1_zs{i}", 128, TS_D, F32) for i in range(2)]
    bo_t = p.sb("d1_bo", 128, D, F32)
    xo2_t = p.sb("d1_xo", 128, D, F32)
    x2_t = p.sb("d1_x2", 128, D, F32)
    scr2 = p.sb("d1_scr", 128, D, F32)
    cols2 = p.sb("d1_cols", 128, 8, F32)
    for ts in range(T // TS_D):
        for kc in range(NDC):
            p.dma("SP", xn2_sl[:, kc * TS_D:(kc + 1) * TS_D],
                  PK_xn2[kc][:, ts * TS_D:(ts + 1) * TS_D],
                  reads=["pk_xn2"], writes=["d1_xn2"])
        for j in range(NJ):
            p.dma("SP", yd_sl[:, j * TS_D:(j + 1) * TS_D],
                  PK_y[j][:, ts * TS_D:(ts + 1) * TS_D],
                  reads=["pk_y"], writes=["d1_y"])
        for j in range(NJ):
            par = j % 2
            pl, kl = (pA, kA) if par == 0 else (pB, kB)
            zs, kz = zs_j[par], f"d1_zs{par}"
            def mmz(e, j=j, pl=pl):
                last = None
                for kc in range(NDC):
                    last = e.matmul(pl[:, :TS_D],
                                    WzT[kc][:, j * 128:(j + 1) * 128],
                                    xn2_sl[:, kc * TS_D:(kc + 1) * TS_D],
                                    start=(kc == 0), stop=(kc == NDC - 1))
                return last
            p.op("PE", mmz, reads=WzT_bufs + ["d1_xn2"], writes=[kl],
                 lhsT_reads=WzT_bufs)
            p.op("ACT", lambda e, j=j, pl=pl, zs=zs: e.activation(
                out=zs[:], in_=pl[:, :TS_D], func=AF.Silu,
                bias=bz_cols[:, j:j + 1], scale=1.0),
                reads=[kl, "a1_bz"], writes=[kz])
            p.op("DVE", lambda e, j=j, zs=zs: e.tensor_mul(
                out=gate_sl[:, j * TS_D:(j + 1) * TS_D],
                in0=yd_sl[:, j * TS_D:(j + 1) * TS_D], in1=zs[:]),
                reads=["d1_y", kz], writes=["d1_gate"])
        for tc in range(TS_D // 128):
            g = ts * (TS_D // 128) + tc
            for h, (pl, kl) in enumerate(((pC, kC), (pD, kD))):
                def mmo(e, h=h, pl=pl, tc=tc):
                    last = None
                    for kc in range(NJ):
                        last = e.matmul(pl[:, :512],
                                        gate_sl[:, kc * TS_D + tc * 128:kc * TS_D + (tc + 1) * 128],
                                        WoT2[kc][:, h * 512:(h + 1) * 512],
                                        start=(kc == 0), stop=(kc == NJ - 1))
                    return last
                p.op("PE", mmo, reads=WoT2_bufs + ["d1_gate"], writes=[kl],
                     lhsT_reads=["d1_gate"])
                p.op("ACT", lambda e, h=h, pl=pl: e.activation(
                    out=bo_t[:, h * 512:(h + 1) * 512], in_=pl[:, :512],
                    func=AF.Identity, bias=0.0, scale=1.0),
                    reads=[kl], writes=["d1_bo"])
            p.op("DVE", lambda e: e.tensor_add(out=bo_t[:], in0=bo_t[:],
                                               in1=ob_bc[:]),
                 reads=["d1_bo", "a1_obc"], writes=["d1_bo"])
            p.dma("SP", xo2_t[:], PK_xo[g * 128:(g + 1) * 128, :],
                  reads=["pk_xo"], writes=["d1_xo"])
            p.op("DVE", lambda e: e.tensor_add(out=bo_t[:], in0=bo_t[:],
                                               in1=xo2_t[:]),
                 reads=["d1_bo", "d1_xo"], writes=["d1_bo"])
            emit_ln(p, bo_t, "d1_bo", scr2[:], "d1_scr", (xo2_t, "d1_xo"),
                    (cols2, "d1_cols"), kind="inner", alpha_bc=na_bc,
                    beta_bc=nb_bc, ab_bufs=("a1_nabc", "a1_nbbc"))
            emit_ln(p, scr2, "d1_scr", bo_t[:], "d1_bo", (xo2_t, "d1_xo"),
                    (cols2, "d1_cols"))
            p.dma("SP", x2_t[:], xin[g * 128:(g + 1) * 128, :],
                  reads=[], writes=["d1_x2"])
            p.op("DVE", lambda e: e.tensor_add(out=bo_t[:], in0=x2_t[:],
                                               in1=bo_t[:]),
                 reads=["d1_x2", "d1_bo"], writes=["d1_bo"])
            p.dma("POOL", xout[g * 128:(g + 1) * 128, :], bo_t[:],
                  reads=["d1_bo"], writes=["d1_out"])
    p.pop_region()




def emit_l1ffn(p, xin, xout, W, C):
    eye32, ones_f32, ones_bf16, ones_col = p.cc
    (pA, kA), (pB, kB), (pC, kC), (pD, kD), (pE, kE), (pF, kF) = p.pool[:6]
    NE, H = 8, 2048
    NIC = H // 128

    xn3FM = [p.sb(f"m_xn3fm{kc}", 128, T, BF16) for kc in range(NDC)]
    acc = [p.sb(f"m_acc{g}", 128, D, F32) for g in range(NTC)]
    Wgt = p.sb("m_wgt", 128, NTC * NE, F32)

    routerT = p.sb("m_rt", 128, 64, F32)
    p.push_region()
    rw32 = p.sb("m_rw32", 8, D, F32)
    p.dma("SP", rw32[:], W["L1_ffn_router_w"], writes=["m_rw32"])
    for kc in range(NDC):
        p.op("PE", lambda e, kc=kc: e.transpose(
            pC[:, :8], rw32[:, kc * 128:(kc + 1) * 128], eye32[:8, :8]),
            reads=["m_rw32", "eye32"], writes=[kC])
        p.op("ACT", lambda e, kc=kc: e.copy(out=routerT[:, kc * 8:(kc + 1) * 8],
                                            in_=pC[:, :8]),
             reads=[kC], writes=["m_rt"])
    p.pop_region()

    xld = [p.sb(f"m_x{i}", 128, D, F32) for i in range(2)]
    xn3_2 = [p.sb(f"m_xn3_{i}", 128, D, F32) for i in range(2)]
    _xn3T = p.sb("m_xn3T_0", 128, D, F32)
    xn3T_2 = [_xn3T, _xn3T]
    _mscr = p.sb("m_scr_0", 128, D, F32)
    scr_2 = [_mscr, _mscr]
    cols_2 = [p.sb(f"m_cols_{i}", 128, 8, F32) for i in range(2)]
    lg_2 = [p.sb(f"m_lg_{i}", 128, 8, F32) for i in range(2)]
    t8_2 = [p.sb(f"m_t8_{i}", 128, 8, F32) for i in range(2)]
    _pR = p.ps("m_psr0", 128, 8, F32)
    pR2 = [_pR, _pR]

    for i in range(NTC):
        jj = i % 2
        xk = f"m_x{jj}"
        xn3_t, kxn3 = xn3_2[jj], f"m_xn3_{jj}"
        xn3T, kxn3T = xn3T_2[jj], "m_xn3T_0"
        scr, kscr = scr_2[jj], "m_scr_0"
        cols, kcols = cols_2[jj], f"m_cols_{jj}"
        lg, klg = lg_2[jj], f"m_lg_{jj}"
        t8, kt8 = t8_2[jj], f"m_t8_{jj}"
        pR, kpR = pR2[jj], "m_psr0"
        pT, kpT = ((pC, kC) if jj == 0 else (pD, kD))
        p.dma("SP", xld[jj][:], xin[i * 128:(i + 1) * 128, :], writes=[xk])
        emit_ln(p, xld[jj], xk, xn3_t[:], kxn3, (scr, kscr), (cols, kcols))
        p.op("DVE", lambda e: e.memset(acc[i][:], 0.0), writes=[f"m_acc{i}"])
        for kc in range(NDC):
            p.op("PE", lambda e, kc=kc, xn3_t=xn3_t, pT=pT: e.transpose(
                pT[:, :128], xn3_t[:, kc * 128:(kc + 1) * 128], eye32[:]),
                reads=[kxn3, "eye32"], writes=[kpT])
            p.op("ACT", lambda e, kc=kc, i=i, pT=pT: e.copy(
                out=xn3FM[kc][:, i * 128:(i + 1) * 128], in_=pT[:, :128]),
                reads=[kpT], writes=[f"m_xn3fm{kc}"])
            p.op("ACT", lambda e, kc=kc, pT=pT, xn3T=xn3T: e.copy(
                out=xn3T[:, kc * 128:(kc + 1) * 128], in_=pT[:, :128]),
                reads=[kpT], writes=[kxn3T])
        def mmr(e, xn3T=xn3T, pR=pR):
            last = None
            for kc in range(NDC):
                last = e.matmul(pR[:], xn3T[:, kc * 128:(kc + 1) * 128],
                                routerT[:, kc * 8:(kc + 1) * 8],
                                start=(kc == 0), stop=(kc == NDC - 1))
            return last
        p.op("PE", mmr, reads=[kxn3T, "m_rt"], writes=[kpR])
        p.op("ACT", lambda e, lg=lg, pR=pR: e.copy(out=lg[:], in_=pR[:]),
             reads=[kpR], writes=[klg])
        m1 = cols[:, 0:1]
        m2 = cols[:, 1:2]
        dd = cols[:, 2:3]
        s1 = cols[:, 3:4]
        s2 = cols[:, 4:5]
        p.op("DVE", lambda e, lg=lg, m1=m1: e.reduce_max(out=m1, in_=lg[:], axis=AX.X),
             reads=[klg], writes=[kcols])
        p.op("DVE", lambda e, lg=lg, scr=scr, m1=m1: e.tensor_single_scalar(
            out=scr[:, 0:8], in_=lg[:], scalar=m1, op=ALU.is_equal),
             reads=[klg, kcols], writes=[kscr])
        p.op("DVE", lambda e, scr=scr, lg=lg, t8=t8: e.scalar_tensor_tensor(
            out=t8[:], in0=scr[:, 0:8], scalar=-1e30, in1=lg[:],
            op0=ALU.mult, op1=ALU.add),
             reads=[kscr, klg], writes=[kt8])
        p.op("DVE", lambda e, t8=t8, m2=m2: e.reduce_max(out=m2, in_=t8[:], axis=AX.X),
             reads=[kt8], writes=[kcols])
        p.op("DVE", lambda e, m1=m1, m2=m2, dd=dd: e.tensor_sub(out=dd, in0=m1, in1=m2),
             reads=[kcols], writes=[kcols])
        p.op("ACT", lambda e, dd=dd, s1=s1: e.activation(out=s1, in_=dd, func=AF.Sigmoid,
                                           bias=0.0, scale=1.0),
             reads=[kcols], writes=[kcols])
        p.op("DVE", lambda e, s1=s1, s2=s2: e.tensor_scalar(out=s2, in0=s1, scalar1=-1.0,
                                              scalar2=1.0, op0=ALU.mult,
                                              op1=ALU.add),
             reads=[kcols], writes=[kcols])
        p.op("DVE", lambda e, t8=t8, m2=m2: e.tensor_single_scalar(out=t8[:], in_=t8[:],
                                                     scalar=m2, op=ALU.is_equal),
             reads=[kt8, kcols], writes=[kt8])
        p.op("DVE", lambda e, t8=t8, s2=s2: e.tensor_scalar(out=t8[:], in0=t8[:], scalar1=s2,
                                              scalar2=None, op0=ALU.mult),
             reads=[kt8, kcols], writes=[kt8])
        p.op("DVE", lambda e, i=i, scr=scr, s1=s1, t8=t8: e.scalar_tensor_tensor(
            out=Wgt[:, i * NE:(i + 1) * NE], in0=scr[:, 0:8], scalar=s1,
            in1=t8[:], op0=ALU.mult, op1=ALU.add),
            reads=[kscr, kcols, kt8], writes=["m_wgt"])

    w1b = [p.sb(f"m_w1b{kc}", 128, H, BF16) for kc in range(NDC)]
    w2b = [p.sb(f"m_w2b{ic}", 128, D, BF16) for ic in range(NIC)]
    h2g = p.sb("m_h2g", 128, NIC * 512, BF16)
    hrt = [p.sb(f"m_hrt{i}", 128, 512, BF16) for i in range(2)]
    for e_i in range(NE):
        for kc in range(NDC):
            p.dma("SP", w1b[kc][:], p.PK_w1b[e_i, kc * 128:(kc + 1) * 128, :],
                  reads=["pk_wb"], writes=[f"m_w1b{kc}"])
        for ic in range(NIC):
            p.dma("POOL", w2b[ic][:], p.PK_w2b[e_i, ic * 128:(ic + 1) * 128, :],
                  reads=["pk_wb"], writes=[f"m_w2b{ic}"])

        for ts in range(4):
            for ic in range(NIC):
                par = ic % 2
                pl, kl = (pA, kA) if par == 0 else (pB, kB)
                hr, kh = hrt[par], f"m_hrt{par}"
                def mmh(e, ic=ic, ts=ts, pl=pl):
                    last = None
                    for kc in range(NDC):
                        last = e.matmul(pl[:, :512],
                                        w1b[kc][:, ic * 128:(ic + 1) * 128],
                                        xn3FM[kc][:, ts * 512:(ts + 1) * 512],
                                        start=(kc == 0), stop=(kc == NDC - 1))
                    return last
                p.op("PE", mmh,
                     reads=[f"m_w1b{kc}" for kc in range(NDC)] +
                           [f"m_xn3fm{kc}" for kc in range(NDC)],
                     writes=[kl], lhsT_reads=[f"m_w1b{kc}" for kc in range(NDC)])
                p.op("ACT", lambda e, pl=pl, hr=hr: e.activation(
                    out=hr[:], in_=pl[:, :512], func=AF.Relu, bias=0.0,
                    scale=1.0),
                     reads=[kl], writes=[kh])
                p.op("DVE", lambda e, ic=ic, hr=hr: e.tensor_mul(
                    out=h2g[:, ic * 512:(ic + 1) * 512], in0=hr[:], in1=hr[:]),
                    reads=[kh], writes=["m_h2g"])
            for tc in range(4):
                g = ts * 4 + tc
                for h, (pl, kl) in enumerate(((pC, kC), (pD, kD))):
                    def mmy(e, h=h, pl=pl, tc=tc):
                        last = None
                        for ic in range(NIC):
                            last = e.matmul(pl[:, :512],
                                            h2g[:, ic * 512 + tc * 128:ic * 512 + (tc + 1) * 128],
                                            w2b[ic][:, h * 512:(h + 1) * 512],
                                            start=(ic == 0), stop=(ic == NIC - 1))
                        return last
                    p.op("PE", mmy,
                         reads=["m_h2g"] + [f"m_w2b{ic}" for ic in range(NIC)],
                         writes=[kl], lhsT_reads=["m_h2g"])
                    p.op("DVE", lambda e, h=h, pl=pl, g=g, e_i=e_i: e.scalar_tensor_tensor(
                        out=acc[g][:, h * 512:(h + 1) * 512], in0=pl[:, :512],
                        scalar=Wgt[:, g * NE + e_i:g * NE + e_i + 1],
                        in1=acc[g][:, h * 512:(h + 1) * 512],
                        op0=ALU.mult, op1=ALU.add),
                        reads=[kl, "m_wgt", f"m_acc{g}"], writes=[f"m_acc{g}"])
    if getattr(p, "dbg_m", False):
        d_acc = p.dram_out("d_acc", (128, D), F32)
        p.dma("POOL", d_acc, acc[15][:], reads=["m_acc15"], writes=["d_acc"])
        d_acc0 = p.dram_out("d_acc0", (128, D), F32)
        p.dma("POOL", d_acc0, acc[0][:], reads=["m_acc0"], writes=["d_acc0"])
    for i in range(NTC):
        jj = i % 2
        xk = f"m_x{jj}"
        scr, kscr = scr_2[jj], "m_scr_0"
        cols, kcols = cols_2[jj], f"m_cols_{jj}"
        xsc, kxsc = xn3_2[jj], f"m_xn3_{jj}"
        p.dma("SP", xld[jj][:], xin[i * 128:(i + 1) * 128, :], writes=[xk])
        emit_ln(p, acc[i], f"m_acc{i}", scr[:], kscr, (xsc, kxsc),
                (cols, kcols))
        p.op("DVE", lambda e, i=i, jj=jj, scr=scr: e.tensor_add(
            out=scr[:], in0=xld[jj][:], in1=scr[:]),
             reads=[xk, kscr], writes=[kscr])
        p.dma("POOL", xout[i * 128:(i + 1) * 128, :], scr[:],
              reads=[kscr], writes=[f"m_out{i}"])



INPUT_SPECS = {
    "L0_attn_omega_proj_w": ((2016, 1024), F32),
    "L0_attn_omega_proj_b": ((2016,), F32),
    "L0_attn_omega_out_w": ((1024, 64), F32),
    "L0_attn_omega_out_b": ((1024,), F32),
    "L0_attn_vq_proj_in_w": ((64, 1024), F32),
    "L0_attn_vq_proj_in_b": ((64,), F32),
    "L0_attn_vq_proj_out_w": ((1024, 64), F32),
    "L0_attn_vq_proj_out_b": ((1024,), F32),
    "L0_attn_codebook": ((256, 64), F32),
    "L0_attn_omega_mix": ((1, 1), F32),
    "L0_attn_vq_mix": ((1, 1), F32),
    "L0_attn_norm_w": ((1024,), F32),
    "L0_ffn_weight": ((1024, 1024), U8),
    "L0_ffn_weight_scale": ((1, 1), F32),
    "L0_ffn_bias": ((1024,), F32),
    "L1_attn_in_proj_w": ((4096, 1024), F32),
    "L1_attn_in_proj_b": ((4096,), F32),
    "L1_attn_conv_w": ((2048, 4), F32),
    "L1_attn_conv_b": ((2048,), F32),
    "L1_attn_x_proj_w": ((32, 2048), F32),
    "L1_attn_x_proj_b": ((32,), F32),
    "L1_attn_dt_proj_w": ((2048, 2048), F32),
    "L1_attn_dt_proj_b": ((2048,), F32),
    "L1_attn_A_log": ((2048, 16), F32),
    "L1_attn_D_param": ((2048,), F32),
    "L1_attn_out_proj_w": ((1024, 2048), F32),
    "L1_attn_out_proj_b": ((1024,), F32),
    "L1_attn_norm_alpha": ((1024,), F32),
    "L1_attn_norm_bias": ((1024,), F32),
    "L1_attn_in_norm_alpha": ((1024,), F32),
    "L1_attn_in_norm_bias": ((1024,), F32),
    "L1_ffn_router_w": ((8, 1024), F32),
    "L1_ffn_w1": ((8, 1024, 2048), F32),
    "L1_ffn_w2": ((8, 2048, 1024), F32),
}

PHASE_WEIGHTS = {
    "l0attn": [k for k in INPUT_SPECS if k.startswith("L0_attn")],
    "l0ffn": [k for k in INPUT_SPECS if k.startswith("L0_ffn")] +
             ["L1_ffn_w1", "L1_ffn_w2"],
    "l1attn": [k for k in INPUT_SPECS if k.startswith("L1_attn")],
    "l1ffn": [k for k in INPUT_SPECS if k.startswith("L1_ffn")],
}

EMITTERS = {}


def build_program(phases):
    p = Prog()
    p.pool = [(p.ps(f"pool{i}", 128, 512, F32), f"pool{i}") for i in range(7)]
    wnames = sorted({w for ph in phases for w in PHASE_WEIGHTS[ph]})
    W = {n: p.dram_in(n, INPUT_SPECS[n][0], INPUT_SPECS[n][1]) for n in wnames}
    C = {
        "eye32": p.dram_in("c_eye32", (128, 128), F32),
        "triu64": p.dram_in("c_triu64", (64, 64), F32),
    }
    p.cc = common_consts(p, C)
    p.constf(0.0)
    xin = p.dram_in("x_in", (T, D), F32)
    cur = xin
    for i, ph in enumerate(phases):
        last = i == len(phases) - 1
        if last:
            nxt = p.dram_out("x_out", (T, D), F32)
        else:
            nxt = p.dram_tmp(f"x_mid{i}", (T, D), F32)
        p.push_region()
        EMITTERS[ph](p, cur, nxt, W, C)
        p.pop_region()
        cur = nxt
    nc = p.finalize()
    return nc, wnames


_CONSTS = None


def _consts():
    global _CONSTS
    if _CONSTS is None:
        eye = np.eye(128, dtype=np.float32)
        triu = np.triu(np.ones((64, 64), dtype=np.float32), k=1)
        _CONSTS = {"eye32": eye, "triu64": triu}
    return _CONSTS


LAUNCH_GROUPS = [["l0attn", "l0ffn", "l1attn", "l1ffn"]]


def _prep_inputs(inputs):
    arrs = {}
    for name, (shape, dt) in INPUT_SPECS.items():
        a = np.asarray(inputs[name])
        if dt == F32:
            a = np.ascontiguousarray(a, dtype=np.float32).reshape(shape)
        elif dt == U8:
            a = np.ascontiguousarray(a).view(np.uint8).reshape(shape)
        else:
            a = np.ascontiguousarray(a).reshape(shape)
        arrs[name] = a
    return arrs


def run_groups(inputs, groups):
    arrs = _prep_inputs(inputs)
    x = np.ascontiguousarray(np.asarray(inputs["L0_attn_x"], dtype=np.float32))
    cur = [np.ascontiguousarray(x[b]) for b in range(8)]
    consts = _consts()
    for phases in groups:
        nc, wnames = build_program(phases)
        in_maps = []
        for b in range(8):
            m = {"x_in": cur[b], "c_eye32": consts["eye32"],
                 "c_triu64": consts["triu64"]}
            for n in wnames:
                m[n] = arrs[n]
            in_maps.append(m)
        res = run_bass_kernel_spmd(nc, in_maps, core_ids=list(range(8)))
        cur = [np.asarray(res.results[b]["x_out"]) for b in range(8)]
    return np.stack(cur).astype(np.float32)


def kernel(**inputs):
    return run_groups(inputs, LAUNCH_GROUPS)


EMITTERS["l0attn"] = emit_l0attn
EMITTERS["l0ffn"] = emit_l0ffn
EMITTERS["l1attn"] = emit_l1attn
EMITTERS["l1ffn"] = emit_l1ffn
